# version 07
# speedup vs baseline: 1.5727x; 1.0416x over previous
import numpy as np
import ml_dtypes
import concourse.bass as bass
import concourse.bacc as bacc
import concourse.mybir as mybir
from concourse.bass_utils import run_bass_kernel_spmd
from contextlib import ExitStack

F32 = mybir.dt.float32
BF16 = mybir.dt.bfloat16
AF = mybir.ActivationFunctionType
ALU = mybir.AluOpType
ds = bass.ds
ts = bass.ts

ENGS = ("sync", "gpsimd", "tensor", "vector", "scalar")
KCFG = {"a1": 8, "a23": 4, "a4": 4, "b1": 16, "b2": 8}


class _MockInstr:
    __slots__ = ("_m",)
    def then_inc(self, sem, v):
        self._m._note(sem, v)
        return self

class MockEngine:
    def __init__(self):
        self.counts = {}
        self.n_ops = 0
    def _note(self, sem, v):
        self.counts[sem.name] = self.counts.get(sem.name, 0) + v
    def __getattr__(self, name):
        if name.startswith("_"):
            raise AttributeError(name)
        def fn(*a, **k):
            self.n_ops += 1
            mi = _MockInstr(); mi._m = self
            return mi
        return fn
    def sem_inc(self, sem, v):
        self.n_ops += 1
        self._note(sem, v)
    def wait_ge(self, sem, v):
        self.n_ops += 1


class M:
    def __init__(self, nc, H, sems, eng, E, shared, count_mode):
        self.nc = nc; self.H = H; self.sems = sems
        self.eng = eng; self.E = E
        self.S = shared
        self.count_mode = count_mode
        self.cur_stage = None
        self.cur_tag = "pro"
        self.ladpos = 0

    def stage_begin(self, name, niter, sq=None):
        self.cur_stage = name
        self.sq = sq if sq is not None else name
        if self.count_mode:
            st = self.S["stages"].setdefault(
                name, {"niter": niter, "pre": {}, "per": {}, "engops": {},
                       "base": {}, "shbase": {}, "sq": self.sq})
            st["niter"] = niter
            st["sq"] = self.sq
            self.S.setdefault("sqmap", {})[name] = self.sq
            self._snap0 = dict(self.E.counts)
            b0 = st.setdefault("base0", {})
            for k, v in self._snap0.items():
                b0[k] = max(b0.get(k, 0), v)
        else:
            self.sq = self.S["stages"][name]["sq"]
        self.cur_tag = "pro"

    def loop_begin(self):
        if self.count_mode:
            st = self.S["stages"][self.cur_stage]
            self._snap_pre = dict(self.E.counts)
            self._ops_pre = self.E.n_ops
            for k, v in self._snap_pre.items():
                st["base"][k] = max(st["base"].get(k, 0), v)
            sn = self.mysem()
            st["shbase"][sn + "i"] = max(st["shbase"].get(sn + "i", 0),
                                         self.S.setdefault("shctr", {}).get(sn + "i", 0))
        self.cur_tag = "body"

    def loop_end(self):
        if self.count_mode:
            st = self.S["stages"][self.cur_stage]
            for k, v in list(self.E.counts.items()):
                d = v - self._snap_pre.get(k, 0)
                if d:
                    st["per"][k] = st["per"].get(k, 0) + d
                    self.E.counts[k] = self._snap_pre.get(k, 0) + d * st["niter"]
            st["engops"][self.eng] = self.E.n_ops - self._ops_pre
            sn = self.mysem()
            if st["engops"][self.eng] > 0:
                self.S.setdefault("shctr", {})
                self.S["shctr"][sn + "i"] = (self.S["shctr"].get(sn + "i", 0)
                                             + st["niter"])
                self.E.counts[sn + "i"] = self.S["shctr"][sn + "i"]
        self.cur_tag = "epi"

    def stage_end(self):
        self.cur_stage = None

    def body_ops(self, stage):
        return self.S["stages"][stage]["engops"].get(self.eng, 0)

    ENG_LETTER = {"sync": "sy", "gpsimd": "g", "tensor": "t",
                  "vector": "v", "scalar": "s"}

    def mysem(self):
        return self.sq + "_" + self.ENG_LETTER[self.eng]

    def sembase(self, stage, semname):
        if self.count_mode:
            return 0
        return self.S["stages"][stage]["base"].get(semname, 0)

    def op(self, instr, name=None):
        semname = self.mysem()
        if self.count_mode:
            st = self.S["stages"][self.cur_stage]
            st.setdefault("boosted", set()).add(semname)
        return self.ms(instr, name, semname) if name else self.ms(instr, "_", semname)

    def b(self, semname):
        return 0

    def wprev(self, mname, cc):
        if self.count_mode:
            self.E.n_ops += 1
            return
        semname, stage = self.S["reg"][mname][0], self.S["reg"][mname][1]
        shb = self.S["stages"][stage]["shbase"].get(semname + "i", 0)
        self.E.wait_ge(self.sems[semname + "i"], shb + cc)

    def run_stage(self, name, niter, pro, body, epi=None, sq=None):
        self.stage_begin(name, niter, sq=sq)
        pro(self)
        self.loop_begin()
        if self.count_mode:
            st = self.S["stages"][name]
            st["ladbase"] = max(st.get("ladbase", 0),
                                self.S.setdefault("ladctr", {}).get(st["sq"], 0))
            self.ladpos = 0
            body(self, 0)
            self.E.n_ops += 2
            self.S["lad"][name] = max(self.S["lad"].get(name, 0), self.ladpos)
            if self.ladpos > 0:
                self.S["ladctr"][st["sq"]] = (st["ladbase"]
                                              + niter * self.S["lad"][name])
        else:
            if self.body_ops(name) > 0:
                st = self.S["stages"][name]
                sn = self.mysem()
                pre = st["pre"].get(sn, 0)
                per = st["per"].get(sn, 0)
                with self.E.Fori(0, niter) as cc:
                    self.ladpos = 0
                    body(self, cc)
                    if per > 0:
                        self.E.wait_ge(self.sems[sn], pre + (cc + 1) * per)
                    self.E.sem_inc(self.sems[sn + "i"], 1)
        self.loop_end()
        if epi:
            epi(self)
        self.stage_end()

    def ms(self, instr, mname, semname, v=1):
        semname = self._rs(semname)
        instr.then_inc(self.sems[semname], v)
        if self.count_mode:
            st = self.S["stages"][self.cur_stage]
            cnt = self.E.counts.get(semname, 0)
            if self.cur_tag == "pro":
                off = cnt
            elif self.cur_tag == "body":
                off = cnt - self._snap_pre.get(semname, 0)
            else:
                off = (cnt - st["base"].get(semname, 0)
                       - st["niter"] * st["per"].get(semname, 0))
            self.S["reg"][mname] = (semname, self.cur_stage, self.cur_tag, off)
        return instr

    def msval(self, mname, cc=None, dn=0):
        semname, stage, tag, off = self.S["reg"][mname]
        st = self.S["stages"][stage]
        base = st["base"].get(semname, 0)
        per = st["per"].get(semname, 0)
        if tag == "pro":
            return off + dn
        if tag == "epi":
            return base + st["niter"] * per + off + dn
        return base + cc * per + off + dn

    def w(self, mname, cc=None, dn=0):
        if self.count_mode:
            self.E.n_ops += 1
            return
        semname = self.S["reg"][mname][0]
        self.E.wait_ge(self.sems[semname], self.msval(mname, cc, dn))

    def _rs(self, name):
        sqmap = self.S.get("sqmap", {})
        for stn in sqmap:
            if name.startswith(stn):
                return sqmap[stn] + name[len(stn):]
        return name

    def _basedval(self, name, val):
        if self.count_mode:
            return val
        for stn, sq in self.S.get("sqmap", {}).items():
            if name.startswith(stn) and stn != "PRO":
                rs = sq + name[len(stn):]
                st = self.S["stages"][stn]
                if rs.endswith("i") and rs[:-1] in (
                        sq + "_sy", sq + "_g", sq + "_t", sq + "_v", sq + "_s"):
                    return st["shbase"].get(rs, 0) + val
                return st["base"].get(rs, 0) + val
        return val

    def sem(self, name):
        return self.sems[self._rs(name)]

    def wsem(self, semname, val):
        self.E.wait_ge(self.sems[self._rs(semname)], self._basedval(semname, val))

    def wsem0(self, semname, val):
        if self.count_mode:
            self.E.wait_ge(self.sems[self._rs(semname)], val)
            return
        rs = self._rs(semname)
        base = self.S["stages"][self.cur_stage].get("base0", {}).get(rs, 0)
        self.E.wait_ge(self.sems[rs], base + val)

    def epims(self, mname, semname):
        semname = self._rs(semname)
        st = self.S["stages"][self.cur_stage]
        if not self.count_mode:
            per = st["per"].get(semname, 0)
            prev = st["base"].get(semname, 0) + st["niter"] * per
            self.E.wait_ge(self.sems[semname], prev)
        self.ms(self.E.nop(), mname, semname)

    def credit(self, semname, v):
        self.E.sem_inc(self.sems[self._rs(semname)], v)

    def rel(self, semname, v=1):
        self.E.sem_inc(self.sems[self._rs(semname)], v)

    def lad(self, owner, fn, name=None, extra_waits=(), extra_incs=(), pre=None):
        pos = self.ladpos
        self.ladpos += 1
        if name is not None and self.count_mode:
            self.S["reg"].setdefault("LAD:" + self.cur_stage + ":" + name, pos + 1)
        if self.eng != owner:
            return
        semk = self.sems[self.sq + "k"]
        if self.count_mode:
            self.E.n_ops += 1
            if pre is not None:
                pre(self)
            instr = fn(self.E)
            instr.then_inc(semk, 1)
            for s_, v_ in extra_incs:
                instr.then_inc(self.sems[self._rs(s_)], v_)
            return instr
        self.E.wait_ge(semk, self.lad_base + pos)
        if pre is not None:
            pre(self)
        for s_, v_ in extra_waits:
            self.E.wait_ge(self.sems[self._rs(s_)], self._basedval(s_, v_))
        instr = fn(self.E)
        instr.then_inc(semk, 1)
        for s_, v_ in extra_incs:
            instr.then_inc(self.sems[s_], v_)
        return instr

    def ladwait(self, name, cc, stage=None):
        stage = stage or self.cur_stage
        if self.count_mode:
            self.E.n_ops += 1
            return
        pos1 = self.S["reg"]["LAD:" + stage + ":" + name]
        per = self.S["lad"][stage]
        st = self.S["stages"][stage]
        self.E.wait_ge(self.sems[st["sq"] + "k"],
                       st.get("ladbase", 0) + cc * per + pos1)

    def lbase(self, cc, stage=None):
        if self.count_mode:
            self.lad_base = 0
            return
        st = self.S["stages"][stage or self.cur_stage]
        self.lad_base = (st.get("ladbase", 0)
                         + cc * self.S["lad"][stage or self.cur_stage])


def wpc(m, namefmt, p, cc):
    pm = 1 - p
    nm = namefmt.format(p=pm)
    if p == 1:
        m.w(nm, cc)
    else:
        m.wprev(nm, cc)


def host_prepare(inputs):
    f32 = np.float32
    g = lambda k: np.asarray(inputs[k], f32)
    com = {}
    com["w1T"] = np.ascontiguousarray(g("L0_ffn_w1_w").T)
    com["w2T"] = np.ascontiguousarray(g("L0_ffn_w2_w").T)
    com["h1T"] = np.ascontiguousarray(g("L0_ffn_h1_w").T)
    com["h2T"] = np.ascontiguousarray(g("L0_ffn_h2_w").T)
    com["w3T"] = np.ascontiguousarray(g("L0_ffn_w3_w").T)
    com["s1c"] = np.ascontiguousarray(g("L0_ffn_w1_s").reshape(32, 128).T)
    com["s2c"] = np.ascontiguousarray(g("L0_ffn_w2_s").reshape(32, 128).T)
    com["h1sc"] = np.ascontiguousarray(g("L0_ffn_h1_s").reshape(32, 128).T)
    com["h2sc"] = np.ascontiguousarray(g("L0_ffn_h2_s").reshape(32, 128).T)
    com["s3c"] = np.ascontiguousarray(g("L0_ffn_w3_s").reshape(8, 128).T)
    com["Wq"] = np.ascontiguousarray(g("L0_attn_Wq"))
    com["Wk"] = np.ascontiguousarray(g("L0_attn_Wk"))
    com["Wv"] = np.ascontiguousarray(g("L0_attn_Wv"))
    com["Wo"] = np.ascontiguousarray(g("L0_attn_Wo"))
    com["Wba"] = np.ascontiguousarray(
        np.concatenate([g("L0_attn_Wa"), g("L0_attn_Wb")], axis=1))
    com["W1"] = np.ascontiguousarray(g("L0_attn_W1"))
    com["W2"] = np.ascontiguousarray(g("L0_attn_W2"))
    com["b1c"] = np.ascontiguousarray(g("L0_attn_b1").reshape(32, 128).T)
    com["b2c"] = np.ascontiguousarray(g("L0_attn_b2").reshape(8, 128).T)
    com["ln1wc"] = np.ascontiguousarray(g("L0_attn_ln1_w").reshape(8, 128).T)
    com["ln1bc"] = np.ascontiguousarray(g("L0_attn_ln1_b").reshape(8, 128).T)
    com["ln2wc"] = np.ascontiguousarray(g("L0_attn_ln2_w").reshape(8, 128).T)
    com["ln2bc"] = np.ascontiguousarray(g("L0_attn_ln2_b").reshape(8, 128).T)
    com["IDB"] = np.eye(128, dtype=ml_dtypes.bfloat16)
    com["IDF"] = np.eye(128, dtype=f32)
    com["TRILS"] = np.tril(np.ones((128, 128), f32), -1)
    com["TRIUI"] = np.triu(np.ones((128, 128), f32), 0)
    com["ONESB"] = np.ones((128, 128), ml_dtypes.bfloat16)
    com["ONESF"] = np.ones((1, 128), f32)
    com["CONSTV"] = np.broadcast_to(
        np.array([1e-5, 1e-6, 1.0, 0.0], f32), (128, 4)).copy()
    x = np.asarray(inputs["L0_attn_x"], f32)
    shards = [np.ascontiguousarray(x[b].T).reshape(8, 128, 4096) for b in range(8)]
    return com, shards


def _inner(m, n, fn):
    if m.count_mode:
        for i in range(n):
            fn(i)
    else:
        with m.E.Fori(0, n) as i:
            fn(i)
M.inner = _inner


def _dead_emit_program(m, phases=("A",), dbg=False):
    H = m.H; E = m.E; eng = m.eng

    m.stage_begin("PRO", 0)
    if eng == "sync":
        pairs = [("IDB", "idb"), ("IDF", "idf"), ("TRILS", "trils"), ("TRIUI", "triui"),
                 ("ONESB", "onesb"), ("ONESF", "onesf"), ("s1c", "s1c"), ("s2c", "s2c"),
                 ("h1sc", "h1sc"), ("h2sc", "h2sc"), ("s3c", "s3c"), ("b1c", "b1c"),
                 ("b2c", "b2c"), ("ln1wc", "ln1wc"), ("ln1bc", "ln1bc"),
                 ("ln2wc", "ln2wc"), ("ln2bc", "ln2bc")]
        for src, dst in pairs:
            E.dma_start(out=H[dst][:], in_=H[src]).then_inc(m.sem("cl"), 16)
    if eng == "gpsimd":
        E.dma_start(out=H["AA"].ap().rearrange("p (n o) -> p n o", o=4096),
                    in_=H["w1T"].rearrange("(n p) o -> p n o", p=128)
                    ).then_inc(m.sem("w_w1"), 16)
        E.dma_start(out=H["AB"].ap().rearrange("p (n o) -> p n o", o=4096),
                    in_=H["w2T"].rearrange("(n p) o -> p n o", p=128)
                    ).then_inc(m.sem("w_w2"), 16)
        E.dma_start(out=H["h1b"], in_=H["h1T"]).then_inc(m.sem("w_h1"), 16)
        E.dma_start(out=H["h2b"], in_=H["h2T"]).then_inc(m.sem("w_h2"), 16)
    if eng == "vector":
        E.wait_ge(m.sem("cl"), 18 * 16)
        m.ms(E.tensor_scalar(out=H["s3h"][:], in0=H["s3c"][:], scalar1=0.5,
                             scalar2=None, op0=ALU.mult), "s3h", "s3hd")
    m.stage_end()

    if "A" in phases:
        ffn_phase(m, "a", H["xT"], H["x1b"],
                  src_gate=None, first=True)
    if "B" in phases:
        emit_b(m)
    if "C" in phases:
        ffn_phase(m, "c", H["x2b"], H["outT"], src_gate="b2end_v", first=False,
                  srcx=lambda mm, cc, p: mm.wsem("b2xst",
                                                 mm.sembase("b2", "b2xst")
                                                 + (2 * cc + p + 1) * 16))


def ffn_phase(m, ph, xsrc, xdst, src_gate, first, srcx=None):
    H = m.H
    stage_a1(m, ph + "1", xsrc, first, src_gate, srcx=srcx)
    stage_a23(m, ph + "2", H["ybuf"], H["gbuf"], H["h1b"], "h1sc",
              prev_end=ph + "1end_t", whsem="w_h1",
              srcw=lambda mm, cc, p: (mm.wsem(ph + "1ysta", (8 * cc + 4 * p + 4) * 16),
                                      mm.wsem(ph + "1ystb", (8 * cc + 4 * p + 4) * 16)),
              prev_pb=(ph + "1ystb", KCFG["a1"] * 2 * 16))
    stage_a23(m, ph + "3", H["gbuf"], H["ybuf"], H["h2b"], "h2sc",
              prev_end=ph + "2end_t", whsem="w_h2",
              srcw=lambda mm, cc, p: mm.wsem(ph + "2gst", (8 * cc + 4 * p + 4) * 16),
              prev_gst=ph + "2gst")
    stage_a4(m, ph + "4", H["ybuf"], xsrc, xdst,
             prev_end=ph + "3end_t", a1_v_end=ph + "1end_v",
             srcw=lambda mm, cc, p: mm.wsem(ph + "3gst", (4 * cc + 4) * 16),
             prev_gst=ph + "3gst")


def stage_a1(m, sg, xsrc, first, src_gate, srcx=None):
    H = m.H; E = m.E; eng = m.eng
    AA, AB, XF, XN, PB = H["AA"], H["AB"], H["XF"], H["XN"], H["PB"]
    STT, T1 = H["STT"], H["T1"]
    ps = H["ps"]; pstat = H["pstat"]
    n_iter = KCFG["a1"]

    def pro(m):
        if eng == "sync":
            if src_gate is not None:
                m.w(src_gate)

        if eng in ("vector", "scalar", "tensor"):
            E.wait_ge(m.sem("cl"), 18 * 16)
        if eng == "tensor":
            if first:
                E.wait_ge(m.sem("w_w1"), 16)
                E.wait_ge(m.sem("w_w2"), 16)
            else:
                E.wait_ge(m.sem("w_cw1"), 16)
                E.wait_ge(m.sem("w_cw2"), 16)
        if eng == "gpsimd" and not first:
            m.w("b2end_t")
            m.w("b2end_s")
            E.dma_start(out=AA.ap().rearrange("p (n o) -> p n o", o=4096),
                        in_=H["w1T"].rearrange("(n p) o -> p n o", p=128)
                        ).then_inc(m.sem("w_cw1"), 16)
            E.dma_start(out=AB.ap().rearrange("p (n o) -> p n o", o=4096),
                        in_=H["w2T"].rearrange("(n p) o -> p n o", p=128)
                        ).then_inc(m.sem("w_cw2"), 16)

    def body(m, cc):
        for p in range(2):
            cg = cc * 2 + p
            xoff = p * 2048
            noff = p * 2048
            ro = p * 1024
            if p == 0:
                m.lbase(cc)
            if eng == "sync":
                m.wprev(sg + f"g_{p}_31", cc)
                if srcx is not None:
                    srcx(m, cc, p)
                E.dma_start(
                    out=XF.ap()[:, xoff:xoff + 2048].rearrange("p (n t) -> p n t", t=256),
                    in_=xsrc.rearrange("n p t -> p n t")[:, :, ts(cg, 256)]
                ).then_inc(m.sem(sg + "ldx" + str(p)), 16)
            m.lad("vector", lambda E_: E_.tensor_copy(
                out=XN[:, noff:noff + 2048], in_=XF[:, xoff:xoff + 2048]),
                extra_waits=((sg + "ldx" + str(p), (cc + 1) * 16),))
            m.lad("scalar", lambda E_: E_.activation(
                out=PB[:, 0:2048], in_=XN[:, noff:noff + 2048], func=AF.Square),
                extra_waits=((sg + "ysta", cg * 16),))
            def _s1(E_):
                for it in range(8):
                    i = E_.matmul(out=pstat[0:1, 0:256], lhsT=H["onesb"][:, 0:1],
                                  rhs=XN[:, noff + it * 256: noff + (it + 1) * 256],
                                  start=(it == 0), stop=(it == 7))
                return i
            m.lad("tensor", _s1)
            def _s2(E_):
                for it in range(8):
                    i = E_.matmul(out=pstat[0:1, 256:512], lhsT=H["onesb"][:, 0:1],
                                  rhs=PB[:, it * 256:(it + 1) * 256],
                                  start=(it == 0), stop=(it == 7))
                return i
            m.lad("tensor", _s2)
            mrow = STT[0:1, ro:ro + 256]
            erow = STT[0:1, ro + 256:ro + 512]
            m2row = STT[0:1, ro + 512:ro + 768]
            m.lad("vector", lambda E_: E_.tensor_scalar(
                out=mrow, in0=pstat[0:1, 0:256], scalar1=1.0 / 1024,
                scalar2=None, op0=ALU.mult))
            m.lad("vector", lambda E_: E_.tensor_scalar(
                out=erow, in0=pstat[0:1, 256:512], scalar1=1.0 / 1024,
                scalar2=None, op0=ALU.mult))
            m.lad("scalar", lambda E_: E_.activation(out=m2row, in_=mrow, func=AF.Square))
            m.lad("vector", lambda E_: E_.tensor_sub(out=erow, in0=erow, in1=m2row))
            m.lad("scalar", lambda E_: E_.activation(out=erow, in_=erow,
                                                     func=AF.Sqrt, bias=1e-5))
            m.lad("vector", lambda E_: E_.reciprocal(out=erow, in_=erow))
            m.lad("tensor", lambda E_: E_.matmul(
                out=ps[0][:, 0:256], lhsT=H["onesf"][:], rhs=mrow,
                start=True, stop=True))
            m.lad("tensor", lambda E_: E_.matmul(
                out=ps[0][:, 256:512], lhsT=H["onesf"][:], rhs=erow,
                start=True, stop=True))
            for it in range(8):
                tt = XF[:, 4096 + it * 256: 4096 + (it + 1) * 256]
                m.lad("vector", lambda E_, it=it, tt=tt: E_.tensor_sub(
                    out=tt, in0=XF[:, xoff + it * 256: xoff + (it + 1) * 256],
                    in1=ps[0][:, 0:256]))
                m.lad("vector", lambda E_, it=it, tt=tt: E_.tensor_mul(
                    out=XN[:, noff + it * 256: noff + (it + 1) * 256],
                    in0=tt, in1=ps[0][:, 256:512]),
                    name=(f"xr{p}" if it == 7 else None))
            if eng == "tensor":
                m.ladwait(f"xr{p}", cc)
                for ot in range(32):
                    a = ot % 2
                    if ot >= 2:
                        m.w(sg + f"t1_{p}_{ot - 2}", cc)
                        m.w(sg + f"y_{p}_{ot - 2}", cc)
                    else:
                        pm = 1 - p
                        if p == 1:
                            m.w(sg + f"t1_{pm}_{ot + 30}", cc)
                            m.w(sg + f"y_{pm}_{ot + 30}", cc)
                        else:
                            m.wprev(sg + f"t1_{pm}_{ot + 30}", cc)
                            m.wprev(sg + f"y_{pm}_{ot + 30}", cc)
                    for it in range(8):
                        E.matmul(out=ps[1 + 2 * a][:, 0:256],
                                 lhsT=AA[:, it * 4096 + ot * 128: it * 4096 + ot * 128 + 128],
                                 rhs=XN[:, noff + it * 256: noff + (it + 1) * 256],
                                 start=(it == 0), stop=(it == 7))
                    for it in range(8):
                        i = E.matmul(out=ps[2 + 2 * a][:, 0:256],
                                     lhsT=AB[:, it * 4096 + ot * 128: it * 4096 + ot * 128 + 128],
                                     rhs=XN[:, noff + it * 256: noff + (it + 1) * 256],
                                     start=(it == 0), stop=(it == 7))
                    m.op(i, sg + f"g_{p}_{ot}")
            if eng == "scalar":
                for ot in range(32):
                    a = ot % 2
                    m.w(sg + f"g_{p}_{ot}", cc)
                    if ot >= 2:
                        m.w(sg + f"y_{p}_{ot - 2}", cc)
                    else:
                        pm = 1 - p
                        if p == 1:
                            m.w(sg + f"y_{pm}_{ot + 30}", cc)
                        else:
                            m.wprev(sg + f"y_{pm}_{ot + 30}", cc)
                    i = E.activation(out=T1[:, (ot % 2) * 256:(ot % 2) * 256 + 256],
                                     in_=ps[1 + 2 * a][:, 0:256], func=AF.Silu,
                                     scale=H["s1c"][:, ot:ot + 1])
                    m.op(i, sg + f"t1_{p}_{ot}")
            if eng == "vector":
                for ot in range(32):
                    a = ot % 2
                    if ot == 0:
                        m.wsem(sg + "ysta", cg * 16)
                    elif ot == 16:
                        m.wsem(sg + "ystb", cg * 16)
                    m.w(sg + f"t1_{p}_{ot}", cc)
                    i = E.scalar_tensor_tensor(
                        out=PB[:, ot * 256:(ot + 1) * 256],
                        in0=ps[2 + 2 * a][:, 0:256],
                        scalar=H["s2c"][:, ot:ot + 1],
                        in1=T1[:, (ot % 2) * 256:(ot % 2) * 256 + 256],
                        op0=ALU.mult, op1=ALU.mult)
                    m.op(i, sg + f"y_{p}_{ot}")
            if eng == "sync":
                for hh in range(2):
                    m.w(sg + f"y_{p}_{15 if hh == 0 else 31}", cc)
                    E.dma_start(
                        out=H["ybuf"].rearrange("n p t -> p n t")[:, ts(hh, 16), :][:, :, ts(cg, 256)],
                        in_=PB.ap()[:, hh * 4096:(hh + 1) * 4096].rearrange(
                            "p (n t) -> p n t", t=256)
                    ).then_inc(m.sem(sg + "yst" + "ab"[hh]), 16)

    def epi(m):
        if eng == "tensor":
            m.epims(sg + "end_t", sg + "_t")
        if eng == "vector":
            m.epims(sg + "end_v", sg + "_v")

    m.run_stage(sg, n_iter, pro, body, epi, sq="f1")


def stage_a23(m, sg, src, dst, wsrc, scname, prev_end, whsem, srcw, prev_gst=None, prev_pb=None):
    H = m.H; E = m.E; eng = m.eng
    AA, AB, PB = H["AA"], H["AB"], H["PB"]
    XFB = H["XF"].ap().bitcast(BF16)
    ps = H["ps"]
    n_iter = KCFG["a23"] // 2

    def pro(m):
        if eng == "sync":
            m.w(prev_end)
        if eng == "gpsimd":
            m.w(prev_end)
            E.wait_ge(m.sem(whsem), 16)
        if eng == "scalar":
            m.w(prev_end)
            if prev_gst is not None:
                m.wsem(prev_gst, KCFG["a23"] * 2 * 2 * 16)
            if prev_pb is not None:
                m.wsem(prev_pb[0], prev_pb[1])
                m.wsem(prev_pb[0][:-1] + "a", prev_pb[1])
        if eng == "tensor":
            m.credit(sg + "ssd", 2)
            m.credit(sg + "yfr0", 1)
            m.credit(sg + "yfr1", 1)

    def body(m, cc):
        for p in range(2):
            c = cc * 2 + p
            yslot = AA if p == 0 else AB
            if eng == "sync":
                m.wsem(sg + "yfr" + str(p), cc)
                srcw(m, cc, p)
                E.dma_start(
                    out=yslot.ap().rearrange("p (n t) -> p n t", t=1024),
                    in_=src.rearrange("n p t -> p n t")[:, :, ts(c, 1024)]
                ).then_inc(m.sem(sg + "ldy" + str(p)), 16)
                for q in range(4):
                    m.wsem(sg + "_s", c * 64 + (q + 1) * 16)
                    E.dma_start(
                        out=dst.rearrange("n p t -> p n t")[:, ts(q, 8), :][:, :, ts(c, 1024)],
                        in_=PB.ap().rearrange("p (n t) -> p n t", t=1024)
                    ).then_inc(m.sem(sg + "gst"), 16)
            if eng == "gpsimd":
                def gslab(sccc):
                    for si in range(2):
                        s = sccc * 2 + si
                        slot = si % 2
                        m.wsem(sg + "ssd", c * 16 + s - 1)
                        E.dma_start(
                            out=XFB[:, slot * 8192:(slot + 1) * 8192].rearrange(
                                "p (n o) -> p n o", o=256),
                            in_=wsrc.rearrange("(n p) o -> p n o", p=128)[:, :, ts(s, 256)]
                        ).then_inc(m.sem(sg + "lds" + str(slot)), 16)
                m.inner(8, gslab)
            if eng == "tensor":
                m.wsem(sg + "ldy" + str(p), (cc + 1) * 16)
                def tslab(sccc):
                    for si in range(2):
                        s = sccc * 2 + si
                        slot = si % 2
                        m.wsem(sg + "lds" + str(slot),
                               (16 * cc + 8 * p + sccc + 1) * 16)
                        m.wsem(sg + "sse", c * 16 + s)
                        for ot2 in range(2):
                            for th in range(2):
                                for it in range(32):
                                    i = E.matmul(
                                        out=ps[ot2 * 2 + th][:, 0:512],
                                        lhsT=XFB[:, slot * 8192 + it * 256 + ot2 * 128:
                                                slot * 8192 + it * 256 + ot2 * 128 + 128],
                                        rhs=yslot[:, it * 1024 + th * 512:
                                                  it * 1024 + th * 512 + 512],
                                        start=(it == 0), stop=(it == 31))
                                m.op(i)
                        m.wsem(sg + "_t", c * 64 + (s + 1) * 4)
                        m.credit(sg + "ssd", 1)
                m.inner(8, tslab)
                m.credit(sg + "yfr" + str(p), 1)
            if eng == "scalar":
                def sslab(sccc):
                    for si in range(2):
                        s = sccc * 2 + si
                        for ot2 in range(2):
                            for th in range(2):
                                os = s * 2 + ot2
                                m.wsem(sg + "_t", c * 64 + s * 4 + ot2 * 2 + th + 1)
                                m.wsem(sg + "gst", (c * 4 + sccc // 2) * 16)
                                i = E.activation(
                                    out=PB[:, ts(((sccc * 4 + si * 2 + ot2) % 8) * 2
                                                 + th, 512)],
                                    in_=ps[ot2 * 2 + th][:, 0:512], func=AF.Gelu,
                                    scale=H[scname][:, ds(os, 1)])
                                m.op(i)
                        m.wsem(sg + "_s", c * 64 + (s + 1) * 4)
                        m.credit(sg + "sse", 1)
                m.inner(8, sslab)

    def epi(m):
        if eng == "tensor":
            m.epims(sg + "end_t", sg + "_t")

    m.run_stage(sg, n_iter, pro, body, epi, sq="f2")

def stage_a4(m, sg, gsrc, xsrc, xdst, prev_end, a1_v_end, srcw, prev_gst=None):
    H = m.H; E = m.E; eng = m.eng
    AA, AB, XF = H["AA"], H["AB"], H["XF"]
    ps = H["ps"]
    n_iter = KCFG["a4"]

    def pro(m):
        if eng == "gpsimd":
            m.w(prev_end)
            E.dma_start(out=AA.ap().rearrange("p (n o) -> p n o", o=1024),
                        in_=H["w3T"].rearrange("(n p) o -> p n o", p=128)
                        ).then_inc(m.sem(sg + "w3"), 16)
        if eng == "sync":
            m.w(prev_end)
            m.w(a1_v_end)
            if prev_gst is not None:
                m.wsem(prev_gst, KCFG["a23"] * 2 * 2 * 16)
        if eng == "tensor":
            m.wsem0(sg + "w3", 16)
        if eng == "vector":
            m.w("s3h")

    def body(m, cc):
        for p in range(2):
            c = cc * 2 + p
            xoff = p * 4096
            if eng == "sync":
                m.wsem(sg + "_ti", cc)
                srcw(m, cc, p)
                E.dma_start(
                    out=AB.ap()[:, p * 16384:(p + 1) * 16384].rearrange("p (n t) -> p n t", t=512),
                    in_=gsrc.rearrange("n p t -> p n t")[:, :, ts(c, 512)]
                ).then_inc(m.sem(sg + "ldg" + str(p)), 16)
                E.dma_start(
                    out=XF.ap()[:, xoff:xoff + 4096].rearrange("p (n t) -> p n t", t=512),
                    in_=xsrc.rearrange("n p t -> p n t")[:, :, ts(c, 512)]
                ).then_inc(m.sem(sg + "ldx" + str(p)), 16)
                m.w(sg + f"x1_{p}", cc)
                E.dma_start(
                    out=xdst.rearrange("n p t -> p n t")[:, :, ts(c, 512)],
                    in_=XF.ap()[:, xoff:xoff + 4096].rearrange("p (n t) -> p n t", t=512)
                ).then_inc(m.sem(sg + "xst"), 16)
                m.wsem(sg + "xst", (c + 1) * 16)
            if eng == "tensor":
                m.wsem(sg + "ldg" + str(p), (cc + 1) * 16)
                for ot in range(8):
                    if ot >= 2:
                        m.w(sg + f"x1e_{p}_{ot - 2}", cc)
                    else:
                        pm = 1 - p
                        if p == 1:
                            m.w(sg + f"x1e_{pm}_{ot + 6}", cc)
                        else:
                            m.wprev(sg + f"x1e_{pm}_{ot + 6}", cc)
                    for it in range(32):
                        i = E.matmul(
                            out=ps[ot % 2][:, 0:512],
                            lhsT=AA[:, it * 1024 + ot * 128: it * 1024 + ot * 128 + 128],
                            rhs=AB[:, p * 16384 + it * 512: p * 16384 + (it + 1) * 512],
                            start=(it == 0), stop=(it == 31))
                    m.op(i, sg + f"g_{p}_{ot}")
            if eng == "vector":
                m.wsem(sg + "ldx" + str(p), (cc + 1) * 16)
                for ot in range(8):
                    m.w(sg + f"g_{p}_{ot}", cc)
                    xt = XF[:, xoff + ot * 512: xoff + (ot + 1) * 512]
                    i = E.scalar_tensor_tensor(
                        out=xt, in0=ps[ot % 2][:, 0:512],
                        scalar=H["s3h"][:, ot:ot + 1], in1=xt,
                        op0=ALU.mult, op1=ALU.add)
                    m.op(i, sg + f"x1e_{p}_{ot}")
                    if ot == 7:
                        if m.count_mode:
                            m.S["reg"][sg + f"x1_{p}"] = m.S["reg"][sg + f"x1e_{p}_{ot}"]

    def epi(m):
        if eng == "sync":
            m.epims(sg + "end_st", sg + "_sy")
        if eng == "vector":
            m.epims(sg + "end_v", sg + "_v")
        if eng == "tensor":
            m.epims(sg + "end_t", sg + "_t")

    m.run_stage(sg, n_iter, pro, body, epi, sq="f4")


def stage_b1(m, sg="b1"):
    H = m.H; E = m.E; eng = m.eng
    AA, AB, XF, XN, PB, PF = H["AA"], H["AB"], H["XF"], H["XN"], H["PB"], H["PF"]
    STT = H["STT"]
    ps = H["ps"]; pstat = H["pstat"]; pbf = H["pbf"]
    n_iter = KCFG["b1"]
    QO, KO, VO, OFM = 0, 1024, 2048, 3072
    PT = 4096
    XC, YO, SB, QKT, KH = 4992, 5120, 5248, 5376, 5504
    PT_B, XC_B, YO_B, SB_B, QKT_B, KH_B = 5632, 6528, 6656, 6784, 6912, 7040
    EO, ETO, FBO, FBTO, U1O = 0, 128, 256, 384, 512
    TRLG, TRSG, GCO, GCT = 640, 656, 672, 680
    SOFF = 1024
    RAW, T16A, T16B, Z16 = 2048, 2176, 2304, 2432
    EO_B, ETO_B, FBO_B = 2048, 2176, 2304
    FBTO_B, U1O_B, KF_A, KF_B = 4096, 4224, 4352, 4480
    GCT_B = 681
    VF = 7168

    def pro(m):
        if eng == "gpsimd":
            m.w("a4end_t")
            for nm, off in [("Wq", 0), ("Wk", 8192), ("Wv", 16384), ("Wo", 24576)]:
                E.dma_start(out=AB.ap()[:, off:off + 8192].rearrange("p (n o) -> p n o", o=1024),
                            in_=H[nm].rearrange("(n p) o -> p n o", p=128)
                            ).then_inc(m.sem("w_" + nm.lower()), 16)
            E.dma_start(out=H["WBA"].ap().rearrange("p (n o) -> p n o", o=16),
                        in_=H["Wba"].rearrange("(n p) o -> p n o", p=128)
                        ).then_inc(m.sem("w_wba"), 16)
            E.dma_start(out=AA.ap().rearrange("p (n o) -> p n o", o=4096),
                        in_=H["W1"].rearrange("(n p) o -> p n o", p=128)
                        ).then_inc(m.sem("w_bw1"), 16)
        if eng == "tensor":
            for s_ in ["w_wq", "w_wk", "w_wv", "w_wo", "w_wba"]:
                E.wait_ge(m.sem(s_), 16)
            m.w("a4end_t")
        if eng == "vector":
            m.w("a4end_v")
            m.op(E.memset(PF[:, SOFF:SOFF + 1024], 0.0))
            m.op(E.memset(PF[0:16, Z16:Z16 + 128], 0.0))
        if eng == "sync":
            m.w("a4end_st")
        if eng == "scalar":
            m.w("a4end_t")

    def body(m, cc):
        for p in range(2):
            cg = cc * 2 + p
            x1o = p * 1024
            xao = 2048 + p * 1024
            xpo = 5120 + p * 1024
            no = p * 1024
            xbo = 2048 + p * 1024
            ro = p * 1024
            if p == 0:
                m.lbase(cc)
            if eng == "sync":
                m.wprev(sg + f"pj_{p}", cc)
                m.wsem("a4xst", ((cc * 2 + p) // 4 + 1) * 16)
                E.dma_start(
                    out=XF.ap()[:, x1o:x1o + 1024].rearrange("p (n t) -> p n t", t=128),
                    in_=H["x1b"].rearrange("n p t -> p n t")[:, :, ts(cg, 128)]
                ).then_inc(m.sem(sg + "ldx" + str(p)), 16)
                m.w(sg + f"woe_{p}_7", cc)
                E.dma_start(
                    out=H["xpb"].rearrange("n p t -> p n t")[:, :, ts(cg, 128)],
                    in_=XF.ap()[:, xpo:xpo + 1024].rearrange("p (n t) -> p n t", t=128)
                ).then_inc(m.sem(sg + "xst"), 16)
                m.wsem(sg + "xst", (cg + 1) * 16)
                m.op(E.nop(), sg + f"sst_{p}")

            L = m.lad
            mrow = STT[0:1, ro:ro + 128]
            erow = STT[0:1, ro + 128:ro + 256]
            m2row = STT[0:1, ro + 256:ro + 384]
            L("vector", lambda E_: E_.tensor_copy(out=XN[:, xbo:xbo + 1024],
                                                  in_=XF[:, x1o:x1o + 1024]),
              extra_waits=((sg + "ldx" + str(p), (cc + 1) * 16),))
            def _s1(E_, src_off=xbo):
                for it in range(8):
                    i = E_.matmul(out=pstat[0:1, 0:128], lhsT=H["onesb"][:, 0:1],
                                  rhs=XN[:, src_off + it * 128: src_off + (it + 1) * 128],
                                  start=(it == 0), stop=(it == 7))
                return i
            L("tensor", _s1)
            L("scalar", lambda E_: E_.activation(out=XN[:, xbo:xbo + 1024],
                                                 in_=XN[:, xbo:xbo + 1024], func=AF.Square))
            def _s2(E_, src_off=xbo):
                for it in range(8):
                    i = E_.matmul(out=pstat[0:1, 128:256], lhsT=H["onesb"][:, 0:1],
                                  rhs=XN[:, src_off + it * 128: src_off + (it + 1) * 128],
                                  start=(it == 0), stop=(it == 7))
                return i
            L("tensor", _s2)
            L("vector", lambda E_: E_.tensor_scalar(out=mrow, in0=pstat[0:1, 0:128],
                                                    scalar1=1.0 / 1024, scalar2=None, op0=ALU.mult))
            L("vector", lambda E_: E_.tensor_scalar(out=erow, in0=pstat[0:1, 128:256],
                                                    scalar1=1.0 / 1024, scalar2=None, op0=ALU.mult))
            L("scalar", lambda E_: E_.activation(out=m2row, in_=mrow, func=AF.Square))
            L("vector", lambda E_: E_.tensor_sub(out=erow, in0=erow, in1=m2row))
            L("scalar", lambda E_: E_.activation(out=erow, in_=erow, func=AF.Sqrt, bias=1e-5))
            L("vector", lambda E_: E_.reciprocal(out=erow, in_=erow))
            L("tensor", lambda E_: E_.matmul(out=ps[0][:, 0:128], lhsT=H["onesf"][:],
                                             rhs=mrow, start=True, stop=True))
            L("tensor", lambda E_: E_.matmul(out=ps[0][:, 128:256], lhsT=H["onesf"][:],
                                             rhs=erow, start=True, stop=True))
            for it in range(8):
                tt = XF[:, 4096 + it * 128: 4096 + (it + 1) * 128]
                pre0 = None
                if it == 0:
                    pre0 = (lambda mm_: mm_.wprev(sg + f"woe_{p}_7", cc))
                L("vector", lambda E_, it=it, tt=tt: E_.tensor_sub(
                    out=tt, in0=XF[:, x1o + it * 128: x1o + (it + 1) * 128],
                    in1=ps[0][:, 0:128]), pre=pre0)
                L("vector", lambda E_, it=it, tt=tt: E_.tensor_mul(
                    out=XF[:, xao + it * 128: xao + (it + 1) * 128],
                    in0=tt, in1=ps[0][:, 128:256]),
                  name=(f"xa{p}" if it == 7 else None))
            m4row = STT[0:1, ro + 384:ro + 512]
            e4row = STT[0:1, ro + 512:ro + 640]
            m24row = STT[0:1, ro + 640:ro + 768]
            L("vector", lambda E_: E_.tensor_copy(out=XN[:, xbo:xbo + 1024],
                                                  in_=XF[:, xao:xao + 1024]))
            L("tensor", _s1)
            L("scalar", lambda E_: E_.activation(out=XN[:, xbo:xbo + 1024],
                                                 in_=XN[:, xbo:xbo + 1024], func=AF.Square))
            L("tensor", _s2)
            L("vector", lambda E_: E_.tensor_scalar(out=m4row, in0=pstat[0:1, 0:128],
                                                    scalar1=1.0 / 1024, scalar2=None, op0=ALU.mult))
            L("vector", lambda E_: E_.tensor_scalar(out=e4row, in0=pstat[0:1, 128:256],
                                                    scalar1=1.0 / 1024, scalar2=None, op0=ALU.mult))
            L("scalar", lambda E_: E_.activation(out=m24row, in_=m4row, func=AF.Square))
            L("vector", lambda E_: E_.tensor_sub(out=e4row, in0=e4row, in1=m24row))
            L("scalar", lambda E_: E_.activation(out=e4row, in_=e4row, func=AF.Sqrt, bias=1e-5))
            L("vector", lambda E_: E_.reciprocal(out=e4row, in_=e4row))
            L("tensor", lambda E_: E_.matmul(out=ps[0][:, 256:384], lhsT=H["onesf"][:],
                                             rhs=m4row, start=True, stop=True))
            L("tensor", lambda E_: E_.matmul(out=ps[0][:, 384:512], lhsT=H["onesf"][:],
                                             rhs=e4row, start=True, stop=True))
            for it in range(8):
                tt = XF[:, 4096 + it * 128: 4096 + (it + 1) * 128]
                L("vector", lambda E_, it=it, tt=tt: E_.tensor_sub(
                    out=tt, in0=XF[:, xao + it * 128: xao + (it + 1) * 128],
                    in1=ps[0][:, 256:384]))
                L("vector", lambda E_, it=it, tt=tt: E_.tensor_mul(
                    out=tt, in0=tt, in1=ps[0][:, 384:512]))
                L("vector", lambda E_, it=it, tt=tt: E_.tensor_scalar(
                    out=XN[:, no + it * 128: no + (it + 1) * 128],
                    in0=tt, scalar1=H["ln1wc"][:, it:it + 1],
                    scalar2=H["ln1bc"][:, it:it + 1], op0=ALU.mult, op1=ALU.add),
                  name=(f"n{p}" if it == 7 else None))
            PRJORD = [0, 1, 8, 9, 16, 17, 2, 3, 10, 11, 18, 19,
                      4, 5, 12, 13, 20, 21, 6, 7, 14, 15, 22, 23]

            def proj_steps(idx):
                g = PRJORD[idx]
                wi, ot = g // 8, g % 8
                bank = ps[5] if idx % 2 == 0 else ps[0]
                boff = 0 if idx % 2 == 0 else 128
                woff = wi * 8192

                def tpre(mm, idx=idx):
                    if idx >= 2:
                        mm.w(sg + f"pse_{p}_{PRJORD[idx - 2]}", cc)
                    elif idx == 0:
                        wpc(mm, sg + "pse_{p}_22", p, cc)

                def temit(E_, ot=ot, woff=woff, bank=bank, boff=boff):
                    for it in range(8):
                        i = E_.matmul(out=bank[:, boff:boff + 128],
                                      lhsT=AB[:, woff + it * 1024 + ot * 128:
                                              woff + it * 1024 + ot * 128 + 128],
                                      rhs=XN[:, no + it * 128: no + (it + 1) * 128],
                                      start=(it == 0), stop=(it == 7))
                    return i

                def spre(mm, g=g, wi=wi, ot=ot):
                    mm.w(sg + f"pjg_{p}_{g}", cc)
                    fmt = {0: "s74", 1: "s54", 2: "s55"}[wi]
                    wpc(mm, sg + fmt + "_{p}_" + str(ot), p, cc)

                def semit(E_, wi=wi, ot=ot, bank=bank, boff=boff):
                    if wi == 0:
                        dst = PB[:, QO + ot * 128: QO + (ot + 1) * 128]
                    elif wi == 1:
                        dst = PB[:, KO + ot * 128: KO + (ot + 1) * 128]
                    else:
                        dst = XF[:, VF + ot * 128: VF + (ot + 1) * 128]
                    return E_.activation(out=dst, in_=bank[:, boff:boff + 128],
                                         func=AF.Silu)

                return [("tensor", sg + f"pjg_{p}_{g}", tpre, temit),
                        ("scalar", sg + f"pse_{p}_{g}", spre, semit)]

            def run_steps(stlist):
                for e_, nm_, pre_, emit_ in stlist:
                    if eng == e_:
                        if pre_ is not None:
                            pre_(m)
                        m.op(emit_(E), nm_)
                        if nm_ == sg + f"pjg_{p}_23" and m.count_mode:
                            m.S["reg"][sg + f"pj_{p}"] = m.S["reg"][nm_]

            if eng == "tensor":
                m.ladwait(f"n{p}", cc)
                for it in range(8):
                    i = E.matmul(out=pbf[0:16, 0:128],
                                 lhsT=H["WBA"][:, it * 16:(it + 1) * 16],
                                 rhs=XN[:, no + it * 128: no + (it + 1) * 128],
                                 start=(it == 0), stop=(it == 7))
                m.op(i, sg + f"bap_{p}")
            for idx in range(6):
                run_steps(proj_steps(idx))

            L("vector", lambda E_: E_.tensor_copy(out=PF[0:16, RAW:RAW + 128],
                                                  in_=pbf[0:16, 0:128]),
              pre=(lambda mm_: mm_.w(sg + f"bap_{p}", cc)))
            L("scalar", lambda E_: E_.activation(out=PF[0:16, T16A:T16A + 128],
                                                 in_=PF[0:16, RAW:RAW + 128], func=AF.Abs))
            L("scalar", lambda E_: E_.activation(out=PF[0:16, T16A:T16A + 128],
                                                 in_=PF[0:16, T16A:T16A + 128], func=AF.Exp, scale=-1.0))
            L("scalar", lambda E_: E_.activation(out=PF[0:16, T16A:T16A + 128],
                                                 in_=PF[0:16, T16A:T16A + 128], func=AF.Ln, bias=1.0))
            L("scalar", lambda E_: E_.activation(out=PF[0:16, T16B:T16B + 128],
                                                 in_=PF[0:16, RAW:RAW + 128], func=AF.Relu, scale=-1.0))
            L("vector", lambda E_: E_.tensor_add(out=PF[0:16, T16A:T16A + 128],
                                                 in0=PF[0:16, T16A:T16A + 128],
                                                 in1=PF[0:16, T16B:T16B + 128]))
            L("vector", lambda E_: E_.tensor_scalar(out=PF[0:16, T16A:T16A + 128],
                                                    in0=PF[0:16, T16A:T16A + 128],
                                                    scalar1=-1.0, scalar2=None, op0=ALU.mult))
            L("scalar", lambda E_: E_.activation(out=PF[0:16, T16B:T16B + 128],
                                                 in_=PF[0:16, RAW:RAW + 128], func=AF.Sigmoid))
            L("vector", lambda E_: E_.tensor_tensor_scan(out=PF[0:16, RAW:RAW + 128],
                                                         data0=PF[0:16, T16A:T16A + 128],
                                                         data1=PF[0:16, Z16:Z16 + 128],
                                                         initial=0.0, op0=ALU.add, op1=ALU.add))
            L("tensor", lambda E_: E_.transpose(out=pbf[0:128, 128:144],
                                                in_=PF[0:16, RAW:RAW + 128],
                                                identity=H["idf"][0:16, 0:16]))
            L("vector", lambda E_: E_.tensor_copy(out=PF[:, TRLG:TRLG + 16],
                                                  in_=pbf[0:128, 128:144]))
            L("tensor", lambda E_: E_.transpose(out=pbf[0:128, 128:144],
                                                in_=PF[0:16, T16B:T16B + 128],
                                                identity=H["idf"][0:16, 0:16]))
            L("vector", lambda E_: E_.tensor_copy(out=PF[:, TRSG:TRSG + 16],
                                                  in_=pbf[0:128, 128:144]))
            L("scalar", lambda E_: E_.activation(out=PF[:, GCO:GCO + 8],
                                                 in_=PF[:, TRLG:TRLG + 8], func=AF.Exp),
              name=f"ba{p}")

            def hsteps(h):
                ch = h % 2
                A0 = ps[1] if ch == 0 else ps[3]
                B0 = ps[2] if ch == 0 else ps[4]
                EOc = (EO, EO_B)[ch]; ETOc = (ETO, ETO_B)[ch]
                FBOc = (FBO, FBO_B)[ch]
                FBTa = (PF[:, FBTO:FBTO + 128], XF[:, FBTO_B:FBTO_B + 128])[ch]
                U1a = (PF[:, U1O:U1O + 128], XF[:, U1O_B:U1O_B + 128])[ch]
                KFa = XF[:, (KF_A, KF_B)[ch]:(KF_A, KF_B)[ch] + 128]
                GCTc = (GCT, GCT_B)[ch]
                PTc = (PT, PT_B)[ch]; XCc = (XC, XC_B)[ch]
                YOc = (YO, YO_B)[ch]; SBc = (SB, SB_B)[ch]
                QKc = (QKT, QKT_B)[ch]; KHc = (KH, KH_B)[ch]
                sqo = xbo + ch * 256
                rowo = (ro + 768) if ch == 0 else (ro + 256)
                qs = PB[:, QO + h * 128: QO + (h + 1) * 128]
                ks = PB[:, KO + h * 128: KO + (h + 1) * 128]
                Sh = PF[:, SOFF + h * 128: SOFF + (h + 1) * 128]
                lgc = PF[:, TRLG + h:TRLG + h + 1]
                bcl = PF[:, TRSG + 8 + h:TRSG + 8 + h + 1]
                gcl = PF[:, GCO + h:GCO + h + 1]
                qkrow = STT[0:1, rowo:rowo + 256]
                VFt = XF[:, VF + h * 128: VF + (h + 1) * 128]

                def N(base, hh=h):
                    return sg + base + f"_{p}_{hh}"

                def WN(*bases):
                    def f(mm):
                        for b_ in bases:
                            mm.w(N(b_), cc)
                    return f

                St = []

                def add(e_, nm_, pre_, emit_):
                    St.append((e_, N(nm_), pre_, emit_))

                add("vector", "s1",
                    lambda mm: wpc(mm, sg + "s84_{p}_" + str(h), p, cc),
                    lambda E_: E_.tensor_copy(out=PB[:, SBc:SBc + 128], in_=Sh))

                def pre16(mm):
                    mm.ladwait(f"ba{p}", cc)
                    if h >= 2:
                        mm.w(N("s21", h - 2), cc)
                        mm.w(N("s23", h - 2), cc)
                    else:
                        wpc(mm, sg + "s21_{p}_" + str(h + 6), p, cc)
                        wpc(mm, sg + "s23_{p}_" + str(h + 6), p, cc)
                add("vector", "s16", pre16,
                    lambda E_: E_.tensor_copy(out=PF[:, EOc:EOc + 128],
                                              in_=lgc.broadcast_to([128, 128])))

                def pre2a(mm):
                    mm.w(sg + f"pse_{p}_{h}", cc)
                    if h >= 2:
                        mm.w(N("s3", h - 2), cc)
                add("scalar", "s2a", pre2a,
                    lambda E_: E_.activation(out=XN[:, sqo:sqo + 128], in_=qs,
                                             func=AF.Square))
                add("scalar", "s2b",
                    lambda mm: mm.w(sg + f"pse_{p}_{8 + h}", cc),
                    lambda E_: E_.activation(out=XN[:, sqo + 128:sqo + 256], in_=ks,
                                             func=AF.Square))

                def pre17(mm):
                    mm.w(N("s16"), cc)
                    if h >= 2:
                        mm.w(N("s81", h - 2), cc)
                    elif h == 0:
                        wpc(mm, sg + "woe_{p}_7", p, cc)
                    else:
                        wpc(mm, sg + "s81_{p}_7", p, cc)
                add("tensor", "s17", pre17,
                    lambda E_: E_.transpose(out=B0[:, 0:128], in_=PF[:, EOc:EOc + 128],
                                            identity=H["idf"][:]))

                def pre3(mm):
                    mm.w(N("s2b"), cc)
                    if h >= 2:
                        mm.w(N("s84", h - 2), cc)
                    elif h == 0:
                        wpc(mm, sg + "woe_{p}_6", p, cc)
                    else:
                        wpc(mm, sg + "s84_{p}_7", p, cc)
                add("tensor", "s3", pre3,
                    lambda E_: E_.matmul(out=A0[0:1, 0:256], lhsT=H["onesb"][:, 0:1],
                                         rhs=XN[:, sqo:sqo + 256], start=True, stop=True))
                add("vector", "s18", WN("s17"),
                    lambda E_: E_.tensor_sub(out=PF[:, EOc:EOc + 128],
                                             in0=PF[:, EOc:EOc + 128], in1=B0[:, 0:128]))
                add("scalar", "s4", WN("s3"),
                    lambda E_: E_.activation(out=qkrow, in_=A0[0:1, 0:256],
                                             func=AF.Sqrt, bias=1e-6))
                add("vector", "s19", WN("s18"),
                    lambda E_: E_.tensor_scalar(out=PF[:, EOc:EOc + 128],
                                                in0=PF[:, EOc:EOc + 128],
                                                scalar1=0.0, scalar2=None, op0=ALU.min))
                add("scalar", "s20", WN("s19"),
                    lambda E_: E_.activation(out=PF[:, EOc:EOc + 128],
                                             in_=PF[:, EOc:EOc + 128], func=AF.Exp))
                add("vector", "s5", WN("s4"),
                    lambda E_: E_.reciprocal(out=qkrow, in_=qkrow))
                add("tensor", "s21", WN("s20"),
                    lambda E_: E_.transpose(out=B0[:, 128:256], in_=PF[:, EOc:EOc + 128],
                                            identity=H["idf"][:]))
                add("tensor", "s6", WN("s5"),
                    lambda E_: E_.matmul(out=A0[:, 128:384], lhsT=H["onesf"][:],
                                         rhs=qkrow, start=True, stop=True))
                add("vector", "s22", WN("s21"),
                    lambda E_: E_.tensor_copy(out=PF[:, ETOc:ETOc + 128],
                                              in_=B0[:, 128:256]))
                add("vector", "s7", WN("s6"),
                    lambda E_: E_.tensor_mul(out=qs, in0=qs, in1=A0[:, 128:256]))
                add("vector", "s23", None,
                    lambda E_: E_.scalar_tensor_tensor(out=PF[:, FBOc:FBOc + 128],
                                                       in0=PF[:, EOc:EOc + 128],
                                                       scalar=bcl, in1=H["trils"][:],
                                                       op0=ALU.mult, op1=ALU.mult))
                add("vector", "s13", None,
                    lambda E_: E_.tensor_mul(out=ks, in0=ks, in1=A0[:, 256:384]))
                add("tensor", "s24", WN("s23", "s22"),
                    lambda E_: E_.transpose(out=B0[:, 256:384], in_=PF[:, FBOc:FBOc + 128],
                                            identity=H["idf"][:]))
                add("tensor", "s14", WN("s13"),
                    lambda E_: E_.matmul(out=A0[:, 0:128], lhsT=ks, rhs=ks,
                                         start=True, stop=True))
                add("scalar", "s25", WN("s24"),
                    lambda E_: E_.copy(out=FBTa, in_=B0[:, 256:384]))
                add("tensor", "s15", None,
                    lambda E_: E_.matmul(out=A0[:, 384:512], lhsT=ks, rhs=qs,
                                         start=True, stop=True))
                add("vector", "s26", WN("s15"),
                    lambda E_: E_.tensor_mul(out=PB[:, XCc:XCc + 128],
                                             in0=A0[:, 0:128], in1=PF[:, FBOc:FBOc + 128]))
                add("vector", "s27", WN("s25"),
                    lambda E_: E_.tensor_mul(out=PB[:, PTc:PTc + 128],
                                             in0=A0[:, 0:128], in1=FBTa))
                add("vector", "s13b", None,
                    lambda E_: E_.tensor_copy(out=KFa, in_=ks))
                add("vector", "s28", None,
                    lambda E_: E_.tensor_mul(out=U1a,
                                             in0=PF[:, ETOc:ETOc + 128],
                                             in1=H["triui"][:]))
                add("vector", "s29", WN("s28"),
                    lambda E_: E_.tensor_mul(out=PB[:, QKc:QKc + 128],
                                             in0=A0[:, 384:512], in1=U1a))
                for j in range(1, 7):
                    pjm = PTc + (j - 1) * 128
                    if j == 1:
                        pa = WN("s26", "s27", "s29")
                        pb_ = WN("s18")
                    else:
                        pa = WN(f"sqc{j - 1}", f"sqd{j - 1}")
                        pb_ = None
                    add("tensor", f"sqa{j}", pa,
                        lambda E_, pjm=pjm: E_.matmul(out=A0[:, 128:256],
                                                      lhsT=PB[:, pjm:pjm + 128],
                                                      rhs=PB[:, XCc:XCc + 128],
                                                      start=True, stop=True))
                    add("tensor", f"sqb{j}", pb_,
                        lambda E_, pjm=pjm: E_.matmul(out=B0[:, 0:128],
                                                      lhsT=PB[:, XCc:XCc + 128],
                                                      rhs=PB[:, pjm:pjm + 128],
                                                      start=True, stop=True))
                    add("vector", f"sqc{j}", WN(f"sqb{j}"),
                        lambda E_: E_.tensor_copy(out=PB[:, XCc:XCc + 128],
                                                  in_=A0[:, 128:256]))
                    add("scalar", f"sqd{j}", WN(f"sqb{j}"),
                        lambda E_, j=j: E_.copy(out=PB[:, PTc + j * 128:PTc + (j + 1) * 128],
                                                in_=B0[:, 0:128]))
                add("tensor", "s54", WN("sqc6", "s1"),
                    lambda E_: E_.matmul(out=A0[:, 256:384], lhsT=ks,
                                         rhs=PB[:, SBc:SBc + 128], start=True, stop=True))

                def pre55(mm):
                    mm.w(N("sqd6"), cc)
                    mm.w(N("s22"), cc)
                    mm.w(sg + f"pse_{p}_{16 + h}", cc)
                add("tensor", "s55", pre55,
                    lambda E_: E_.transpose(out=B0[:, 128:256], in_=VFt,
                                            identity=H["idf"][:]))
                add("vector", "s56", WN("s54"),
                    lambda E_: E_.tensor_scalar(out=U1a,
                                                in0=A0[:, 256:384], scalar1=gcl,
                                                scalar2=bcl, op0=ALU.mult, op1=ALU.mult))
                add("vector", "s57", WN("s55", "s56"),
                    lambda E_: E_.scalar_tensor_tensor(out=PB[:, YOc:YOc + 128],
                                                       in0=B0[:, 128:256], scalar=bcl,
                                                       in1=U1a,
                                                       op0=ALU.mult, op1=ALU.subtract))
                for j in range(6, -1, -1):
                    pj = PTc + j * 128
                    pa = WN("s57") if j == 6 else WN(f"ap{j + 1}")
                    add("tensor", f"apm{j}", pa,
                        lambda E_, pj=pj: E_.matmul(out=B0[:, 256:384],
                                                    lhsT=PB[:, pj:pj + 128],
                                                    rhs=PB[:, YOc:YOc + 128],
                                                    start=True, stop=True))
                    add("vector", f"ap{j}" if j > 0 else "s73", WN(f"apm{j}"),
                        lambda E_, j=j: E_.tensor_tensor(out=PB[:, YOc:YOc + 128],
                                                         in0=PB[:, YOc:YOc + 128],
                                                         in1=B0[:, 256:384],
                                                         op=(ALU.add if j > 0 else ALU.subtract)))
                add("tensor", "s74", None,
                    lambda E_: E_.matmul(out=A0[:, 384:512], lhsT=qs,
                                         rhs=PB[:, SBc:SBc + 128], start=True, stop=True))
                add("vector", "s75", WN("s74"),
                    lambda E_: E_.tensor_scalar(out=PB[:, XCc:XCc + 128],
                                                in0=A0[:, 384:512], scalar1=gcl,
                                                scalar2=None, op0=ALU.mult))
                add("tensor", "s76", WN("s73", "s75"),
                    lambda E_: E_.matmul(out=A0[:, 128:256], lhsT=PB[:, QKc:QKc + 128],
                                         rhs=PB[:, YOc:YOc + 128], start=True, stop=True))
                add("vector", "s77", WN("s76", "s75"),
                    lambda E_: E_.tensor_tensor(out=U1a,
                                                in0=PB[:, XCc:XCc + 128],
                                                in1=A0[:, 128:256], op=ALU.add))
                add("tensor", "s78", WN("s77"),
                    lambda E_: E_.transpose(out=B0[:, 128:256], in_=U1a,
                                            identity=H["idf"][:]))

                def pre79(mm):
                    mm.w(N("s78"), cc)
                    wpc(mm, sg + "wog_{p}_7", p, cc)
                add("scalar", "s79", pre79,
                    lambda E_: E_.copy(out=PB[:, OFM + h * 128: OFM + (h + 1) * 128],
                                       in_=B0[:, 128:256]))
                add("tensor", "s80", WN("s79"),
                    lambda E_: E_.transpose(out=B0[:, 0:128], in_=KFa,
                                            identity=H["idf"][:]))
                add("vector", "s81", WN("s80"),
                    lambda E_: E_.tensor_scalar(out=PB[:, KHc:KHc + 128],
                                                in0=B0[:, 0:128],
                                                scalar1=PF[:, ETOc + 127:ETOc + 128],
                                                scalar2=None, op0=ALU.mult))
                add("tensor", "s82", WN("s81"),
                    lambda E_: E_.matmul(out=A0[:, 256:384], lhsT=PB[:, KHc:KHc + 128],
                                         rhs=PB[:, YOc:YOc + 128], start=True, stop=True))
                add("vector", "s83", None,
                    lambda E_: E_.tensor_mul(out=PF[:, GCTc:GCTc + 1], in0=gcl,
                                             in1=PF[:, ETOc + 127:ETOc + 128]))
                add("vector", "s84", WN("s82", "s83"),
                    lambda E_: E_.scalar_tensor_tensor(out=Sh, in0=Sh,
                                                       scalar=PF[:, GCTc:GCTc + 1],
                                                       in1=A0[:, 256:384],
                                                       op0=ALU.mult, op1=ALU.add))
                return St

            for pr in range(4):
                a_ = hsteps(2 * pr)
                b_ = hsteps(2 * pr + 1)
                merged = []
                for k_ in range(max(len(a_), len(b_))):
                    if k_ < len(a_):
                        merged.append(a_[k_])
                    if k_ < len(b_):
                        merged.append(b_[k_])
                if pr < 3:
                    extra = []
                    for idx in range(6 * (pr + 1), 6 * (pr + 1) + 6):
                        extra += proj_steps(idx)
                    stride = max(1, len(merged) // (len(extra) + 1))
                    out_ = []
                    ei = 0
                    for k_, stp in enumerate(merged):
                        out_.append(stp)
                        if ei < len(extra) and (k_ + 1) % stride == 0:
                            out_.append(extra[ei])
                            ei += 1
                    out_.extend(extra[ei:])
                    merged = out_
                run_steps(merged)

            if eng == "tensor":
                m.w(sg + f"s79_{p}_6", cc)
                m.w(sg + f"s79_{p}_7", cc)
                for ot in range(8):
                    bank = ps[1 + ot % 2]
                    if ot >= 2:
                        m.w(sg + f"woe_{p}_{ot - 2}", cc)
                    elif ot == 0:
                        m.w(sg + f"s84_{p}_6", cc)
                    else:
                        m.w(sg + f"s81_{p}_6", cc)
                    for it in range(8):
                        i = E.matmul(out=bank[:, 0:128],
                                     lhsT=AB[:, 24576 + it * 1024 + ot * 128:
                                             24576 + it * 1024 + ot * 128 + 128],
                                     rhs=PB[:, OFM + it * 128: OFM + (it + 1) * 128],
                                     start=(it == 0), stop=(it == 7))
                    m.op(i, sg + f"wog_{p}_{ot}")
            if eng == "vector":
                m.wprev(sg + f"sst_{p}", cc)
                for ot in range(8):
                    m.w(sg + f"wog_{p}_{ot}", cc)
                    i = E.tensor_tensor(out=XF[:, xpo + ot * 128: xpo + (ot + 1) * 128],
                                        in0=ps[1 + ot % 2][:, 0:128],
                                        in1=XF[:, xao + ot * 128: xao + (ot + 1) * 128],
                                        op=ALU.add)
                    m.op(i, sg + f"woe_{p}_{ot}")

    def epi(m):
        if eng == "tensor":
            m.epims(sg + "end_t", sg + "_t")
        if eng == "vector":
            m.epims(sg + "end_v", sg + "_v")
        if eng == "scalar":
            m.epims(sg + "end_s", sg + "_s")

    m.run_stage(sg, n_iter, pro, body, epi)


def stage_b2(m, sg="b2"):
    H = m.H; E = m.E; eng = m.eng
    AA, AB, XF, XN, PB = H["AA"], H["AB"], H["XF"], H["XN"], H["PB"]
    STT = H["STT"]
    ps = H["ps"]; pstat = H["pstat"]
    n_iter = KCFG["b2"]

    def pro(m):
        if eng == "gpsimd":
            m.w("b1end_t")
            E.dma_start(out=AB.ap().rearrange("p (n o) -> p n o", o=1024),
                        in_=H["W2"].rearrange("(n p) o -> p n o", p=128)
                        ).then_inc(m.sem("w_bw2"), 16)
        if eng == "tensor":
            E.wait_ge(m.sem("w_bw1"), 16)
            m.w("b1end_t")
        if eng == "sync":
            m.w("b1end_v")
        if eng == "vector":
            m.w("b1end_v")
        if eng == "scalar":
            m.w("b1end_s")

    def body(m, cc):
        for p in range(2):
            c = cc * 2 + p
            xpo = p * 2048
            x1o = 4096 + p * 2048
            no = p * 2048
            xbo = no
            ro = p * 1024
            if p == 0:
                m.lbase(cc)
            if eng == "sync":
                m.wprev(sg + f"w2e_{p}_7", cc)
                m.wsem("b1xst", (2 * c + 2) * 16)
                E.dma_start(
                    out=XF.ap()[:, xpo:xpo + 2048].rearrange("p (n t) -> p n t", t=256),
                    in_=H["xpb"].rearrange("n p t -> p n t")[:, :, ts(c, 256)]
                ).then_inc(m.sem(sg + "ldp" + str(p)), 16)
                E.dma_start(
                    out=XF.ap()[:, x1o:x1o + 2048].rearrange("p (n t) -> p n t", t=256),
                    in_=H["x1b"].rearrange("n p t -> p n t")[:, :, ts(c, 256)]
                ).then_inc(m.sem(sg + "ldx" + str(p)), 16)
                m.w(sg + f"w2e_{p}_7", cc)
                E.dma_start(
                    out=H["x2b"].rearrange("n p t -> p n t")[:, :, ts(c, 256)],
                    in_=XF.ap()[:, x1o:x1o + 2048].rearrange("p (n t) -> p n t", t=256)
                ).then_inc(m.sem(sg + "xst"), 16)
                m.wsem(sg + "xst", (c + 1) * 16)

            L = m.lad
            mrow = STT[0:1, ro:ro + 256]
            erow = STT[0:1, ro + 256:ro + 512]
            m2row = STT[0:1, ro + 512:ro + 768]
            L("vector", lambda E_: E_.tensor_copy(out=XN[:, xbo:xbo + 2048],
                                                  in_=XF[:, xpo:xpo + 2048]),
              extra_waits=((sg + "ldp" + str(p), (cc + 1) * 16),))
            def _s1(E_):
                for it in range(8):
                    i = E_.matmul(out=pstat[0:1, 0:256], lhsT=H["onesb"][:, 0:1],
                                  rhs=XN[:, xbo + it * 256: xbo + (it + 1) * 256],
                                  start=(it == 0), stop=(it == 7))
                return i
            def _s2(E_):
                for it in range(8):
                    i = E_.matmul(out=pstat[0:1, 256:512], lhsT=H["onesb"][:, 0:1],
                                  rhs=XN[:, xbo + it * 256: xbo + (it + 1) * 256],
                                  start=(it == 0), stop=(it == 7))
                return i
            L("tensor", _s1)
            L("scalar", lambda E_: E_.activation(out=XN[:, xbo:xbo + 2048],
                                                 in_=XN[:, xbo:xbo + 2048], func=AF.Square))
            L("tensor", _s2)
            L("vector", lambda E_: E_.tensor_scalar(out=mrow, in0=pstat[0:1, 0:256],
                                                    scalar1=1.0 / 1024, scalar2=None, op0=ALU.mult))
            L("vector", lambda E_: E_.tensor_scalar(out=erow, in0=pstat[0:1, 256:512],
                                                    scalar1=1.0 / 1024, scalar2=None, op0=ALU.mult))
            L("scalar", lambda E_: E_.activation(out=m2row, in_=mrow, func=AF.Square))
            L("vector", lambda E_: E_.tensor_sub(out=erow, in0=erow, in1=m2row))
            L("scalar", lambda E_: E_.activation(out=erow, in_=erow, func=AF.Sqrt, bias=1e-5))
            L("vector", lambda E_: E_.reciprocal(out=erow, in_=erow))
            L("tensor", lambda E_: E_.matmul(out=ps[0][:, 0:256], lhsT=H["onesf"][:],
                                             rhs=mrow, start=True, stop=True))
            L("tensor", lambda E_: E_.matmul(out=ps[0][:, 256:512], lhsT=H["onesf"][:],
                                             rhs=erow, start=True, stop=True))
            for it in range(8):
                tt = H["PF"][:, (it % 2) * 256: (it % 2) * 256 + 256]
                L("vector", lambda E_, it=it, tt=tt: E_.tensor_sub(
                    out=tt, in0=XF[:, xpo + it * 256: xpo + (it + 1) * 256],
                    in1=ps[0][:, 0:256]))
                L("vector", lambda E_, it=it, tt=tt: E_.tensor_mul(out=tt, in0=tt,
                                                                  in1=ps[0][:, 256:512]))
                L("vector", lambda E_, it=it, tt=tt: E_.tensor_scalar(
                    out=XN[:, no + it * 256: no + (it + 1) * 256],
                    in0=tt, scalar1=H["ln2wc"][:, it:it + 1],
                    scalar2=H["ln2bc"][:, it:it + 1], op0=ALU.mult, op1=ALU.add),
                  name=(f"n{p}" if it == 7 else None))

            if eng == "tensor":
                m.ladwait(f"n{p}", cc)
                for ot in range(32):
                    bank = ps[1 + ot % 2]
                    if ot >= 2:
                        m.w(sg + f"ge_{p}_{ot - 2}", cc)
                    else:
                        wpc(m, sg + "ge_{p}_" + str(30 + ot), p, cc)
                    for it in range(8):
                        i = E.matmul(out=bank[:, 0:256],
                                     lhsT=AA[:, it * 4096 + ot * 128: it * 4096 + ot * 128 + 128],
                                     rhs=XN[:, no + it * 256: no + (it + 1) * 256],
                                     start=(it == 0), stop=(it == 7))
                    m.op(i, sg + f"w1g_{p}_{ot}")
            if eng == "scalar":
                for ot in range(32):
                    bank = ps[1 + ot % 2]
                    m.w(sg + f"w1g_{p}_{ot}", cc)
                    if ot == 0:
                        wpc(m, sg + "w2g_{p}_7", p, cc)
                    i = E.activation(out=PB[:, ot * 256:(ot + 1) * 256], in_=bank[:, 0:256],
                                     func=AF.Gelu, bias=H["b1c"][:, ot:ot + 1])
                    m.op(i, sg + f"ge_{p}_{ot}")
            if eng == "tensor":
                E.wait_ge(m.sem("w_bw2"), 16)
                m.w(sg + f"ge_{p}_31", cc)
                for ot in range(8):
                    bank = ps[3 + ot % 2]
                    if ot >= 2:
                        m.w(sg + f"w2e_{p}_{ot - 2}", cc)
                    else:
                        wpc(m, sg + "e1_{p}_" + str(6 + ot), p, cc)
                    for it in range(32):
                        i = E.matmul(out=bank[:, 0:256],
                                     lhsT=AB[:, it * 1024 + ot * 128: it * 1024 + ot * 128 + 128],
                                     rhs=PB[:, it * 256:(it + 1) * 256],
                                     start=(it == 0), stop=(it == 31))
                    m.op(i, sg + f"w2g_{p}_{ot}")
            if eng == "vector":
                m.wsem(sg + "ldx" + str(p), (cc + 1) * 16)
                for ot in range(8):
                    bank = ps[3 + ot % 2]
                    m.w(sg + f"w2g_{p}_{ot}", cc)
                    x1t = XF[:, x1o + ot * 256: x1o + (ot + 1) * 256]
                    i = E.scalar_tensor_tensor(out=x1t, in0=bank[:, 0:256],
                                               scalar=H["b2c"][:, ot:ot + 1], in1=x1t,
                                               op0=ALU.add, op1=ALU.add)
                    m.op(i, sg + f"e1_{p}_{ot}")
                    m.w(sg + f"e1_{p}_{ot}", cc)
                    i2 = E.tensor_add(out=x1t, in0=x1t,
                                      in1=XF[:, xpo + ot * 256: xpo + (ot + 1) * 256])
                    m.op(i2, sg + f"w2e_{p}_{ot}")

    def epi(m):
        if eng == "tensor":
            m.epims(sg + "end_t", sg + "_t")
        if eng == "vector":
            m.epims(sg + "end_v", sg + "_v")
        if eng == "scalar":
            m.epims(sg + "end_s", sg + "_s")
        if eng == "sync":
            m.epims(sg + "end_st", sg + "_sy")

    m.run_stage(sg, n_iter, pro, body, epi)


def emit_b(m):
    stage_b1(m)
    stage_b2(m)


def build_core(phases=("A",), dbg=False):
    nc = bacc.Bacc("TRN2", target_bir_lowering=False)
    H = {}
    H["xT"] = nc.dram_tensor("xT", [8, 128, 4096], F32, kind="ExternalInput").ap()
    for nm, shp, dt in [
        ("w1T", [1024, 4096], F32), ("w2T", [1024, 4096], F32),
        ("h1T", [4096, 4096], F32), ("h2T", [4096, 4096], F32),
        ("w3T", [4096, 1024], F32),
        ("s1c", [128, 32], F32), ("s2c", [128, 32], F32),
        ("h1sc", [128, 32], F32), ("h2sc", [128, 32], F32), ("s3c", [128, 8], F32),
        ("Wq", [1024, 1024], F32), ("Wk", [1024, 1024], F32),
        ("Wv", [1024, 1024], F32), ("Wo", [1024, 1024], F32),
        ("Wba", [1024, 16], F32), ("W1", [1024, 4096], F32), ("W2", [4096, 1024], F32),
        ("b1c", [128, 32], F32), ("b2c", [128, 8], F32),
        ("ln1wc", [128, 8], F32), ("ln1bc", [128, 8], F32),
        ("ln2wc", [128, 8], F32), ("ln2bc", [128, 8], F32),
        ("IDB", [128, 128], BF16), ("IDF", [128, 128], F32),
        ("TRILS", [128, 128], F32), ("TRIUI", [128, 128], F32),
        ("ONESB", [128, 128], BF16), ("ONESF", [1, 128], F32),
        ("CONSTV", [128, 4], F32),
    ]:
        H[nm] = nc.dram_tensor(nm, shp, dt, kind="ExternalInput").ap()
    sk = "ExternalOutput" if dbg else "Internal"
    H["h1b"] = nc.dram_tensor("h1b", [4096, 4096], BF16, kind="Internal").ap()
    H["h2b"] = nc.dram_tensor("h2b", [4096, 4096], BF16, kind="Internal").ap()
    H["ybuf"] = nc.dram_tensor("ybuf", [32, 128, 4096], BF16, kind="Internal").ap()
    H["gbuf"] = nc.dram_tensor("gbuf", [32, 128, 4096], BF16, kind="Internal").ap()
    H["x1b"] = nc.dram_tensor("x1b", [8, 128, 4096], F32, kind=sk).ap()
    H["xpb"] = nc.dram_tensor("xpb", [8, 128, 4096], F32, kind=sk).ap()
    H["x2b"] = nc.dram_tensor("x2b", [8, 128, 4096], F32, kind=sk).ap()
    H["outT"] = nc.dram_tensor("outT", [8, 128, 4096], F32, kind="ExternalOutput").ap()

    with ExitStack() as ctx:
        sb = lambda name, shp, dt: ctx.enter_context(nc.sbuf_tensor(name, shp, dt))
        H["AA"] = sb("AA", [128, 32768], BF16)
        H["AB"] = sb("AB", [128, 32768], BF16)
        H["XF"] = sb("XF", [128, 8192], F32)
        H["XN"] = sb("XNs", [128, 4096], BF16)
        H["PB"] = sb("PBs", [128, 8192], BF16)
        H["PF"] = sb("PFs", [128, 2560], F32)
        H["STT"] = sb("STT", [128, 2048], F32)
        H["T1"] = sb("T1", [128, 512], BF16)
        for nm, shp, dt in [("idb", [128, 128], BF16), ("idf", [128, 128], F32),
                            ("trils", [128, 128], F32), ("triui", [128, 128], F32),
                            ("onesb", [128, 128], BF16), ("onesf", [1, 128], F32),
                            ("s1c", [128, 32], F32), ("s2c", [128, 32], F32),
                            ("h1sc", [128, 32], F32), ("h2sc", [128, 32], F32),
                            ("s3c", [128, 8], F32), ("s3h", [128, 8], F32),
                            ("b1c", [128, 32], F32), ("b2c", [128, 8], F32),
                            ("ln1wc", [128, 8], F32), ("ln1bc", [128, 8], F32),
                            ("ln2wc", [128, 8], F32), ("ln2bc", [128, 8], F32)]:
            key = nm if nm not in H else nm
            H[key + "_sb"] = sb(key + "_sb", shp, dt)
        for nm in ["idb", "idf", "trils", "triui", "onesb", "onesf",
                   "s1c", "s2c", "h1sc", "h2sc", "s3c", "s3h",
                   "b1c", "b2c", "ln1wc", "ln1bc", "ln2wc", "ln2bc"]:
            H["dram_" + nm] = H.get(nm)
            H[nm] = H[nm + "_sb"]
        H["WBA"] = sb("WBA", [128, 128], BF16)
        H["constv"] = sb("constv", [128, 4], F32)
        nc.const_aps.register(F32, 1e-5, H["constv"][:, 0:1])
        nc.const_aps.register(F32, 1e-6, H["constv"][:, 1:2])
        nc.const_aps.register(F32, 1.0, H["constv"][:, 2:3])
        H["ps"] = [ctx.enter_context(nc.psum_tensor(f"ps{i}", [128, 512], F32))
                   for i in range(6)]
        H["pbf"] = ctx.enter_context(nc.psum_tensor("pbf", [128, 512], F32))
        H["pstat"] = ctx.enter_context(nc.psum_tensor("pstat", [1, 512], F32))

        semnames = ["cl", "w_w1", "w_w2", "w_h1", "w_h2", "w_cw1", "w_cw2", "s3hd"]
        engsfx = ["_sy", "_g", "_t", "_v", "_s",
                  "_syi", "_gi", "_ti", "_vi", "_si"]
        semnames += ["f1" + x for x in ["k", "ldx0", "ldx1", "ysta", "ystb"] + engsfx]
        semnames += ["f2" + x for x in ["ldy0", "ldy1", "lds0", "lds1", "lds2", "lds3",
                                        "gst", "ssd", "sse", "yfr0", "yfr1"] + engsfx]
        semnames += ["f4" + x for x in ["ldg0", "ldg1", "ldx0", "ldx1", "xst", "w3"] + engsfx]
        if "B" in phases:
            semnames += ["w_wq", "w_wk", "w_wv", "w_wo", "w_wba", "w_bw1", "w_bw2"]
            semnames += ["b1" + x for x in ["k", "ldx0", "ldx1", "xst"] + engsfx]
            semnames += ["b2" + x for x in ["k", "ldp0", "ldp1", "ldx0", "ldx1", "xst"] + engsfx]
        sems = {s_: ctx.enter_context(nc.semaphore(s_)) for s_ in semnames}

        shared = {"reg": {}, "stages": {}, "lad": {}}

        def fix_pro_consts(Hl):
            pass

        def emit_all(eng, E, count_mode):
            mm_ = M(nc, H, sems, eng, E, shared, count_mode)
            emit_program_patched(mm_, phases=phases, dbg=dbg)

        for eng in ENGS:
            shared["ladctr"] = {}
            shared["shctr"] = {}
            emit_all(eng, MockEngine(), True)
        block = ctx.enter_context(nc.Block())
        block.sync(lambda E: emit_all("sync", E, False))
        block.gpsimd(lambda E: emit_all("gpsimd", E, False))
        block.tensor(lambda E: emit_all("tensor", E, False))
        block.vector(lambda E: emit_all("vector", E, False))
        block.scalar(lambda E: emit_all("scalar", E, False))
    return nc


def emit_program_patched(m, phases, dbg):
    H = m.H; E = m.E; eng = m.eng
    m.stage_begin("PRO", 0)
    if eng == "sync":
        for nm in ["IDB", "IDF", "TRILS", "TRIUI", "ONESB", "ONESF"]:
            low = {"IDB": "idb", "IDF": "idf", "TRILS": "trils", "TRIUI": "triui",
                   "ONESB": "onesb", "ONESF": "onesf"}[nm]
            E.dma_start(out=H[low][:], in_=H[nm]).then_inc(m.sem("cl"), 16)
        for nm in ["s1c", "s2c", "h1sc", "h2sc", "s3c", "b1c", "b2c",
                   "ln1wc", "ln1bc", "ln2wc", "ln2bc"]:
            E.dma_start(out=H[nm][:], in_=H["dram_" + nm]).then_inc(m.sem("cl"), 16)
        E.dma_start(out=H["constv"][:], in_=H["CONSTV"]).then_inc(m.sem("cl"), 16)
    if eng == "gpsimd":
        E.dma_start(out=H["AA"].ap().rearrange("p (n o) -> p n o", o=4096),
                    in_=H["w1T"].rearrange("(n p) o -> p n o", p=128)
                    ).then_inc(m.sem("w_w1"), 16)
        E.dma_start(out=H["AB"].ap().rearrange("p (n o) -> p n o", o=4096),
                    in_=H["w2T"].rearrange("(n p) o -> p n o", p=128)
                    ).then_inc(m.sem("w_w2"), 16)
        E.dma_start(out=H["h1b"], in_=H["h1T"]).then_inc(m.sem("w_h1"), 16)
        E.dma_start(out=H["h2b"], in_=H["h2T"]).then_inc(m.sem("w_h2"), 16)
    if eng == "vector":
        E.wait_ge(m.sem("cl"), 18 * 16)
        m.ms(E.tensor_scalar(out=H["s3h"][:], in0=H["s3c"][:], scalar1=0.5,
                             scalar2=None, op0=ALU.mult), "s3h", "s3hd")
    m.stage_end()

    if "A" in phases:
        ffn_phase(m, "a", H["xT"], H["x1b"], src_gate=None, first=True)
    if "B" in phases:
        emit_b(m)
    if "C" in phases:
        ffn_phase(m, "c", H["x2b"], H["outT"], src_gate="b2end_v", first=False)


def kernel(**inputs):
    com, shards = host_prepare(inputs)
    nc = build_core(phases=("A", "B", "C"), dbg=False)
    nc.finalize()
    in_maps = [dict(com, xT=shards[b]) for b in range(8)]
    res = run_bass_kernel_spmd(nc, in_maps, core_ids=list(range(8)))
    out = np.stack([np.ascontiguousarray(
        np.asarray(r["outT"], np.float32).reshape(1024, 4096).T)
        for r in res.results])
    return np.ascontiguousarray(out)


# version 08
# speedup vs baseline: 1.6300x; 1.0364x over previous
import numpy as np
import ml_dtypes
import concourse.bass as bass
import concourse.bacc as bacc
import concourse.mybir as mybir
from concourse.bass_utils import run_bass_kernel_spmd
from contextlib import ExitStack

F32 = mybir.dt.float32
BF16 = mybir.dt.bfloat16
AF = mybir.ActivationFunctionType
ALU = mybir.AluOpType
ds = bass.ds
ts = bass.ts

ENGS = ("sync", "gpsimd", "tensor", "vector", "scalar")
KCFG = {"a1": 8, "a23": 4, "a4": 4, "b1": 16, "b2": 8}


class _MockInstr:
    __slots__ = ("_m",)
    def then_inc(self, sem, v):
        self._m._note(sem, v)
        return self

class MockEngine:
    def __init__(self):
        self.counts = {}
        self.n_ops = 0
    def _note(self, sem, v):
        self.counts[sem.name] = self.counts.get(sem.name, 0) + v
    def __getattr__(self, name):
        if name.startswith("_"):
            raise AttributeError(name)
        def fn(*a, **k):
            self.n_ops += 1
            mi = _MockInstr(); mi._m = self
            return mi
        return fn
    def sem_inc(self, sem, v):
        self.n_ops += 1
        self._note(sem, v)
    def wait_ge(self, sem, v):
        self.n_ops += 1


class M:
    def __init__(self, nc, H, sems, eng, E, shared, count_mode):
        self.nc = nc; self.H = H; self.sems = sems
        self.eng = eng; self.E = E
        self.S = shared
        self.count_mode = count_mode
        self.cur_stage = None
        self.cur_tag = "pro"
        self.ladpos = 0

    def stage_begin(self, name, niter, sq=None):
        self.cur_stage = name
        self.sq = sq if sq is not None else name
        if self.count_mode:
            st = self.S["stages"].setdefault(
                name, {"niter": niter, "pre": {}, "per": {}, "engops": {},
                       "base": {}, "shbase": {}, "sq": self.sq})
            st["niter"] = niter
            st["sq"] = self.sq
            self.S.setdefault("sqmap", {})[name] = self.sq
            self._snap0 = dict(self.E.counts)
            b0 = st.setdefault("base0", {})
            for k, v in self._snap0.items():
                b0[k] = max(b0.get(k, 0), v)
        else:
            self.sq = self.S["stages"][name]["sq"]
        self.cur_tag = "pro"

    def loop_begin(self):
        if self.count_mode:
            st = self.S["stages"][self.cur_stage]
            self._snap_pre = dict(self.E.counts)
            self._ops_pre = self.E.n_ops
            for k, v in self._snap_pre.items():
                st["base"][k] = max(st["base"].get(k, 0), v)
            sn = self.mysem()
            st["shbase"][sn + "i"] = max(st["shbase"].get(sn + "i", 0),
                                         self.S.setdefault("shctr", {}).get(sn + "i", 0))
        self.cur_tag = "body"

    def loop_end(self):
        if self.count_mode:
            st = self.S["stages"][self.cur_stage]
            for k, v in list(self.E.counts.items()):
                d = v - self._snap_pre.get(k, 0)
                if d:
                    st["per"][k] = st["per"].get(k, 0) + d
                    self.E.counts[k] = self._snap_pre.get(k, 0) + d * st["niter"]
            st["engops"][self.eng] = self.E.n_ops - self._ops_pre
            sn = self.mysem()
            if st["engops"][self.eng] > 0:
                self.S.setdefault("shctr", {})
                self.S["shctr"][sn + "i"] = (self.S["shctr"].get(sn + "i", 0)
                                             + st["niter"])
                self.E.counts[sn + "i"] = self.S["shctr"][sn + "i"]
        self.cur_tag = "epi"

    def stage_end(self):
        self.cur_stage = None

    def body_ops(self, stage):
        return self.S["stages"][stage]["engops"].get(self.eng, 0)

    ENG_LETTER = {"sync": "sy", "gpsimd": "g", "tensor": "t",
                  "vector": "v", "scalar": "s"}

    def mysem(self):
        return self.sq + "_" + self.ENG_LETTER[self.eng]

    def sembase(self, stage, semname):
        if self.count_mode:
            return 0
        return self.S["stages"][stage]["base"].get(semname, 0)

    def op(self, instr, name=None):
        semname = self.mysem()
        if self.count_mode:
            st = self.S["stages"][self.cur_stage]
            st.setdefault("boosted", set()).add(semname)
        return self.ms(instr, name, semname) if name else self.ms(instr, "_", semname)

    def b(self, semname):
        return 0

    def wprev(self, mname, cc):
        if self.count_mode:
            self.E.n_ops += 1
            return
        semname, stage = self.S["reg"][mname][0], self.S["reg"][mname][1]
        shb = self.S["stages"][stage]["shbase"].get(semname + "i", 0)
        self.E.wait_ge(self.sems[semname + "i"], shb + cc)

    def run_stage(self, name, niter, pro, body, epi=None, sq=None):
        self.stage_begin(name, niter, sq=sq)
        pro(self)
        self.loop_begin()
        if self.count_mode:
            st = self.S["stages"][name]
            st["ladbase"] = max(st.get("ladbase", 0),
                                self.S.setdefault("ladctr", {}).get(st["sq"], 0))
            self.ladpos = 0
            body(self, 0)
            self.E.n_ops += 2
            self.S["lad"][name] = max(self.S["lad"].get(name, 0), self.ladpos)
            if self.ladpos > 0:
                self.S["ladctr"][st["sq"]] = (st["ladbase"]
                                              + niter * self.S["lad"][name])
        else:
            if self.body_ops(name) > 0:
                st = self.S["stages"][name]
                sn = self.mysem()
                pre = st["pre"].get(sn, 0)
                per = st["per"].get(sn, 0)
                with self.E.Fori(0, niter) as cc:
                    self.ladpos = 0
                    body(self, cc)
                    if per > 0:
                        self.E.wait_ge(self.sems[sn], pre + (cc + 1) * per)
                    self.E.sem_inc(self.sems[sn + "i"], 1)
        self.loop_end()
        if epi:
            epi(self)
        self.stage_end()

    def ms(self, instr, mname, semname, v=1):
        semname = self._rs(semname)
        instr.then_inc(self.sems[semname], v)
        if self.count_mode:
            st = self.S["stages"][self.cur_stage]
            cnt = self.E.counts.get(semname, 0)
            if self.cur_tag == "pro":
                off = cnt
            elif self.cur_tag == "body":
                off = cnt - self._snap_pre.get(semname, 0)
            else:
                off = (cnt - st["base"].get(semname, 0)
                       - st["niter"] * st["per"].get(semname, 0))
            self.S["reg"][mname] = (semname, self.cur_stage, self.cur_tag, off)
        return instr

    def msval(self, mname, cc=None, dn=0):
        semname, stage, tag, off = self.S["reg"][mname]
        st = self.S["stages"][stage]
        base = st["base"].get(semname, 0)
        per = st["per"].get(semname, 0)
        if tag == "pro":
            return off + dn
        if tag == "epi":
            return base + st["niter"] * per + off + dn
        return base + cc * per + off + dn

    def w(self, mname, cc=None, dn=0):
        if self.count_mode:
            self.E.n_ops += 1
            return
        semname = self.S["reg"][mname][0]
        self.E.wait_ge(self.sems[semname], self.msval(mname, cc, dn))

    def _rs(self, name):
        sqmap = self.S.get("sqmap", {})
        for stn in sqmap:
            if name.startswith(stn):
                return sqmap[stn] + name[len(stn):]
        return name

    def _basedval(self, name, val):
        if self.count_mode:
            return val
        for stn, sq in self.S.get("sqmap", {}).items():
            if name.startswith(stn) and stn != "PRO":
                rs = sq + name[len(stn):]
                st = self.S["stages"][stn]
                if rs.endswith("i") and rs[:-1] in (
                        sq + "_sy", sq + "_g", sq + "_t", sq + "_v", sq + "_s"):
                    return st["shbase"].get(rs, 0) + val
                return st["base"].get(rs, 0) + val
        return val

    def sem(self, name):
        return self.sems[self._rs(name)]

    def wsem(self, semname, val):
        self.E.wait_ge(self.sems[self._rs(semname)], self._basedval(semname, val))

    def wsem0(self, semname, val):
        if self.count_mode:
            self.E.wait_ge(self.sems[self._rs(semname)], val)
            return
        rs = self._rs(semname)
        base = self.S["stages"][self.cur_stage].get("base0", {}).get(rs, 0)
        self.E.wait_ge(self.sems[rs], base + val)

    def epims(self, mname, semname):
        semname = self._rs(semname)
        st = self.S["stages"][self.cur_stage]
        if not self.count_mode:
            per = st["per"].get(semname, 0)
            prev = st["base"].get(semname, 0) + st["niter"] * per
            self.E.wait_ge(self.sems[semname], prev)
        self.ms(self.E.nop(), mname, semname)

    def credit(self, semname, v):
        self.E.sem_inc(self.sems[self._rs(semname)], v)

    def rel(self, semname, v=1):
        self.E.sem_inc(self.sems[self._rs(semname)], v)

    def lad(self, owner, fn, name=None, extra_waits=(), extra_incs=(), pre=None):
        pos = self.ladpos
        self.ladpos += 1
        if name is not None and self.count_mode:
            self.S["reg"].setdefault("LAD:" + self.cur_stage + ":" + name, pos + 1)
        if self.eng != owner:
            return
        semk = self.sems[self.sq + "k"]
        if self.count_mode:
            self.E.n_ops += 1
            if pre is not None:
                pre(self)
            instr = fn(self.E)
            instr.then_inc(semk, 1)
            for s_, v_ in extra_incs:
                instr.then_inc(self.sems[self._rs(s_)], v_)
            return instr
        self.E.wait_ge(semk, self.lad_base + pos)
        if pre is not None:
            pre(self)
        for s_, v_ in extra_waits:
            self.E.wait_ge(self.sems[self._rs(s_)], self._basedval(s_, v_))
        instr = fn(self.E)
        instr.then_inc(semk, 1)
        for s_, v_ in extra_incs:
            instr.then_inc(self.sems[s_], v_)
        return instr

    def ladwait(self, name, cc, stage=None):
        stage = stage or self.cur_stage
        if self.count_mode:
            self.E.n_ops += 1
            return
        pos1 = self.S["reg"]["LAD:" + stage + ":" + name]
        per = self.S["lad"][stage]
        st = self.S["stages"][stage]
        self.E.wait_ge(self.sems[st["sq"] + "k"],
                       st.get("ladbase", 0) + cc * per + pos1)

    def lbase(self, cc, stage=None):
        if self.count_mode:
            self.lad_base = 0
            return
        st = self.S["stages"][stage or self.cur_stage]
        self.lad_base = (st.get("ladbase", 0)
                         + cc * self.S["lad"][stage or self.cur_stage])


def wpc(m, namefmt, p, cc):
    pm = 1 - p
    nm = namefmt.format(p=pm)
    if p == 1:
        m.w(nm, cc)
    else:
        m.wprev(nm, cc)


def host_prepare(inputs):
    f32 = np.float32
    g = lambda k: np.asarray(inputs[k], f32)
    com = {}
    com["w1T"] = np.ascontiguousarray(g("L0_ffn_w1_w").T)
    com["w2T"] = np.ascontiguousarray(g("L0_ffn_w2_w").T)
    com["h1T"] = np.ascontiguousarray(g("L0_ffn_h1_w").T)
    com["h2T"] = np.ascontiguousarray(g("L0_ffn_h2_w").T)
    com["w3T"] = np.ascontiguousarray(g("L0_ffn_w3_w").T)
    com["s1c"] = np.ascontiguousarray(g("L0_ffn_w1_s").reshape(32, 128).T)
    com["s2c"] = np.ascontiguousarray(g("L0_ffn_w2_s").reshape(32, 128).T)
    com["h1sc"] = np.ascontiguousarray(g("L0_ffn_h1_s").reshape(32, 128).T)
    com["h2sc"] = np.ascontiguousarray(g("L0_ffn_h2_s").reshape(32, 128).T)
    com["s3c"] = np.ascontiguousarray(g("L0_ffn_w3_s").reshape(8, 128).T)
    com["Wq"] = np.ascontiguousarray(g("L0_attn_Wq"))
    com["Wk"] = np.ascontiguousarray(g("L0_attn_Wk"))
    com["Wv"] = np.ascontiguousarray(g("L0_attn_Wv"))
    com["Wo"] = np.ascontiguousarray(g("L0_attn_Wo"))
    com["Wba"] = np.ascontiguousarray(
        np.concatenate([g("L0_attn_Wa"), g("L0_attn_Wb")], axis=1))
    com["W1"] = np.ascontiguousarray(g("L0_attn_W1"))
    com["W2"] = np.ascontiguousarray(g("L0_attn_W2"))
    com["b1c"] = np.ascontiguousarray(g("L0_attn_b1").reshape(32, 128).T)
    com["b2c"] = np.ascontiguousarray(g("L0_attn_b2").reshape(8, 128).T)
    com["ln1wc"] = np.ascontiguousarray(g("L0_attn_ln1_w").reshape(8, 128).T)
    com["ln1bc"] = np.ascontiguousarray(g("L0_attn_ln1_b").reshape(8, 128).T)
    com["ln2wc"] = np.ascontiguousarray(g("L0_attn_ln2_w").reshape(8, 128).T)
    com["ln2bc"] = np.ascontiguousarray(g("L0_attn_ln2_b").reshape(8, 128).T)
    com["IDB"] = np.eye(128, dtype=ml_dtypes.bfloat16)
    com["IDF"] = np.eye(128, dtype=f32)
    com["TRILS"] = np.tril(np.ones((128, 128), f32), -1)
    com["TRIUI"] = np.triu(np.ones((128, 128), f32), 0)
    com["ONESB"] = np.ones((128, 128), ml_dtypes.bfloat16)
    com["ONESF"] = np.ones((1, 128), f32)
    com["CONSTV"] = np.broadcast_to(
        np.array([1e-5, 1e-6, 1.0, 0.0], f32), (128, 4)).copy()
    x = np.asarray(inputs["L0_attn_x"], f32)
    shards = [np.ascontiguousarray(x[b].T).reshape(8, 128, 4096) for b in range(8)]
    return com, shards


def _inner(m, n, fn):
    if m.count_mode:
        for i in range(n):
            fn(i)
    else:
        with m.E.Fori(0, n) as i:
            fn(i)
M.inner = _inner


def _dead_emit_program(m, phases=("A",), dbg=False):
    H = m.H; E = m.E; eng = m.eng

    m.stage_begin("PRO", 0)
    if eng == "sync":
        pairs = [("IDB", "idb"), ("IDF", "idf"), ("TRILS", "trils"), ("TRIUI", "triui"),
                 ("ONESB", "onesb"), ("ONESF", "onesf"), ("s1c", "s1c"), ("s2c", "s2c"),
                 ("h1sc", "h1sc"), ("h2sc", "h2sc"), ("s3c", "s3c"), ("b1c", "b1c"),
                 ("b2c", "b2c"), ("ln1wc", "ln1wc"), ("ln1bc", "ln1bc"),
                 ("ln2wc", "ln2wc"), ("ln2bc", "ln2bc")]
        for src, dst in pairs:
            E.dma_start(out=H[dst][:], in_=H[src]).then_inc(m.sem("cl"), 16)
    if eng == "gpsimd":
        E.dma_start(out=H["AA"].ap().rearrange("p (n o) -> p n o", o=4096),
                    in_=H["w1T"].rearrange("(n p) o -> p n o", p=128)
                    ).then_inc(m.sem("w_w1"), 16)
        E.dma_start(out=H["AB"].ap().rearrange("p (n o) -> p n o", o=4096),
                    in_=H["w2T"].rearrange("(n p) o -> p n o", p=128)
                    ).then_inc(m.sem("w_w2"), 16)
        E.dma_start(out=H["h1b"], in_=H["h1T"]).then_inc(m.sem("w_h1"), 16)
        E.dma_start(out=H["h2b"], in_=H["h2T"]).then_inc(m.sem("w_h2"), 16)
    if eng == "vector":
        E.wait_ge(m.sem("cl"), 18 * 16)
        m.ms(E.tensor_scalar(out=H["s3h"][:], in0=H["s3c"][:], scalar1=0.5,
                             scalar2=None, op0=ALU.mult), "s3h", "s3hd")
    m.stage_end()

    if "A" in phases:
        ffn_phase(m, "a", H["xT"], H["x1b"],
                  src_gate=None, first=True)
    if "B" in phases:
        emit_b(m)
    if "C" in phases:
        ffn_phase(m, "c", H["x2b"], H["outT"], src_gate="b2end_v", first=False,
                  srcx=lambda mm, cc, p: mm.wsem("b2xst",
                                                 mm.sembase("b2", "b2xst")
                                                 + (2 * cc + p + 1) * 16))


def ffn_phase(m, ph, xsrc, xdst, src_gate, first, srcx=None):
    H = m.H
    stage_a1(m, ph + "1", xsrc, first, src_gate, srcx=srcx)
    stage_a23(m, ph + "2", H["ybuf"], H["gbuf"], H["h1b"], "h1sc",
              prev_end=ph + "1end_t", whsem="w_h1",
              srcw=lambda mm, cc, p: (mm.wsem(ph + "1ysta", (4 * cc + 2 * p + 2) * 16),
                                      mm.wsem(ph + "1ystb", (4 * cc + 2 * p + 2) * 16)),
              prev_pb=(ph + "1ystb", KCFG["a1"] * 2 * 16))
    stage_a23(m, ph + "3", H["gbuf"], H["ybuf"], H["h2b"], "h2sc",
              prev_end=ph + "2end_t", whsem="w_h2",
              srcw=lambda mm, cc, p: (mm.wsem(ph + "2gsta", (4 * cc + 2 * p + 2) * 16),
                                      mm.wsem(ph + "2gstb", (4 * cc + 2 * p + 2) * 16)),
              prev_gst=ph + "2gst")
    stage_a4(m, ph + "4", H["ybuf"], xsrc, xdst,
             prev_end=ph + "3end_t", a1_v_end=ph + "1end_v",
             srcw=lambda mm, cc, p: (mm.wsem(ph + "3gsta", (4 * cc + 2 * p + 2) * 16),
                                     mm.wsem(ph + "3gstb", (4 * cc + 2 * p + 2) * 16)),
             prev_gst=ph + "3gst",
             srcx=(None if srcx is None else
                   (lambda mm, cc, p: mm.wsem(
                       "b2xst", mm.sembase("b2", "b2xst") + (4 * cc + 2 * p + 2) * 16))))


def stage_a1(m, sg, xsrc, first, src_gate, srcx=None):
    H = m.H; E = m.E; eng = m.eng
    AA, AB, XF, XN, PB = H["AA"], H["AB"], H["XF"], H["XN"], H["PB"]
    STT, T1 = H["STT"], H["T1"]
    ps = H["ps"]; pstat = H["pstat"]
    n_iter = KCFG["a1"]

    def pro(m):
        if eng == "sync":
            if src_gate is not None:
                m.w(src_gate)

        if eng in ("vector", "scalar", "tensor"):
            E.wait_ge(m.sem("cl"), 18 * 16)
        if eng == "tensor":
            if first:
                E.wait_ge(m.sem("w_w1"), 16)
                E.wait_ge(m.sem("w_w2"), 16)
            else:
                E.wait_ge(m.sem("w_cw1"), 16)
                E.wait_ge(m.sem("w_cw2"), 16)
        if eng == "gpsimd" and not first:
            m.w("b2end_t")
            m.w("b2end_s")
            m.w("b2end_v")
            for s_ in ["b2ldp0", "b2ldp1", "b2ldx0", "b2ldx1"]:
                m.wsem(s_, KCFG["b2"] * 16)
            E.dma_start(out=AA.ap().rearrange("p (n o) -> p n o", o=4096),
                        in_=H["w1T"].rearrange("(n p) o -> p n o", p=128)
                        ).then_inc(m.sem("w_cw1"), 16)
            E.dma_start(out=AB.ap().rearrange("p (n o) -> p n o", o=4096),
                        in_=H["w2T"].rearrange("(n p) o -> p n o", p=128)
                        ).then_inc(m.sem("w_cw2"), 16)

    def body(m, cc):
        for p in range(2):
            cg = cc * 2 + p
            xoff = p * 2048
            noff = p * 2048
            ro = p * 1024
            if p == 0:
                m.lbase(cc)
            if eng == "gpsimd":
                m.wprev(sg + f"g_{p}_31", cc)
                if srcx is not None:
                    srcx(m, cc, p)
                E.dma_start(
                    out=XF.ap()[:, xoff:xoff + 2048].rearrange("p (n t) -> p n t", t=256),
                    in_=xsrc.rearrange("n p t -> p n t")[:, :, ts(cg, 256)]
                ).then_inc(m.sem(sg + "ldx" + str(p)), 16)
            m.lad("vector", lambda E_: E_.tensor_copy(
                out=XN[:, noff:noff + 2048], in_=XF[:, xoff:xoff + 2048]),
                extra_waits=((sg + "ldx" + str(p), (cc + 1) * 16),))
            m.lad("scalar", lambda E_: E_.activation(
                out=PB[:, 0:2048], in_=XN[:, noff:noff + 2048], func=AF.Square),
                extra_waits=((sg + "ysta", cg * 16),))
            def _s1(E_):
                for it in range(8):
                    i = E_.matmul(out=pstat[0:1, 0:256], lhsT=H["onesb"][:, 0:1],
                                  rhs=XN[:, noff + it * 256: noff + (it + 1) * 256],
                                  start=(it == 0), stop=(it == 7))
                return i
            m.lad("tensor", _s1)
            def _s2(E_):
                for it in range(8):
                    i = E_.matmul(out=pstat[0:1, 256:512], lhsT=H["onesb"][:, 0:1],
                                  rhs=PB[:, it * 256:(it + 1) * 256],
                                  start=(it == 0), stop=(it == 7))
                return i
            m.lad("tensor", _s2)
            mrow = STT[0:1, ro:ro + 256]
            erow = STT[0:1, ro + 256:ro + 512]
            m2row = STT[0:1, ro + 512:ro + 768]
            m.lad("vector", lambda E_: E_.tensor_scalar(
                out=mrow, in0=pstat[0:1, 0:256], scalar1=1.0 / 1024,
                scalar2=None, op0=ALU.mult))
            m.lad("vector", lambda E_: E_.tensor_scalar(
                out=erow, in0=pstat[0:1, 256:512], scalar1=1.0 / 1024,
                scalar2=None, op0=ALU.mult))
            m.lad("scalar", lambda E_: E_.activation(out=m2row, in_=mrow, func=AF.Square))
            m.lad("vector", lambda E_: E_.tensor_sub(out=erow, in0=erow, in1=m2row))
            m.lad("scalar", lambda E_: E_.activation(out=erow, in_=erow,
                                                     func=AF.Sqrt, bias=1e-5))
            m.lad("vector", lambda E_: E_.reciprocal(out=erow, in_=erow))
            m.lad("tensor", lambda E_: E_.matmul(
                out=ps[0][:, 0:256], lhsT=H["onesf"][:], rhs=mrow,
                start=True, stop=True))
            m.lad("tensor", lambda E_: E_.matmul(
                out=ps[0][:, 256:512], lhsT=H["onesf"][:], rhs=erow,
                start=True, stop=True))
            for it in range(8):
                tt = XF[:, 4096 + it * 256: 4096 + (it + 1) * 256]
                m.lad("vector", lambda E_, it=it, tt=tt: E_.tensor_sub(
                    out=tt, in0=XF[:, xoff + it * 256: xoff + (it + 1) * 256],
                    in1=ps[0][:, 0:256]))
                m.lad("vector", lambda E_, it=it, tt=tt: E_.tensor_mul(
                    out=XN[:, noff + it * 256: noff + (it + 1) * 256],
                    in0=tt, in1=ps[0][:, 256:512]),
                    name=(f"xr{p}" if it == 7 else None))
            if eng == "tensor":
                m.ladwait(f"xr{p}", cc)
                for ot in range(32):
                    a = ot % 2
                    if ot >= 2:
                        m.w(sg + f"t1_{p}_{ot - 2}", cc)
                        m.w(sg + f"y_{p}_{ot - 2}", cc)
                    else:
                        pm = 1 - p
                        if p == 1:
                            m.w(sg + f"t1_{pm}_{ot + 30}", cc)
                            m.w(sg + f"y_{pm}_{ot + 30}", cc)
                        else:
                            m.wprev(sg + f"t1_{pm}_{ot + 30}", cc)
                            m.wprev(sg + f"y_{pm}_{ot + 30}", cc)
                    for it in range(8):
                        E.matmul(out=ps[1 + 2 * a][:, 0:256],
                                 lhsT=AA[:, it * 4096 + ot * 128: it * 4096 + ot * 128 + 128],
                                 rhs=XN[:, noff + it * 256: noff + (it + 1) * 256],
                                 start=(it == 0), stop=(it == 7))
                    for it in range(8):
                        i = E.matmul(out=ps[2 + 2 * a][:, 0:256],
                                     lhsT=AB[:, it * 4096 + ot * 128: it * 4096 + ot * 128 + 128],
                                     rhs=XN[:, noff + it * 256: noff + (it + 1) * 256],
                                     start=(it == 0), stop=(it == 7))
                    m.op(i, sg + f"g_{p}_{ot}")
            if eng == "scalar":
                for ot in range(32):
                    a = ot % 2
                    m.w(sg + f"g_{p}_{ot}", cc)
                    if ot >= 2:
                        m.w(sg + f"y_{p}_{ot - 2}", cc)
                    else:
                        pm = 1 - p
                        if p == 1:
                            m.w(sg + f"y_{pm}_{ot + 30}", cc)
                        else:
                            m.wprev(sg + f"y_{pm}_{ot + 30}", cc)
                    i = E.activation(out=T1[:, (ot % 2) * 256:(ot % 2) * 256 + 256],
                                     in_=ps[1 + 2 * a][:, 0:256], func=AF.Silu,
                                     scale=H["s1c"][:, ot:ot + 1])
                    m.op(i, sg + f"t1_{p}_{ot}")
            if eng == "vector":
                for ot in range(32):
                    a = ot % 2
                    if ot == 0:
                        m.wsem(sg + "ysta", cg * 16)
                    elif ot == 16:
                        m.wsem(sg + "ystb", cg * 16)
                    m.w(sg + f"t1_{p}_{ot}", cc)
                    i = E.scalar_tensor_tensor(
                        out=PB[:, ot * 256:(ot + 1) * 256],
                        in0=ps[2 + 2 * a][:, 0:256],
                        scalar=H["s2c"][:, ot:ot + 1],
                        in1=T1[:, (ot % 2) * 256:(ot % 2) * 256 + 256],
                        op0=ALU.mult, op1=ALU.mult)
                    m.op(i, sg + f"y_{p}_{ot}")
            if eng == "sync":
                for hh in range(2):
                    m.w(sg + f"y_{p}_{15 if hh == 0 else 31}", cc)
                    E.dma_start(
                        out=H["ybuf"].rearrange("n p t -> p n t")[:, ts(hh, 16), :][:, :, ts(cg, 256)],
                        in_=PB.ap()[:, hh * 4096:(hh + 1) * 4096].rearrange(
                            "p (n t) -> p n t", t=256)
                    ).then_inc(m.sem(sg + "yst" + "ab"[hh]), 16)

    def epi(m):
        if eng == "tensor":
            m.epims(sg + "end_t", sg + "_t")
        if eng == "vector":
            m.epims(sg + "end_v", sg + "_v")

    m.run_stage(sg, n_iter, pro, body, epi, sq="f1")


def stage_a23(m, sg, src, dst, wsrc, scname, prev_end, whsem, srcw, prev_gst=None, prev_pb=None):
    H = m.H; E = m.E; eng = m.eng
    AA, AB, PB = H["AA"], H["AB"], H["PB"]
    ps = H["ps"]
    n_iter = KCFG["a23"]

    def pro(m):
        if eng == "sync":
            m.w(prev_end)
        if eng == "gpsimd":
            m.w(prev_end)
            E.wait_ge(m.sem(whsem), 16)
        if eng == "scalar":
            m.w(prev_end)
            if prev_gst is not None:
                m.wsem(prev_gst + "a", KCFG["a23"] * 2 * 2 * 16)
                m.wsem(prev_gst + "b", KCFG["a23"] * 2 * 2 * 16)
            if prev_pb is not None:
                m.wsem(prev_pb[0], prev_pb[1])
                m.wsem(prev_pb[0][:-1] + "a", prev_pb[1])
        if eng == "tensor":
            m.credit(sg + "ssd", 2)
            m.credit(sg + "yfr0", 1)
            m.credit(sg + "yfr1", 1)

    def body(m, cc):
        for p in range(2):
            c = cc * 2 + p
            if eng == "sync":
                m.wsem(sg + "yfr" + str(p), cc)
                srcw(m, cc, p)
                E.dma_start(
                    out=AA.ap()[:, p * 16384:(p + 1) * 16384].rearrange("p (n t) -> p n t", t=512),
                    in_=src.rearrange("n p t -> p n t")[:, :, ts(c, 512)]
                ).then_inc(m.sem(sg + "ldy" + str(p)), 16)
            if eng == "gpsimd":
                def gslab(sccc):
                    for sp2 in range(2):
                        P = c * 8 + sccc * 2 + sp2
                        for sp in range(2):
                            s = sccc * 4 + sp2 * 2 + sp
                            slot = sp2 * 2 + sp
                            m.wsem(sg + "ssd", P - 1)
                            E.dma_start(
                                out=AB.ap()[:, slot * 8192:(slot + 1) * 8192].rearrange(
                                    "p (n o) -> p n o", o=256),
                                in_=wsrc.rearrange("(n p) o -> p n o", p=128)[:, :, ts(s, 256)]
                            ).then_inc(m.sem(sg + "lds" + str(slot)), 16)
                m.inner(4, gslab)
            if eng == "tensor":
                m.wsem(sg + "ldy" + str(p), (cc + 1) * 16)
                def tslab(sccc):
                    for sp2 in range(2):
                        P = c * 8 + sccc * 2 + sp2
                        for sp in range(2):
                            s = sccc * 4 + sp2 * 2 + sp
                            slot = sp2 * 2 + sp
                            m.wsem(sg + "lds" + str(slot), (c * 4 + sccc + 1) * 16)
                            for ot2 in range(2):
                                m.wsem(sg + "sse", P)
                                for it in range(32):
                                    i = E.matmul(
                                        out=ps[sp * 2 + ot2][:, 0:512],
                                        lhsT=AB[:, slot * 8192 + it * 256 + ot2 * 128:
                                                slot * 8192 + it * 256 + ot2 * 128 + 128],
                                        rhs=AA[:, p * 16384 + it * 512:
                                               p * 16384 + (it + 1) * 512],
                                        start=(it == 0), stop=(it == 31))
                                m.op(i)
                        m.wsem(sg + "_t", c * 32 + (sccc * 2 + sp2 + 1) * 4)
                        m.credit(sg + "ssd", 1)
                m.inner(4, tslab)
                m.credit(sg + "yfr" + str(p), 1)
            if eng == "scalar":
                def sslab(sc2):
                    for sl in range(2):
                        sccc = sc2 * 2 + sl
                        for sp2 in range(2):
                            for sp in range(2):
                                s = sccc * 4 + sp2 * 2 + sp
                                for ot2 in range(2):
                                    os = s * 2 + ot2
                                    m.wsem(sg + "_t", c * 32 + s * 2 + ot2 + 1)
                                    m.wsem(sg + "gst" + "ab"[sl],
                                           (c * 2 + sc2) * 16)
                                    i = E.activation(
                                        out=PB[:, ts(os % 16, 512)],
                                        in_=ps[sp * 2 + ot2][:, 0:512], func=AF.Gelu,
                                        scale=H[scname][:, ds(os, 1)])
                                    m.op(i)
                            m.wsem(sg + "_s", c * 32 + (sccc * 2 + sp2 + 1) * 4)
                            m.credit(sg + "sse", 1)
                        E.dma_start(
                            out=dst.rearrange("n p t -> p n t")[:, ts(sccc, 8), :][:, :, ts(c, 512)],
                            in_=PB.ap().rearrange("p (n t) -> p n t", t=512)[:, ts(sl, 8), :]
                        ).then_inc(m.sem(sg + "gst" + "ab"[sl]), 16)
                m.inner(2, sslab)

    def epi(m):
        if eng == "tensor":
            m.epims(sg + "end_t", sg + "_t")

    m.run_stage(sg, n_iter, pro, body, epi, sq="f2")


def stage_a4(m, sg, gsrc, xsrc, xdst, prev_end, a1_v_end, srcw, prev_gst=None, srcx=None):
    H = m.H; E = m.E; eng = m.eng
    AA, AB, XF = H["AA"], H["AB"], H["XF"]
    ps = H["ps"]
    n_iter = KCFG["a4"]

    def pro(m):
        if eng == "gpsimd":
            m.w(prev_end)
            if prev_gst is not None:
                m.wsem(prev_gst + "a", KCFG["a23"] * 2 * 2 * 16)
                m.wsem(prev_gst + "b", KCFG["a23"] * 2 * 2 * 16)
            E.dma_start(out=AA.ap().rearrange("p (n o) -> p n o", o=1024),
                        in_=H["w3T"].rearrange("(n p) o -> p n o", p=128)
                        ).then_inc(m.sem(sg + "w3"), 16)
        if eng == "sync":
            m.w(prev_end)
            m.w(a1_v_end)
            if prev_gst is not None:
                m.wsem(prev_gst + "a", KCFG["a23"] * 2 * 2 * 16)
                m.wsem(prev_gst + "b", KCFG["a23"] * 2 * 2 * 16)
        if eng == "tensor":
            m.wsem0(sg + "w3", 16)
        if eng == "vector":
            m.w("s3h")

    def body(m, cc):
        for p in range(2):
            c = cc * 2 + p
            xoff = p * 4096
            if eng == "gpsimd":
                m.wsem(sg + "_ti", cc)
                srcw(m, cc, p)
                if srcx is not None:
                    srcx(m, cc, p)
                E.dma_start(
                    out=AB.ap()[:, p * 16384:(p + 1) * 16384].rearrange("p (n t) -> p n t", t=512),
                    in_=gsrc.rearrange("n p t -> p n t")[:, :, ts(c, 512)]
                ).then_inc(m.sem(sg + "ldg" + str(p)), 16)
                E.dma_start(
                    out=XF.ap()[:, xoff:xoff + 4096].rearrange("p (n t) -> p n t", t=512),
                    in_=xsrc.rearrange("n p t -> p n t")[:, :, ts(c, 512)]
                ).then_inc(m.sem(sg + "ldx" + str(p)), 16)
            if eng == "sync":
                m.w(sg + f"x1_{p}", cc)
                E.dma_start(
                    out=xdst.rearrange("n p t -> p n t")[:, :, ts(c, 512)],
                    in_=XF.ap()[:, xoff:xoff + 4096].rearrange("p (n t) -> p n t", t=512)
                ).then_inc(m.sem(sg + "xst"), 16)
                m.wsem(sg + "xst", (c + 1) * 16)
            if eng == "tensor":
                m.wsem(sg + "ldg" + str(p), (cc + 1) * 16)
                for ot in range(8):
                    if ot >= 2:
                        m.w(sg + f"x1e_{p}_{ot - 2}", cc)
                    else:
                        pm = 1 - p
                        if p == 1:
                            m.w(sg + f"x1e_{pm}_{ot + 6}", cc)
                        else:
                            m.wprev(sg + f"x1e_{pm}_{ot + 6}", cc)
                    for it in range(32):
                        i = E.matmul(
                            out=ps[ot % 2][:, 0:512],
                            lhsT=AA[:, it * 1024 + ot * 128: it * 1024 + ot * 128 + 128],
                            rhs=AB[:, p * 16384 + it * 512: p * 16384 + (it + 1) * 512],
                            start=(it == 0), stop=(it == 31))
                    m.op(i, sg + f"g_{p}_{ot}")
            if eng == "vector":
                m.wsem(sg + "ldx" + str(p), (cc + 1) * 16)
                for ot in range(8):
                    m.w(sg + f"g_{p}_{ot}", cc)
                    xt = XF[:, xoff + ot * 512: xoff + (ot + 1) * 512]
                    i = E.scalar_tensor_tensor(
                        out=xt, in0=ps[ot % 2][:, 0:512],
                        scalar=H["s3h"][:, ot:ot + 1], in1=xt,
                        op0=ALU.mult, op1=ALU.add)
                    m.op(i, sg + f"x1e_{p}_{ot}")
                    if ot == 7:
                        if m.count_mode:
                            m.S["reg"][sg + f"x1_{p}"] = m.S["reg"][sg + f"x1e_{p}_{ot}"]

    def epi(m):
        if eng == "sync":
            m.epims(sg + "end_st", sg + "_sy")
        if eng == "vector":
            m.epims(sg + "end_v", sg + "_v")
        if eng == "tensor":
            m.epims(sg + "end_t", sg + "_t")

    m.run_stage(sg, n_iter, pro, body, epi, sq="f4")


def stage_b1(m, sg="b1"):
    H = m.H; E = m.E; eng = m.eng
    AA, AB, XF, XN, PB, PF = H["AA"], H["AB"], H["XF"], H["XN"], H["PB"], H["PF"]
    STT = H["STT"]
    ps = H["ps"]; pstat = H["pstat"]; pbf = H["pbf"]
    n_iter = KCFG["b1"]
    QO, KO, VO, OFM = 0, 1024, 2048, 3072
    PT = 4096
    XC, YO, SB, QKT, KH = 4992, 5120, 5248, 5376, 5504
    PT_B, XC_B, YO_B, SB_B, QKT_B, KH_B = 5632, 6528, 6656, 6784, 6912, 7040
    EO, ETO, FBO, FBTO, U1O = 0, 128, 256, 384, 512
    TRLG, TRSG, GCO, GCT = 640, 656, 672, 680
    SOFF = 1024
    RAW, T16A, T16B, Z16 = 2048, 2176, 2304, 2432
    EO_B, ETO_B, FBO_B = 2048, 2176, 2304
    FBTO_B, U1O_B, KF_A, KF_B = 4096, 4224, 4352, 4480
    GCT_B = 681
    VF = 7168

    def pro(m):
        if eng == "gpsimd":
            m.w("a4end_t")
            m.wsem("a4xst", KCFG["a4"] * 2 * 16)
            for pp in range(2):
                E.dma_start(
                    out=XF.ap()[:, pp * 1024:pp * 1024 + 1024].rearrange(
                        "p (n t) -> p n t", t=128),
                    in_=H["x1b"].rearrange("n p t -> p n t")[:, :, ts(pp, 128)]
                ).then_inc(m.sem(sg + "ldx" + str(pp)), 16)
            for nm, off in [("Wq", 0), ("Wk", 8192), ("Wv", 16384), ("Wo", 24576)]:
                E.dma_start(out=AB.ap()[:, off:off + 8192].rearrange("p (n o) -> p n o", o=1024),
                            in_=H[nm].rearrange("(n p) o -> p n o", p=128)
                            ).then_inc(m.sem("w_" + nm.lower()), 16)
            E.dma_start(out=H["WBA"].ap().rearrange("p (n o) -> p n o", o=16),
                        in_=H["Wba"].rearrange("(n p) o -> p n o", p=128)
                        ).then_inc(m.sem("w_wba"), 16)
            E.dma_start(out=AA.ap().rearrange("p (n o) -> p n o", o=4096),
                        in_=H["W1"].rearrange("(n p) o -> p n o", p=128)
                        ).then_inc(m.sem("w_bw1"), 16)
        if eng == "tensor":
            for s_ in ["w_wq", "w_wk", "w_wv", "w_wo", "w_wba"]:
                E.wait_ge(m.sem(s_), 16)
            m.w("a4end_t")
        if eng == "vector":
            m.w("a4end_v")
            m.op(E.memset(PF[:, SOFF:SOFF + 1024], 0.0))
            m.op(E.memset(PF[0:16, Z16:Z16 + 128], 0.0))
        if eng == "sync":
            m.w("a4end_st")
        if eng == "scalar":
            m.w("a4end_t")

    def body(m, cc):
        for p in range(2):
            cg = cc * 2 + p
            x1o = p * 1024
            xao = 2048 + p * 1024
            xpo = 5120 + p * 1024
            no = p * 1024
            xbo = 2048 + p * 1024
            ro = p * 1024
            if p == 0:
                m.lbase(cc)
            if eng == "gpsimd":
                m.w(sg + f"pj_{p}", cc)
                E.dma_start(
                    out=XF.ap()[:, x1o:x1o + 1024].rearrange("p (n t) -> p n t", t=128),
                    in_=H["x1b"].rearrange("n p t -> p n t")[:, :, ts((cg + 2) % 32, 128)]
                ).then_inc(m.sem(sg + "ldx" + str(p)), 16)
            if eng == "sync":
                m.w(sg + f"woe_{p}_7", cc)
                E.dma_start(
                    out=H["xpb"].rearrange("n p t -> p n t")[:, :, ts(cg, 128)],
                    in_=XF.ap()[:, xpo:xpo + 1024].rearrange("p (n t) -> p n t", t=128)
                ).then_inc(m.sem(sg + "xst"), 16)
                m.wsem(sg + "xst", (cg + 1) * 16)
                m.op(E.nop(), sg + f"sst_{p}")

            L = m.lad
            mrow = STT[0:1, ro:ro + 128]
            erow = STT[0:1, ro + 128:ro + 256]
            m2row = STT[0:1, ro + 256:ro + 384]
            L("vector", lambda E_: E_.tensor_copy(out=XN[:, xbo:xbo + 1024],
                                                  in_=XF[:, x1o:x1o + 1024]),
              extra_waits=((sg + "ldx" + str(p), cc * 16),), )
            def _s1(E_, src_off=xbo):
                for it in range(8):
                    i = E_.matmul(out=pstat[0:1, 0:128], lhsT=H["onesb"][:, 0:1],
                                  rhs=XN[:, src_off + it * 128: src_off + (it + 1) * 128],
                                  start=(it == 0), stop=(it == 7))
                return i
            L("tensor", _s1)
            L("scalar", lambda E_: E_.activation(out=XN[:, xbo:xbo + 1024],
                                                 in_=XN[:, xbo:xbo + 1024], func=AF.Square))
            def _s2(E_, src_off=xbo):
                for it in range(8):
                    i = E_.matmul(out=pstat[0:1, 128:256], lhsT=H["onesb"][:, 0:1],
                                  rhs=XN[:, src_off + it * 128: src_off + (it + 1) * 128],
                                  start=(it == 0), stop=(it == 7))
                return i
            L("tensor", _s2)
            L("vector", lambda E_: E_.tensor_scalar(out=mrow, in0=pstat[0:1, 0:128],
                                                    scalar1=1.0 / 1024, scalar2=None, op0=ALU.mult))
            L("vector", lambda E_: E_.tensor_scalar(out=erow, in0=pstat[0:1, 128:256],
                                                    scalar1=1.0 / 1024, scalar2=None, op0=ALU.mult))
            L("scalar", lambda E_: E_.activation(out=m2row, in_=mrow, func=AF.Square))
            L("vector", lambda E_: E_.tensor_sub(out=erow, in0=erow, in1=m2row))
            L("scalar", lambda E_: E_.activation(out=erow, in_=erow, func=AF.Sqrt, bias=1e-5))
            L("vector", lambda E_: E_.reciprocal(out=erow, in_=erow))
            L("tensor", lambda E_: E_.matmul(out=ps[0][:, 0:128], lhsT=H["onesf"][:],
                                             rhs=mrow, start=True, stop=True))
            L("tensor", lambda E_: E_.matmul(out=ps[0][:, 128:256], lhsT=H["onesf"][:],
                                             rhs=erow, start=True, stop=True))
            for it in range(8):
                tt = XF[:, 4096 + it * 128: 4096 + (it + 1) * 128]
                pre0 = None
                if it == 0:
                    pre0 = (lambda mm_: mm_.wprev(sg + f"woe_{p}_7", cc))
                L("vector", lambda E_, it=it, tt=tt: E_.tensor_sub(
                    out=tt, in0=XF[:, x1o + it * 128: x1o + (it + 1) * 128],
                    in1=ps[0][:, 0:128]), pre=pre0)
                L("vector", lambda E_, it=it, tt=tt: E_.tensor_mul(
                    out=XF[:, xao + it * 128: xao + (it + 1) * 128],
                    in0=tt, in1=ps[0][:, 128:256]),
                  name=(f"xa{p}" if it == 7 else None))
            m4row = STT[0:1, ro + 384:ro + 512]
            e4row = STT[0:1, ro + 512:ro + 640]
            m24row = STT[0:1, ro + 640:ro + 768]
            L("vector", lambda E_: E_.tensor_copy(out=XN[:, xbo:xbo + 1024],
                                                  in_=XF[:, xao:xao + 1024]))
            L("tensor", _s1)
            L("scalar", lambda E_: E_.activation(out=XN[:, xbo:xbo + 1024],
                                                 in_=XN[:, xbo:xbo + 1024], func=AF.Square))
            L("tensor", _s2)
            L("vector", lambda E_: E_.tensor_scalar(out=m4row, in0=pstat[0:1, 0:128],
                                                    scalar1=1.0 / 1024, scalar2=None, op0=ALU.mult))
            L("vector", lambda E_: E_.tensor_scalar(out=e4row, in0=pstat[0:1, 128:256],
                                                    scalar1=1.0 / 1024, scalar2=None, op0=ALU.mult))
            L("scalar", lambda E_: E_.activation(out=m24row, in_=m4row, func=AF.Square))
            L("vector", lambda E_: E_.tensor_sub(out=e4row, in0=e4row, in1=m24row))
            L("scalar", lambda E_: E_.activation(out=e4row, in_=e4row, func=AF.Sqrt, bias=1e-5))
            L("vector", lambda E_: E_.reciprocal(out=e4row, in_=e4row))
            L("tensor", lambda E_: E_.matmul(out=ps[0][:, 256:384], lhsT=H["onesf"][:],
                                             rhs=m4row, start=True, stop=True))
            L("tensor", lambda E_: E_.matmul(out=ps[0][:, 384:512], lhsT=H["onesf"][:],
                                             rhs=e4row, start=True, stop=True))
            for it in range(8):
                tt = XF[:, 4096 + it * 128: 4096 + (it + 1) * 128]
                L("vector", lambda E_, it=it, tt=tt: E_.tensor_sub(
                    out=tt, in0=XF[:, xao + it * 128: xao + (it + 1) * 128],
                    in1=ps[0][:, 256:384]))
                L("vector", lambda E_, it=it, tt=tt: E_.tensor_mul(
                    out=tt, in0=tt, in1=ps[0][:, 384:512]))
                L("vector", lambda E_, it=it, tt=tt: E_.tensor_scalar(
                    out=XN[:, no + it * 128: no + (it + 1) * 128],
                    in0=tt, scalar1=H["ln1wc"][:, it:it + 1],
                    scalar2=H["ln1bc"][:, it:it + 1], op0=ALU.mult, op1=ALU.add),
                  name=(f"n{p}" if it == 7 else None))
            PRJORD = [0, 1, 8, 9, 16, 17, 2, 3, 10, 11, 18, 19,
                      4, 5, 12, 13, 20, 21, 6, 7, 14, 15, 22, 23]

            def proj_steps(idx):
                g = PRJORD[idx]
                wi, ot = g // 8, g % 8
                bank = ps[5] if idx % 2 == 0 else ps[0]
                boff = 0 if idx % 2 == 0 else 128
                woff = wi * 8192

                def tpre(mm, idx=idx):
                    if idx >= 2:
                        mm.w(sg + f"pse_{p}_{PRJORD[idx - 2]}", cc)
                    elif idx == 0:
                        wpc(mm, sg + "pse_{p}_22", p, cc)

                def temit(E_, ot=ot, woff=woff, bank=bank, boff=boff):
                    for it in range(8):
                        i = E_.matmul(out=bank[:, boff:boff + 128],
                                      lhsT=AB[:, woff + it * 1024 + ot * 128:
                                              woff + it * 1024 + ot * 128 + 128],
                                      rhs=XN[:, no + it * 128: no + (it + 1) * 128],
                                      start=(it == 0), stop=(it == 7))
                    return i

                def spre(mm, g=g, wi=wi, ot=ot):
                    mm.w(sg + f"pjg_{p}_{g}", cc)
                    fmt = {0: "s74", 1: "s54", 2: "s55"}[wi]
                    wpc(mm, sg + fmt + "_{p}_" + str(ot), p, cc)

                def semit(E_, wi=wi, ot=ot, bank=bank, boff=boff):
                    if wi == 0:
                        dst = PB[:, QO + ot * 128: QO + (ot + 1) * 128]
                    elif wi == 1:
                        dst = PB[:, KO + ot * 128: KO + (ot + 1) * 128]
                    else:
                        dst = XF[:, VF + ot * 128: VF + (ot + 1) * 128]
                    return E_.activation(out=dst, in_=bank[:, boff:boff + 128],
                                         func=AF.Silu)

                return [("tensor", sg + f"pjg_{p}_{g}", tpre, temit),
                        ("scalar", sg + f"pse_{p}_{g}", spre, semit)]

            def run_steps(stlist):
                for e_, nm_, pre_, emit_ in stlist:
                    if eng == e_:
                        if pre_ is not None:
                            pre_(m)
                        m.op(emit_(E), nm_)
                        if nm_ == sg + f"pjg_{p}_23" and m.count_mode:
                            m.S["reg"][sg + f"pj_{p}"] = m.S["reg"][nm_]

            if eng == "tensor":
                m.ladwait(f"n{p}", cc)
                for it in range(8):
                    i = E.matmul(out=pbf[0:16, 0:128],
                                 lhsT=H["WBA"][:, it * 16:(it + 1) * 16],
                                 rhs=XN[:, no + it * 128: no + (it + 1) * 128],
                                 start=(it == 0), stop=(it == 7))
                m.op(i, sg + f"bap_{p}")
            for idx in range(6):
                run_steps(proj_steps(idx))

            L("vector", lambda E_: E_.tensor_copy(out=PF[0:16, RAW:RAW + 128],
                                                  in_=pbf[0:16, 0:128]),
              pre=(lambda mm_: mm_.w(sg + f"bap_{p}", cc)))
            L("scalar", lambda E_: E_.activation(out=PF[0:16, T16A:T16A + 128],
                                                 in_=PF[0:16, RAW:RAW + 128], func=AF.Abs))
            L("scalar", lambda E_: E_.activation(out=PF[0:16, T16A:T16A + 128],
                                                 in_=PF[0:16, T16A:T16A + 128], func=AF.Exp, scale=-1.0))
            L("scalar", lambda E_: E_.activation(out=PF[0:16, T16A:T16A + 128],
                                                 in_=PF[0:16, T16A:T16A + 128], func=AF.Ln, bias=1.0))
            L("scalar", lambda E_: E_.activation(out=PF[0:16, T16B:T16B + 128],
                                                 in_=PF[0:16, RAW:RAW + 128], func=AF.Relu, scale=-1.0))
            L("vector", lambda E_: E_.tensor_add(out=PF[0:16, T16A:T16A + 128],
                                                 in0=PF[0:16, T16A:T16A + 128],
                                                 in1=PF[0:16, T16B:T16B + 128]))
            L("vector", lambda E_: E_.tensor_scalar(out=PF[0:16, T16A:T16A + 128],
                                                    in0=PF[0:16, T16A:T16A + 128],
                                                    scalar1=-1.0, scalar2=None, op0=ALU.mult))
            L("scalar", lambda E_: E_.activation(out=PF[0:16, T16B:T16B + 128],
                                                 in_=PF[0:16, RAW:RAW + 128], func=AF.Sigmoid))
            L("vector", lambda E_: E_.tensor_tensor_scan(out=PF[0:16, RAW:RAW + 128],
                                                         data0=PF[0:16, T16A:T16A + 128],
                                                         data1=PF[0:16, Z16:Z16 + 128],
                                                         initial=0.0, op0=ALU.add, op1=ALU.add))
            L("tensor", lambda E_: E_.transpose(out=pbf[0:128, 128:144],
                                                in_=PF[0:16, RAW:RAW + 128],
                                                identity=H["idf"][0:16, 0:16]))
            L("vector", lambda E_: E_.tensor_copy(out=PF[:, TRLG:TRLG + 16],
                                                  in_=pbf[0:128, 128:144]))
            L("tensor", lambda E_: E_.transpose(out=pbf[0:128, 128:144],
                                                in_=PF[0:16, T16B:T16B + 128],
                                                identity=H["idf"][0:16, 0:16]))
            L("vector", lambda E_: E_.tensor_copy(out=PF[:, TRSG:TRSG + 16],
                                                  in_=pbf[0:128, 128:144]))
            L("scalar", lambda E_: E_.activation(out=PF[:, GCO:GCO + 8],
                                                 in_=PF[:, TRLG:TRLG + 8], func=AF.Exp),
              name=f"ba{p}")

            def hsteps(h):
                ch = h % 2
                A0 = ps[1] if ch == 0 else ps[3]
                B0 = ps[2] if ch == 0 else ps[4]
                EOc = (EO, EO_B)[ch]; ETOc = (ETO, ETO_B)[ch]
                FBOc = (FBO, FBO_B)[ch]
                FBTa = (PF[:, FBTO:FBTO + 128], XF[:, FBTO_B:FBTO_B + 128])[ch]
                U1a = (PF[:, U1O:U1O + 128], XF[:, U1O_B:U1O_B + 128])[ch]
                KFa = XF[:, (KF_A, KF_B)[ch]:(KF_A, KF_B)[ch] + 128]
                GCTc = (GCT, GCT_B)[ch]
                PTc = (PT, PT_B)[ch]; XCc = (XC, XC_B)[ch]
                YOc = (YO, YO_B)[ch]; SBc = (SB, SB_B)[ch]
                QKc = (QKT, QKT_B)[ch]; KHc = (KH, KH_B)[ch]
                sqo = xbo + ch * 256
                rowo = (ro + 768) if ch == 0 else (ro + 256)
                qs = PB[:, QO + h * 128: QO + (h + 1) * 128]
                ks = PB[:, KO + h * 128: KO + (h + 1) * 128]
                Sh = PF[:, SOFF + h * 128: SOFF + (h + 1) * 128]
                lgc = PF[:, TRLG + h:TRLG + h + 1]
                bcl = PF[:, TRSG + 8 + h:TRSG + 8 + h + 1]
                gcl = PF[:, GCO + h:GCO + h + 1]
                qkrow = STT[0:1, rowo:rowo + 256]
                VFt = XF[:, VF + h * 128: VF + (h + 1) * 128]

                def N(base, hh=h):
                    return sg + base + f"_{p}_{hh}"

                def WN(*bases):
                    def f(mm):
                        for b_ in bases:
                            mm.w(N(b_), cc)
                    return f

                St = []

                def add(e_, nm_, pre_, emit_):
                    St.append((e_, N(nm_), pre_, emit_))

                add("vector", "s1",
                    lambda mm: wpc(mm, sg + "s84_{p}_" + str(h), p, cc),
                    lambda E_: E_.tensor_copy(out=PB[:, SBc:SBc + 128], in_=Sh))

                def pre16(mm):
                    mm.ladwait(f"ba{p}", cc)
                    if h >= 2:
                        mm.w(N("s21", h - 2), cc)
                        mm.w(N("s23", h - 2), cc)
                    else:
                        wpc(mm, sg + "s21_{p}_" + str(h + 6), p, cc)
                        wpc(mm, sg + "s23_{p}_" + str(h + 6), p, cc)
                add("vector", "s16", pre16,
                    lambda E_: E_.tensor_copy(out=PF[:, EOc:EOc + 128],
                                              in_=lgc.broadcast_to([128, 128])))

                def pre2a(mm):
                    mm.w(sg + f"pse_{p}_{h}", cc)
                    if h >= 2:
                        mm.w(N("s3", h - 2), cc)
                add("scalar", "s2a", pre2a,
                    lambda E_: E_.activation(out=XN[:, sqo:sqo + 128], in_=qs,
                                             func=AF.Square))
                add("scalar", "s2b",
                    lambda mm: mm.w(sg + f"pse_{p}_{8 + h}", cc),
                    lambda E_: E_.activation(out=XN[:, sqo + 128:sqo + 256], in_=ks,
                                             func=AF.Square))

                def pre17(mm):
                    mm.w(N("s16"), cc)
                    if h >= 2:
                        mm.w(N("s81", h - 2), cc)
                    elif h == 0:
                        wpc(mm, sg + "woe_{p}_7", p, cc)
                    else:
                        wpc(mm, sg + "s81_{p}_7", p, cc)
                add("tensor", "s17", pre17,
                    lambda E_: E_.transpose(out=B0[:, 0:128], in_=PF[:, EOc:EOc + 128],
                                            identity=H["idf"][:]))

                def pre3(mm):
                    mm.w(N("s2b"), cc)
                    if h >= 2:
                        mm.w(N("s84", h - 2), cc)
                    elif h == 0:
                        wpc(mm, sg + "woe_{p}_6", p, cc)
                    else:
                        wpc(mm, sg + "s84_{p}_7", p, cc)
                add("tensor", "s3", pre3,
                    lambda E_: E_.matmul(out=A0[0:1, 0:256], lhsT=H["onesb"][:, 0:1],
                                         rhs=XN[:, sqo:sqo + 256], start=True, stop=True))
                add("vector", "s18", WN("s17"),
                    lambda E_: E_.tensor_sub(out=PF[:, EOc:EOc + 128],
                                             in0=PF[:, EOc:EOc + 128], in1=B0[:, 0:128]))
                add("scalar", "s4", WN("s3"),
                    lambda E_: E_.activation(out=qkrow, in_=A0[0:1, 0:256],
                                             func=AF.Sqrt, bias=1e-6))
                add("vector", "s19", WN("s18"),
                    lambda E_: E_.tensor_scalar(out=PF[:, EOc:EOc + 128],
                                                in0=PF[:, EOc:EOc + 128],
                                                scalar1=0.0, scalar2=None, op0=ALU.min))
                add("scalar", "s20", WN("s19"),
                    lambda E_: E_.activation(out=PF[:, EOc:EOc + 128],
                                             in_=PF[:, EOc:EOc + 128], func=AF.Exp))
                add("vector", "s5", WN("s4"),
                    lambda E_: E_.reciprocal(out=qkrow, in_=qkrow))
                add("tensor", "s21", WN("s20"),
                    lambda E_: E_.transpose(out=B0[:, 128:256], in_=PF[:, EOc:EOc + 128],
                                            identity=H["idf"][:]))
                add("tensor", "s6", WN("s5"),
                    lambda E_: E_.matmul(out=A0[:, 128:384], lhsT=H["onesf"][:],
                                         rhs=qkrow, start=True, stop=True))
                add("vector", "s22", WN("s21"),
                    lambda E_: E_.tensor_copy(out=PF[:, ETOc:ETOc + 128],
                                              in_=B0[:, 128:256]))
                add("vector", "s7", WN("s6"),
                    lambda E_: E_.tensor_mul(out=qs, in0=qs, in1=A0[:, 128:256]))
                add("vector", "s23", None,
                    lambda E_: E_.scalar_tensor_tensor(out=PF[:, FBOc:FBOc + 128],
                                                       in0=PF[:, EOc:EOc + 128],
                                                       scalar=bcl, in1=H["trils"][:],
                                                       op0=ALU.mult, op1=ALU.mult))
                add("vector", "s13", None,
                    lambda E_: E_.tensor_mul(out=ks, in0=ks, in1=A0[:, 256:384]))
                add("tensor", "s24", WN("s23", "s22"),
                    lambda E_: E_.transpose(out=B0[:, 256:384], in_=PF[:, FBOc:FBOc + 128],
                                            identity=H["idf"][:]))
                add("tensor", "s14", WN("s13"),
                    lambda E_: E_.matmul(out=A0[:, 0:128], lhsT=ks, rhs=ks,
                                         start=True, stop=True))
                add("scalar", "s25", WN("s24"),
                    lambda E_: E_.copy(out=FBTa, in_=B0[:, 256:384]))
                add("tensor", "s15", None,
                    lambda E_: E_.matmul(out=A0[:, 384:512], lhsT=ks, rhs=qs,
                                         start=True, stop=True))
                add("vector", "s26", WN("s15"),
                    lambda E_: E_.tensor_mul(out=PB[:, XCc:XCc + 128],
                                             in0=A0[:, 0:128], in1=PF[:, FBOc:FBOc + 128]))
                add("vector", "s27", WN("s25"),
                    lambda E_: E_.tensor_mul(out=PB[:, PTc:PTc + 128],
                                             in0=A0[:, 0:128], in1=FBTa))
                add("vector", "s13b", None,
                    lambda E_: E_.tensor_copy(out=KFa, in_=ks))
                add("vector", "s28", None,
                    lambda E_: E_.tensor_mul(out=U1a,
                                             in0=PF[:, ETOc:ETOc + 128],
                                             in1=H["triui"][:]))
                add("vector", "s29", WN("s28"),
                    lambda E_: E_.tensor_mul(out=PB[:, QKc:QKc + 128],
                                             in0=A0[:, 384:512], in1=U1a))
                for j in range(1, 7):
                    pjm = PTc + (j - 1) * 128
                    if j == 1:
                        pa = WN("s26", "s27", "s29")
                        pb_ = WN("s18")
                    else:
                        pa = WN(f"sqc{j - 1}", f"sqd{j - 1}")
                        pb_ = None
                    add("tensor", f"sqa{j}", pa,
                        lambda E_, pjm=pjm: E_.matmul(out=A0[:, 128:256],
                                                      lhsT=PB[:, pjm:pjm + 128],
                                                      rhs=PB[:, XCc:XCc + 128],
                                                      start=True, stop=True))
                    add("tensor", f"sqb{j}", pb_,
                        lambda E_, pjm=pjm: E_.matmul(out=B0[:, 0:128],
                                                      lhsT=PB[:, XCc:XCc + 128],
                                                      rhs=PB[:, pjm:pjm + 128],
                                                      start=True, stop=True))
                    add("vector", f"sqc{j}", WN(f"sqb{j}"),
                        lambda E_: E_.tensor_copy(out=PB[:, XCc:XCc + 128],
                                                  in_=A0[:, 128:256]))
                    add("scalar", f"sqd{j}", WN(f"sqb{j}"),
                        lambda E_, j=j: E_.copy(out=PB[:, PTc + j * 128:PTc + (j + 1) * 128],
                                                in_=B0[:, 0:128]))
                add("tensor", "s54", WN("sqc6", "s1"),
                    lambda E_: E_.matmul(out=A0[:, 256:384], lhsT=ks,
                                         rhs=PB[:, SBc:SBc + 128], start=True, stop=True))

                def pre55(mm):
                    mm.w(N("sqd6"), cc)
                    mm.w(N("s22"), cc)
                    mm.w(sg + f"pse_{p}_{16 + h}", cc)
                add("tensor", "s55", pre55,
                    lambda E_: E_.transpose(out=B0[:, 128:256], in_=VFt,
                                            identity=H["idf"][:]))
                add("vector", "s56", WN("s54"),
                    lambda E_: E_.tensor_scalar(out=U1a,
                                                in0=A0[:, 256:384], scalar1=gcl,
                                                scalar2=bcl, op0=ALU.mult, op1=ALU.mult))
                add("vector", "s57", WN("s55", "s56"),
                    lambda E_: E_.scalar_tensor_tensor(out=PB[:, YOc:YOc + 128],
                                                       in0=B0[:, 128:256], scalar=bcl,
                                                       in1=U1a,
                                                       op0=ALU.mult, op1=ALU.subtract))
                for j in range(6, -1, -1):
                    pj = PTc + j * 128
                    pa = WN("s57") if j == 6 else WN(f"ap{j + 1}")
                    add("tensor", f"apm{j}", pa,
                        lambda E_, pj=pj: E_.matmul(out=B0[:, 256:384],
                                                    lhsT=PB[:, pj:pj + 128],
                                                    rhs=PB[:, YOc:YOc + 128],
                                                    start=True, stop=True))
                    add("vector", f"ap{j}" if j > 0 else "s73", WN(f"apm{j}"),
                        lambda E_, j=j: E_.tensor_tensor(out=PB[:, YOc:YOc + 128],
                                                         in0=PB[:, YOc:YOc + 128],
                                                         in1=B0[:, 256:384],
                                                         op=(ALU.add if j > 0 else ALU.subtract)))
                add("tensor", "s74", None,
                    lambda E_: E_.matmul(out=A0[:, 384:512], lhsT=qs,
                                         rhs=PB[:, SBc:SBc + 128], start=True, stop=True))
                add("vector", "s75", WN("s74"),
                    lambda E_: E_.tensor_scalar(out=PB[:, XCc:XCc + 128],
                                                in0=A0[:, 384:512], scalar1=gcl,
                                                scalar2=None, op0=ALU.mult))
                add("tensor", "s76", WN("s73", "s75"),
                    lambda E_: E_.matmul(out=A0[:, 128:256], lhsT=PB[:, QKc:QKc + 128],
                                         rhs=PB[:, YOc:YOc + 128], start=True, stop=True))
                add("vector", "s77", WN("s76", "s75"),
                    lambda E_: E_.tensor_tensor(out=U1a,
                                                in0=PB[:, XCc:XCc + 128],
                                                in1=A0[:, 128:256], op=ALU.add))
                add("tensor", "s78", WN("s77"),
                    lambda E_: E_.transpose(out=B0[:, 128:256], in_=U1a,
                                            identity=H["idf"][:]))

                def pre79(mm):
                    mm.w(N("s78"), cc)
                    wpc(mm, sg + "wog_{p}_7", p, cc)
                add("scalar", "s79", pre79,
                    lambda E_: E_.copy(out=PB[:, OFM + h * 128: OFM + (h + 1) * 128],
                                       in_=B0[:, 128:256]))
                add("tensor", "s80", WN("s79"),
                    lambda E_: E_.transpose(out=B0[:, 0:128], in_=KFa,
                                            identity=H["idf"][:]))
                add("vector", "s81", WN("s80"),
                    lambda E_: E_.tensor_scalar(out=PB[:, KHc:KHc + 128],
                                                in0=B0[:, 0:128],
                                                scalar1=PF[:, ETOc + 127:ETOc + 128],
                                                scalar2=None, op0=ALU.mult))
                add("tensor", "s82", WN("s81"),
                    lambda E_: E_.matmul(out=A0[:, 256:384], lhsT=PB[:, KHc:KHc + 128],
                                         rhs=PB[:, YOc:YOc + 128], start=True, stop=True))
                add("vector", "s83", None,
                    lambda E_: E_.tensor_mul(out=PF[:, GCTc:GCTc + 1], in0=gcl,
                                             in1=PF[:, ETOc + 127:ETOc + 128]))
                add("vector", "s84", WN("s82", "s83"),
                    lambda E_: E_.scalar_tensor_tensor(out=Sh, in0=Sh,
                                                       scalar=PF[:, GCTc:GCTc + 1],
                                                       in1=A0[:, 256:384],
                                                       op0=ALU.mult, op1=ALU.add))
                return St

            for pr in range(4):
                a_ = hsteps(2 * pr)
                b_ = hsteps(2 * pr + 1)
                merged = []
                for k_ in range(max(len(a_), len(b_))):
                    if k_ < len(a_):
                        merged.append(a_[k_])
                    if k_ < len(b_):
                        merged.append(b_[k_])
                if pr < 3:
                    extra = []
                    for idx in range(6 * (pr + 1), 6 * (pr + 1) + 6):
                        extra += proj_steps(idx)
                    stride = max(1, len(merged) // (len(extra) + 1))
                    out_ = []
                    ei = 0
                    for k_, stp in enumerate(merged):
                        out_.append(stp)
                        if ei < len(extra) and (k_ + 1) % stride == 0:
                            out_.append(extra[ei])
                            ei += 1
                    out_.extend(extra[ei:])
                    merged = out_
                run_steps(merged)

            if eng == "tensor":
                m.w(sg + f"s79_{p}_6", cc)
                m.w(sg + f"s79_{p}_7", cc)
                for ot in range(8):
                    bank = ps[1 + ot % 2]
                    if ot >= 2:
                        m.w(sg + f"woe_{p}_{ot - 2}", cc)
                    elif ot == 0:
                        m.w(sg + f"s84_{p}_6", cc)
                    else:
                        m.w(sg + f"s81_{p}_6", cc)
                    for it in range(8):
                        i = E.matmul(out=bank[:, 0:128],
                                     lhsT=AB[:, 24576 + it * 1024 + ot * 128:
                                             24576 + it * 1024 + ot * 128 + 128],
                                     rhs=PB[:, OFM + it * 128: OFM + (it + 1) * 128],
                                     start=(it == 0), stop=(it == 7))
                    m.op(i, sg + f"wog_{p}_{ot}")
            if eng == "vector":
                m.wprev(sg + f"sst_{p}", cc)
                for ot in range(8):
                    m.w(sg + f"wog_{p}_{ot}", cc)
                    i = E.tensor_tensor(out=XF[:, xpo + ot * 128: xpo + (ot + 1) * 128],
                                        in0=ps[1 + ot % 2][:, 0:128],
                                        in1=XF[:, xao + ot * 128: xao + (ot + 1) * 128],
                                        op=ALU.add)
                    m.op(i, sg + f"woe_{p}_{ot}")

    def epi(m):
        if eng == "tensor":
            m.epims(sg + "end_t", sg + "_t")
        if eng == "vector":
            m.epims(sg + "end_v", sg + "_v")
        if eng == "scalar":
            m.epims(sg + "end_s", sg + "_s")

    m.run_stage(sg, n_iter, pro, body, epi)


def stage_b2(m, sg="b2"):
    H = m.H; E = m.E; eng = m.eng
    AA, AB, XF, XN, PB = H["AA"], H["AB"], H["XF"], H["XN"], H["PB"]
    STT = H["STT"]
    ps = H["ps"]; pstat = H["pstat"]
    n_iter = KCFG["b2"]

    def pro(m):
        if eng == "gpsimd":
            m.w("b1end_t")
            m.w("b1end_v")
            m.wsem("b1xst", KCFG["b1"] * 2 * 16)
            m.wsem("b1ldx0", KCFG["b1"] * 16)
            m.wsem("b1ldx1", KCFG["b1"] * 16)
            for pp in range(2):
                E.dma_start(
                    out=XF.ap()[:, pp * 2048:pp * 2048 + 2048].rearrange(
                        "p (n t) -> p n t", t=256),
                    in_=H["xpb"].rearrange("n p t -> p n t")[:, :, ts(pp, 256)]
                ).then_inc(m.sem(sg + "ldp" + str(pp)), 16)
                E.dma_start(
                    out=XF.ap()[:, 4096 + pp * 2048:4096 + pp * 2048 + 2048].rearrange(
                        "p (n t) -> p n t", t=256),
                    in_=H["x1b"].rearrange("n p t -> p n t")[:, :, ts(pp, 256)]
                ).then_inc(m.sem(sg + "ldx" + str(pp)), 16)
            E.dma_start(out=AB.ap().rearrange("p (n o) -> p n o", o=1024),
                        in_=H["W2"].rearrange("(n p) o -> p n o", p=128)
                        ).then_inc(m.sem("w_bw2"), 16)
        if eng == "tensor":
            E.wait_ge(m.sem("w_bw1"), 16)
            m.w("b1end_t")
        if eng == "sync":
            m.w("b1end_v")
        if eng == "vector":
            m.w("b1end_v")
        if eng == "scalar":
            m.w("b1end_s")

    def body(m, cc):
        for p in range(2):
            c = cc * 2 + p
            xpo = p * 2048
            x1o = 4096 + p * 2048
            no = p * 2048
            xbo = no
            ro = p * 1024
            if p == 0:
                m.lbase(cc)
            if eng == "gpsimd":
                m.w(sg + f"w2e_{p}_7", cc)
                m.wsem(sg + "xst", (c + 1) * 16)
                E.dma_start(
                    out=XF.ap()[:, xpo:xpo + 2048].rearrange("p (n t) -> p n t", t=256),
                    in_=H["xpb"].rearrange("n p t -> p n t")[:, :, ts((c + 2) % 16, 256)]
                ).then_inc(m.sem(sg + "ldp" + str(p)), 16)
                E.dma_start(
                    out=XF.ap()[:, x1o:x1o + 2048].rearrange("p (n t) -> p n t", t=256),
                    in_=H["x1b"].rearrange("n p t -> p n t")[:, :, ts((c + 2) % 16, 256)]
                ).then_inc(m.sem(sg + "ldx" + str(p)), 16)
            if eng == "sync":
                m.w(sg + f"w2e_{p}_7", cc)
                E.dma_start(
                    out=H["x2b"].rearrange("n p t -> p n t")[:, :, ts(c, 256)],
                    in_=XF.ap()[:, x1o:x1o + 2048].rearrange("p (n t) -> p n t", t=256)
                ).then_inc(m.sem(sg + "xst"), 16)
                m.wsem(sg + "xst", (c + 1) * 16)

            L = m.lad
            mrow = STT[0:1, ro:ro + 256]
            erow = STT[0:1, ro + 256:ro + 512]
            m2row = STT[0:1, ro + 512:ro + 768]
            L("vector", lambda E_: E_.tensor_copy(out=XN[:, xbo:xbo + 2048],
                                                  in_=XF[:, xpo:xpo + 2048]),
              extra_waits=((sg + "ldp" + str(p), cc * 16),))
            def _s1(E_):
                for it in range(8):
                    i = E_.matmul(out=pstat[0:1, 0:256], lhsT=H["onesb"][:, 0:1],
                                  rhs=XN[:, xbo + it * 256: xbo + (it + 1) * 256],
                                  start=(it == 0), stop=(it == 7))
                return i
            def _s2(E_):
                for it in range(8):
                    i = E_.matmul(out=pstat[0:1, 256:512], lhsT=H["onesb"][:, 0:1],
                                  rhs=XN[:, xbo + it * 256: xbo + (it + 1) * 256],
                                  start=(it == 0), stop=(it == 7))
                return i
            L("tensor", _s1)
            L("scalar", lambda E_: E_.activation(out=XN[:, xbo:xbo + 2048],
                                                 in_=XN[:, xbo:xbo + 2048], func=AF.Square))
            L("tensor", _s2)
            L("vector", lambda E_: E_.tensor_scalar(out=mrow, in0=pstat[0:1, 0:256],
                                                    scalar1=1.0 / 1024, scalar2=None, op0=ALU.mult))
            L("vector", lambda E_: E_.tensor_scalar(out=erow, in0=pstat[0:1, 256:512],
                                                    scalar1=1.0 / 1024, scalar2=None, op0=ALU.mult))
            L("scalar", lambda E_: E_.activation(out=m2row, in_=mrow, func=AF.Square))
            L("vector", lambda E_: E_.tensor_sub(out=erow, in0=erow, in1=m2row))
            L("scalar", lambda E_: E_.activation(out=erow, in_=erow, func=AF.Sqrt, bias=1e-5))
            L("vector", lambda E_: E_.reciprocal(out=erow, in_=erow))
            L("tensor", lambda E_: E_.matmul(out=ps[0][:, 0:256], lhsT=H["onesf"][:],
                                             rhs=mrow, start=True, stop=True))
            L("tensor", lambda E_: E_.matmul(out=ps[0][:, 256:512], lhsT=H["onesf"][:],
                                             rhs=erow, start=True, stop=True))
            for it in range(8):
                tt = H["PF"][:, (it % 2) * 256: (it % 2) * 256 + 256]
                L("vector", lambda E_, it=it, tt=tt: E_.tensor_sub(
                    out=tt, in0=XF[:, xpo + it * 256: xpo + (it + 1) * 256],
                    in1=ps[0][:, 0:256]))
                L("vector", lambda E_, it=it, tt=tt: E_.tensor_mul(out=tt, in0=tt,
                                                                  in1=ps[0][:, 256:512]))
                L("vector", lambda E_, it=it, tt=tt: E_.tensor_scalar(
                    out=XN[:, no + it * 256: no + (it + 1) * 256],
                    in0=tt, scalar1=H["ln2wc"][:, it:it + 1],
                    scalar2=H["ln2bc"][:, it:it + 1], op0=ALU.mult, op1=ALU.add),
                  name=(f"n{p}" if it == 7 else None))

            if eng == "tensor":
                m.ladwait(f"n{p}", cc)
                for ot in range(32):
                    bank = ps[1 + ot % 2]
                    if ot >= 2:
                        m.w(sg + f"ge_{p}_{ot - 2}", cc)
                    else:
                        wpc(m, sg + "ge_{p}_" + str(30 + ot), p, cc)
                    for it in range(8):
                        i = E.matmul(out=bank[:, 0:256],
                                     lhsT=AA[:, it * 4096 + ot * 128: it * 4096 + ot * 128 + 128],
                                     rhs=XN[:, no + it * 256: no + (it + 1) * 256],
                                     start=(it == 0), stop=(it == 7))
                    m.op(i, sg + f"w1g_{p}_{ot}")
            if eng == "scalar":
                for ot in range(32):
                    bank = ps[1 + ot % 2]
                    m.w(sg + f"w1g_{p}_{ot}", cc)
                    if ot == 0:
                        wpc(m, sg + "w2g_{p}_7", p, cc)
                    i = E.activation(out=PB[:, ot * 256:(ot + 1) * 256], in_=bank[:, 0:256],
                                     func=AF.Gelu, bias=H["b1c"][:, ot:ot + 1])
                    m.op(i, sg + f"ge_{p}_{ot}")
            if eng == "tensor":
                E.wait_ge(m.sem("w_bw2"), 16)
                m.w(sg + f"ge_{p}_31", cc)
                for ot in range(8):
                    bank = ps[3 + ot % 2]
                    if ot >= 2:
                        m.w(sg + f"w2e_{p}_{ot - 2}", cc)
                    else:
                        wpc(m, sg + "e1_{p}_" + str(6 + ot), p, cc)
                    for it in range(32):
                        i = E.matmul(out=bank[:, 0:256],
                                     lhsT=AB[:, it * 1024 + ot * 128: it * 1024 + ot * 128 + 128],
                                     rhs=PB[:, it * 256:(it + 1) * 256],
                                     start=(it == 0), stop=(it == 31))
                    m.op(i, sg + f"w2g_{p}_{ot}")
            if eng == "vector":
                m.wsem(sg + "ldx" + str(p), cc * 16)
                for ot in range(8):
                    bank = ps[3 + ot % 2]
                    m.w(sg + f"w2g_{p}_{ot}", cc)
                    x1t = XF[:, x1o + ot * 256: x1o + (ot + 1) * 256]
                    i = E.scalar_tensor_tensor(out=x1t, in0=bank[:, 0:256],
                                               scalar=H["b2c"][:, ot:ot + 1], in1=x1t,
                                               op0=ALU.add, op1=ALU.add)
                    m.op(i, sg + f"e1_{p}_{ot}")
                    m.w(sg + f"e1_{p}_{ot}", cc)
                    i2 = E.tensor_add(out=x1t, in0=x1t,
                                      in1=XF[:, xpo + ot * 256: xpo + (ot + 1) * 256])
                    m.op(i2, sg + f"w2e_{p}_{ot}")

    def epi(m):
        if eng == "tensor":
            m.epims(sg + "end_t", sg + "_t")
        if eng == "vector":
            m.epims(sg + "end_v", sg + "_v")
        if eng == "scalar":
            m.epims(sg + "end_s", sg + "_s")
        if eng == "sync":
            m.epims(sg + "end_st", sg + "_sy")

    m.run_stage(sg, n_iter, pro, body, epi)


def emit_b(m):
    stage_b1(m)
    stage_b2(m)


def build_core(phases=("A",), dbg=False):
    nc = bacc.Bacc("TRN2", target_bir_lowering=False)
    H = {}
    H["xT"] = nc.dram_tensor("xT", [8, 128, 4096], F32, kind="ExternalInput").ap()
    for nm, shp, dt in [
        ("w1T", [1024, 4096], F32), ("w2T", [1024, 4096], F32),
        ("h1T", [4096, 4096], F32), ("h2T", [4096, 4096], F32),
        ("w3T", [4096, 1024], F32),
        ("s1c", [128, 32], F32), ("s2c", [128, 32], F32),
        ("h1sc", [128, 32], F32), ("h2sc", [128, 32], F32), ("s3c", [128, 8], F32),
        ("Wq", [1024, 1024], F32), ("Wk", [1024, 1024], F32),
        ("Wv", [1024, 1024], F32), ("Wo", [1024, 1024], F32),
        ("Wba", [1024, 16], F32), ("W1", [1024, 4096], F32), ("W2", [4096, 1024], F32),
        ("b1c", [128, 32], F32), ("b2c", [128, 8], F32),
        ("ln1wc", [128, 8], F32), ("ln1bc", [128, 8], F32),
        ("ln2wc", [128, 8], F32), ("ln2bc", [128, 8], F32),
        ("IDB", [128, 128], BF16), ("IDF", [128, 128], F32),
        ("TRILS", [128, 128], F32), ("TRIUI", [128, 128], F32),
        ("ONESB", [128, 128], BF16), ("ONESF", [1, 128], F32),
        ("CONSTV", [128, 4], F32),
    ]:
        H[nm] = nc.dram_tensor(nm, shp, dt, kind="ExternalInput").ap()
    sk = "ExternalOutput" if dbg else "Internal"
    H["h1b"] = nc.dram_tensor("h1b", [4096, 4096], BF16, kind="Internal").ap()
    H["h2b"] = nc.dram_tensor("h2b", [4096, 4096], BF16, kind="Internal").ap()
    H["ybuf"] = nc.dram_tensor("ybuf", [32, 128, 4096], BF16, kind="Internal").ap()
    H["gbuf"] = nc.dram_tensor("gbuf", [32, 128, 4096], BF16, kind="Internal").ap()
    H["x1b"] = nc.dram_tensor("x1b", [8, 128, 4096], F32, kind=sk).ap()
    H["xpb"] = nc.dram_tensor("xpb", [8, 128, 4096], F32, kind=sk).ap()
    H["x2b"] = nc.dram_tensor("x2b", [8, 128, 4096], F32, kind=sk).ap()
    H["outT"] = nc.dram_tensor("outT", [8, 128, 4096], F32, kind="ExternalOutput").ap()

    with ExitStack() as ctx:
        sb = lambda name, shp, dt: ctx.enter_context(nc.sbuf_tensor(name, shp, dt))
        H["AA"] = sb("AA", [128, 32768], BF16)
        H["AB"] = sb("AB", [128, 32768], BF16)
        H["XF"] = sb("XF", [128, 8192], F32)
        H["XN"] = sb("XNs", [128, 4096], BF16)
        H["PB"] = sb("PBs", [128, 8192], BF16)
        H["PF"] = sb("PFs", [128, 2560], F32)
        H["STT"] = sb("STT", [128, 2048], F32)
        H["T1"] = sb("T1", [128, 512], BF16)
        for nm, shp, dt in [("idb", [128, 128], BF16), ("idf", [128, 128], F32),
                            ("trils", [128, 128], F32), ("triui", [128, 128], F32),
                            ("onesb", [128, 128], BF16), ("onesf", [1, 128], F32),
                            ("s1c", [128, 32], F32), ("s2c", [128, 32], F32),
                            ("h1sc", [128, 32], F32), ("h2sc", [128, 32], F32),
                            ("s3c", [128, 8], F32), ("s3h", [128, 8], F32),
                            ("b1c", [128, 32], F32), ("b2c", [128, 8], F32),
                            ("ln1wc", [128, 8], F32), ("ln1bc", [128, 8], F32),
                            ("ln2wc", [128, 8], F32), ("ln2bc", [128, 8], F32)]:
            key = nm if nm not in H else nm
            H[key + "_sb"] = sb(key + "_sb", shp, dt)
        for nm in ["idb", "idf", "trils", "triui", "onesb", "onesf",
                   "s1c", "s2c", "h1sc", "h2sc", "s3c", "s3h",
                   "b1c", "b2c", "ln1wc", "ln1bc", "ln2wc", "ln2bc"]:
            H["dram_" + nm] = H.get(nm)
            H[nm] = H[nm + "_sb"]
        H["WBA"] = sb("WBA", [128, 128], BF16)
        H["constv"] = sb("constv", [128, 4], F32)
        nc.const_aps.register(F32, 1e-5, H["constv"][:, 0:1])
        nc.const_aps.register(F32, 1e-6, H["constv"][:, 1:2])
        nc.const_aps.register(F32, 1.0, H["constv"][:, 2:3])
        H["ps"] = [ctx.enter_context(nc.psum_tensor(f"ps{i}", [128, 512], F32))
                   for i in range(6)]
        H["pbf"] = ctx.enter_context(nc.psum_tensor("pbf", [128, 512], F32))
        H["pstat"] = ctx.enter_context(nc.psum_tensor("pstat", [1, 512], F32))

        semnames = ["cl", "w_w1", "w_w2", "w_h1", "w_h2", "w_cw1", "w_cw2", "s3hd"]
        engsfx = ["_sy", "_g", "_t", "_v", "_s",
                  "_syi", "_gi", "_ti", "_vi", "_si"]
        semnames += ["f1" + x for x in ["k", "ldx0", "ldx1", "ysta", "ystb"] + engsfx]
        semnames += ["f2" + x for x in ["ldy0", "ldy1", "lds0", "lds1", "lds2", "lds3",
                                        "gsta", "gstb", "ssd", "sse", "yfr0", "yfr1"] + engsfx]
        semnames += ["f4" + x for x in ["ldg0", "ldg1", "ldx0", "ldx1", "xst", "w3"] + engsfx]
        if "B" in phases:
            semnames += ["w_wq", "w_wk", "w_wv", "w_wo", "w_wba", "w_bw1", "w_bw2"]
            semnames += ["b1" + x for x in ["k", "ldx0", "ldx1", "xst"] + engsfx]
            semnames += ["b2" + x for x in ["k", "ldp0", "ldp1", "ldx0", "ldx1", "xst"] + engsfx]
        sems = {s_: ctx.enter_context(nc.semaphore(s_)) for s_ in semnames}

        shared = {"reg": {}, "stages": {}, "lad": {}}

        def fix_pro_consts(Hl):
            pass

        def emit_all(eng, E, count_mode):
            mm_ = M(nc, H, sems, eng, E, shared, count_mode)
            emit_program_patched(mm_, phases=phases, dbg=dbg)

        for eng in ENGS:
            shared["ladctr"] = {}
            shared["shctr"] = {}
            emit_all(eng, MockEngine(), True)
        block = ctx.enter_context(nc.Block())
        block.sync(lambda E: emit_all("sync", E, False))
        block.gpsimd(lambda E: emit_all("gpsimd", E, False))
        block.tensor(lambda E: emit_all("tensor", E, False))
        block.vector(lambda E: emit_all("vector", E, False))
        block.scalar(lambda E: emit_all("scalar", E, False))
    return nc


def emit_program_patched(m, phases, dbg):
    H = m.H; E = m.E; eng = m.eng
    m.stage_begin("PRO", 0)
    if eng == "sync":
        for nm in ["IDB", "IDF", "TRILS", "TRIUI", "ONESB", "ONESF"]:
            low = {"IDB": "idb", "IDF": "idf", "TRILS": "trils", "TRIUI": "triui",
                   "ONESB": "onesb", "ONESF": "onesf"}[nm]
            E.dma_start(out=H[low][:], in_=H[nm]).then_inc(m.sem("cl"), 16)
        for nm in ["s1c", "s2c", "h1sc", "h2sc", "s3c", "b1c", "b2c",
                   "ln1wc", "ln1bc", "ln2wc", "ln2bc"]:
            E.dma_start(out=H[nm][:], in_=H["dram_" + nm]).then_inc(m.sem("cl"), 16)
        E.dma_start(out=H["constv"][:], in_=H["CONSTV"]).then_inc(m.sem("cl"), 16)
    if eng == "gpsimd":
        E.dma_start(out=H["AA"].ap().rearrange("p (n o) -> p n o", o=4096),
                    in_=H["w1T"].rearrange("(n p) o -> p n o", p=128)
                    ).then_inc(m.sem("w_w1"), 16)
        E.dma_start(out=H["AB"].ap().rearrange("p (n o) -> p n o", o=4096),
                    in_=H["w2T"].rearrange("(n p) o -> p n o", p=128)
                    ).then_inc(m.sem("w_w2"), 16)
        E.dma_start(out=H["h1b"], in_=H["h1T"]).then_inc(m.sem("w_h1"), 16)
        E.dma_start(out=H["h2b"], in_=H["h2T"]).then_inc(m.sem("w_h2"), 16)
    if eng == "vector":
        E.wait_ge(m.sem("cl"), 18 * 16)
        m.ms(E.tensor_scalar(out=H["s3h"][:], in0=H["s3c"][:], scalar1=0.5,
                             scalar2=None, op0=ALU.mult), "s3h", "s3hd")
    m.stage_end()

    if "A" in phases:
        ffn_phase(m, "a", H["xT"], H["x1b"], src_gate=None, first=True)
    if "B" in phases:
        emit_b(m)
    if "C" in phases:
        ffn_phase(m, "c", H["x2b"], H["outT"], src_gate="b2end_v", first=False,
                  srcx=lambda mm, cc, p: mm.wsem(
                      "b2xst", mm.sembase("b2", "b2xst") + (2 * cc + p + 1) * 16))


def kernel(**inputs):
    com, shards = host_prepare(inputs)
    nc = build_core(phases=("A", "B", "C"), dbg=False)
    nc.finalize()
    in_maps = [dict(com, xT=shards[b]) for b in range(8)]
    res = run_bass_kernel_spmd(nc, in_maps, core_ids=list(range(8)))
    out = np.stack([np.ascontiguousarray(
        np.asarray(r["outT"], np.float32).reshape(1024, 4096).T)
        for r in res.results])
    return np.ascontiguousarray(out)


# version 09
# speedup vs baseline: 1.6349x; 1.0030x over previous
import numpy as np
import ml_dtypes
import concourse.bass as bass
import concourse.bacc as bacc
import concourse.mybir as mybir
from concourse.bass_utils import run_bass_kernel_spmd
from contextlib import ExitStack

F32 = mybir.dt.float32
BF16 = mybir.dt.bfloat16
AF = mybir.ActivationFunctionType
ALU = mybir.AluOpType
ds = bass.ds
ts = bass.ts

ENGS = ("sync", "gpsimd", "tensor", "vector", "scalar")
KCFG = {"a1": 8, "a23": 4, "a4": 4, "b1": 16, "b2": 8}


class _MockInstr:
    __slots__ = ("_m",)
    def then_inc(self, sem, v):
        self._m._note(sem, v)
        return self

class MockEngine:
    def __init__(self):
        self.counts = {}
        self.n_ops = 0
    def _note(self, sem, v):
        self.counts[sem.name] = self.counts.get(sem.name, 0) + v
    def __getattr__(self, name):
        if name.startswith("_"):
            raise AttributeError(name)
        def fn(*a, **k):
            self.n_ops += 1
            mi = _MockInstr(); mi._m = self
            return mi
        return fn
    def sem_inc(self, sem, v):
        self.n_ops += 1
        self._note(sem, v)
    def wait_ge(self, sem, v):
        self.n_ops += 1


class M:
    def __init__(self, nc, H, sems, eng, E, shared, count_mode):
        self.nc = nc; self.H = H; self.sems = sems
        self.eng = eng; self.E = E
        self.S = shared
        self.count_mode = count_mode
        self.cur_stage = None
        self.cur_tag = "pro"
        self.ladpos = 0

    def stage_begin(self, name, niter, sq=None):
        self.cur_stage = name
        self.sq = sq if sq is not None else name
        if self.count_mode:
            st = self.S["stages"].setdefault(
                name, {"niter": niter, "pre": {}, "per": {}, "engops": {},
                       "base": {}, "shbase": {}, "sq": self.sq})
            st["niter"] = niter
            st["sq"] = self.sq
            self.S.setdefault("sqmap", {})[name] = self.sq
            self._snap0 = dict(self.E.counts)
            b0 = st.setdefault("base0", {})
            for k, v in self._snap0.items():
                b0[k] = max(b0.get(k, 0), v)
        else:
            self.sq = self.S["stages"][name]["sq"]
        self.cur_tag = "pro"

    def loop_begin(self):
        if self.count_mode:
            st = self.S["stages"][self.cur_stage]
            self._snap_pre = dict(self.E.counts)
            self._ops_pre = self.E.n_ops
            for k, v in self._snap_pre.items():
                st["base"][k] = max(st["base"].get(k, 0), v)
            sn = self.mysem()
            st["shbase"][sn + "i"] = max(st["shbase"].get(sn + "i", 0),
                                         self.S.setdefault("shctr", {}).get(sn + "i", 0))
        self.cur_tag = "body"

    def loop_end(self):
        if self.count_mode:
            st = self.S["stages"][self.cur_stage]
            for k, v in list(self.E.counts.items()):
                d = v - self._snap_pre.get(k, 0)
                if d:
                    st["per"][k] = st["per"].get(k, 0) + d
                    self.E.counts[k] = self._snap_pre.get(k, 0) + d * st["niter"]
            st["engops"][self.eng] = self.E.n_ops - self._ops_pre
            sn = self.mysem()
            if st["engops"][self.eng] > 0:
                self.S.setdefault("shctr", {})
                self.S["shctr"][sn + "i"] = (self.S["shctr"].get(sn + "i", 0)
                                             + st["niter"])
                self.E.counts[sn + "i"] = self.S["shctr"][sn + "i"]
        self.cur_tag = "epi"

    def stage_end(self):
        self.cur_stage = None

    def body_ops(self, stage):
        return self.S["stages"][stage]["engops"].get(self.eng, 0)

    ENG_LETTER = {"sync": "sy", "gpsimd": "g", "tensor": "t",
                  "vector": "v", "scalar": "s"}

    def mysem(self):
        return self.sq + "_" + self.ENG_LETTER[self.eng]

    def sembase(self, stage, semname):
        if self.count_mode:
            return 0
        return self.S["stages"][stage]["base"].get(semname, 0)

    def op(self, instr, name=None):
        semname = self.mysem()
        if self.count_mode:
            st = self.S["stages"][self.cur_stage]
            st.setdefault("boosted", set()).add(semname)
        return self.ms(instr, name, semname) if name else self.ms(instr, "_", semname)

    def b(self, semname):
        return 0

    def wprev(self, mname, cc):
        if self.count_mode:
            self.E.n_ops += 1
            return
        semname, stage = self.S["reg"][mname][0], self.S["reg"][mname][1]
        shb = self.S["stages"][stage]["shbase"].get(semname + "i", 0)
        self.E.wait_ge(self.sems[semname + "i"], shb + cc)

    def run_stage(self, name, niter, pro, body, epi=None, sq=None):
        self.stage_begin(name, niter, sq=sq)
        pro(self)
        self.loop_begin()
        if self.count_mode:
            st = self.S["stages"][name]
            st["ladbase"] = max(st.get("ladbase", 0),
                                self.S.setdefault("ladctr", {}).get(st["sq"], 0))
            self.ladpos = 0
            body(self, 0)
            self.E.n_ops += 2
            self.S["lad"][name] = max(self.S["lad"].get(name, 0), self.ladpos)
            if self.ladpos > 0:
                self.S["ladctr"][st["sq"]] = (st["ladbase"]
                                              + niter * self.S["lad"][name])
        else:
            if self.body_ops(name) > 0:
                st = self.S["stages"][name]
                sn = self.mysem()
                pre = st["pre"].get(sn, 0)
                per = st["per"].get(sn, 0)
                with self.E.Fori(0, niter) as cc:
                    self.ladpos = 0
                    body(self, cc)
                    if per > 0:
                        self.E.wait_ge(self.sems[sn], pre + (cc + 1) * per)
                    self.E.sem_inc(self.sems[sn + "i"], 1)
        self.loop_end()
        if epi:
            epi(self)
        self.stage_end()

    def ms(self, instr, mname, semname, v=1):
        semname = self._rs(semname)
        instr.then_inc(self.sems[semname], v)
        if self.count_mode:
            st = self.S["stages"][self.cur_stage]
            cnt = self.E.counts.get(semname, 0)
            if self.cur_tag == "pro":
                off = cnt
            elif self.cur_tag == "body":
                off = cnt - self._snap_pre.get(semname, 0)
            else:
                off = (cnt - st["base"].get(semname, 0)
                       - st["niter"] * st["per"].get(semname, 0))
            self.S["reg"][mname] = (semname, self.cur_stage, self.cur_tag, off)
        return instr

    def msval(self, mname, cc=None, dn=0):
        semname, stage, tag, off = self.S["reg"][mname]
        st = self.S["stages"][stage]
        base = st["base"].get(semname, 0)
        per = st["per"].get(semname, 0)
        if tag == "pro":
            return off + dn
        if tag == "epi":
            return base + st["niter"] * per + off + dn
        return base + cc * per + off + dn

    def w(self, mname, cc=None, dn=0):
        if self.count_mode:
            self.E.n_ops += 1
            return
        semname = self.S["reg"][mname][0]
        self.E.wait_ge(self.sems[semname], self.msval(mname, cc, dn))

    def _rs(self, name):
        sqmap = self.S.get("sqmap", {})
        for stn in sqmap:
            if name.startswith(stn):
                return sqmap[stn] + name[len(stn):]
        return name

    def _basedval(self, name, val):
        if self.count_mode:
            return val
        for stn, sq in self.S.get("sqmap", {}).items():
            if name.startswith(stn) and stn != "PRO":
                rs = sq + name[len(stn):]
                st = self.S["stages"][stn]
                if rs.endswith("i") and rs[:-1] in (
                        sq + "_sy", sq + "_g", sq + "_t", sq + "_v", sq + "_s"):
                    return st["shbase"].get(rs, 0) + val
                return st["base"].get(rs, 0) + val
        return val

    def sem(self, name):
        return self.sems[self._rs(name)]

    def wsem(self, semname, val):
        self.E.wait_ge(self.sems[self._rs(semname)], self._basedval(semname, val))

    def wsem0(self, semname, val):
        if self.count_mode:
            self.E.wait_ge(self.sems[self._rs(semname)], val)
            return
        rs = self._rs(semname)
        base = self.S["stages"][self.cur_stage].get("base0", {}).get(rs, 0)
        self.E.wait_ge(self.sems[rs], base + val)

    def epims(self, mname, semname):
        semname = self._rs(semname)
        st = self.S["stages"][self.cur_stage]
        if not self.count_mode:
            per = st["per"].get(semname, 0)
            prev = st["base"].get(semname, 0) + st["niter"] * per
            self.E.wait_ge(self.sems[semname], prev)
        self.ms(self.E.nop(), mname, semname)

    def credit(self, semname, v):
        self.E.sem_inc(self.sems[self._rs(semname)], v)

    def rel(self, semname, v=1):
        self.E.sem_inc(self.sems[self._rs(semname)], v)

    def lad(self, owner, fn, name=None, extra_waits=(), extra_incs=(), pre=None):
        pos = self.ladpos
        self.ladpos += 1
        if name is not None and self.count_mode:
            self.S["reg"].setdefault("LAD:" + self.cur_stage + ":" + name, pos + 1)
        if self.eng != owner:
            return
        semk = self.sems[self.sq + "k"]
        if self.count_mode:
            self.E.n_ops += 1
            if pre is not None:
                pre(self)
            instr = fn(self.E)
            instr.then_inc(semk, 1)
            for s_, v_ in extra_incs:
                instr.then_inc(self.sems[self._rs(s_)], v_)
            return instr
        self.E.wait_ge(semk, self.lad_base + pos)
        if pre is not None:
            pre(self)
        for s_, v_ in extra_waits:
            self.E.wait_ge(self.sems[self._rs(s_)], self._basedval(s_, v_))
        instr = fn(self.E)
        instr.then_inc(semk, 1)
        for s_, v_ in extra_incs:
            instr.then_inc(self.sems[s_], v_)
        return instr

    def ladwait(self, name, cc, stage=None):
        stage = stage or self.cur_stage
        if self.count_mode:
            self.E.n_ops += 1
            return
        pos1 = self.S["reg"]["LAD:" + stage + ":" + name]
        per = self.S["lad"][stage]
        st = self.S["stages"][stage]
        self.E.wait_ge(self.sems[st["sq"] + "k"],
                       st.get("ladbase", 0) + cc * per + pos1)

    def lbase(self, cc, stage=None):
        if self.count_mode:
            self.lad_base = 0
            return
        st = self.S["stages"][stage or self.cur_stage]
        self.lad_base = (st.get("ladbase", 0)
                         + cc * self.S["lad"][stage or self.cur_stage])


def wpc(m, namefmt, p, cc):
    pm = 1 - p
    nm = namefmt.format(p=pm)
    if p == 1:
        m.w(nm, cc)
    else:
        m.wprev(nm, cc)


def host_prepare(inputs):
    f32 = np.float32
    g = lambda k: np.asarray(inputs[k], f32)
    com = {}
    com["w1T"] = np.ascontiguousarray(g("L0_ffn_w1_w").T)
    com["w2T"] = np.ascontiguousarray(g("L0_ffn_w2_w").T)
    com["h1T"] = np.ascontiguousarray(g("L0_ffn_h1_w").T)
    com["h2T"] = np.ascontiguousarray(g("L0_ffn_h2_w").T)
    com["w3T"] = np.ascontiguousarray(g("L0_ffn_w3_w").T)
    com["s1c"] = np.ascontiguousarray(g("L0_ffn_w1_s").reshape(32, 128).T)
    com["s2c"] = np.ascontiguousarray(g("L0_ffn_w2_s").reshape(32, 128).T)
    com["h1sc"] = np.ascontiguousarray(g("L0_ffn_h1_s").reshape(32, 128).T)
    com["h2sc"] = np.ascontiguousarray(g("L0_ffn_h2_s").reshape(32, 128).T)
    com["s3c"] = np.ascontiguousarray(g("L0_ffn_w3_s").reshape(8, 128).T)
    com["Wq"] = np.ascontiguousarray(g("L0_attn_Wq"))
    com["Wk"] = np.ascontiguousarray(g("L0_attn_Wk"))
    com["Wv"] = np.ascontiguousarray(g("L0_attn_Wv"))
    com["Wo"] = np.ascontiguousarray(g("L0_attn_Wo"))
    com["Wba"] = np.ascontiguousarray(
        np.concatenate([g("L0_attn_Wa"), g("L0_attn_Wb")], axis=1))
    com["W1"] = np.ascontiguousarray(g("L0_attn_W1"))
    com["W2"] = np.ascontiguousarray(g("L0_attn_W2"))
    com["b1c"] = np.ascontiguousarray(g("L0_attn_b1").reshape(32, 128).T)
    com["b2c"] = np.ascontiguousarray(g("L0_attn_b2").reshape(8, 128).T)
    com["ln1wc"] = np.ascontiguousarray(g("L0_attn_ln1_w").reshape(8, 128).T)
    com["ln1bc"] = np.ascontiguousarray(g("L0_attn_ln1_b").reshape(8, 128).T)
    com["ln2wc"] = np.ascontiguousarray(g("L0_attn_ln2_w").reshape(8, 128).T)
    com["ln2bc"] = np.ascontiguousarray(g("L0_attn_ln2_b").reshape(8, 128).T)
    com["IDB"] = np.eye(128, dtype=ml_dtypes.bfloat16)
    com["IDF"] = np.eye(128, dtype=f32)
    com["TRILS"] = np.tril(np.ones((128, 128), f32), -1)
    com["TRIUI"] = np.triu(np.ones((128, 128), f32), 0)
    com["ONESB"] = np.ones((128, 128), ml_dtypes.bfloat16)
    com["ONESF"] = np.ones((1, 128), f32)
    com["CONSTV"] = np.broadcast_to(
        np.array([1e-5, 1e-6, 1.0, 0.0], f32), (128, 4)).copy()
    x = np.asarray(inputs["L0_attn_x"], f32)
    shards = [np.ascontiguousarray(x[b].T).reshape(8, 128, 4096) for b in range(8)]
    return com, shards


def _inner(m, n, fn):
    if m.count_mode:
        for i in range(n):
            fn(i)
    else:
        with m.E.Fori(0, n) as i:
            fn(i)
M.inner = _inner


def _dead_emit_program(m, phases=("A",), dbg=False):
    H = m.H; E = m.E; eng = m.eng

    m.stage_begin("PRO", 0)
    if eng == "sync":
        pairs = [("IDB", "idb"), ("IDF", "idf"), ("TRILS", "trils"), ("TRIUI", "triui"),
                 ("ONESB", "onesb"), ("ONESF", "onesf"), ("s1c", "s1c"), ("s2c", "s2c"),
                 ("h1sc", "h1sc"), ("h2sc", "h2sc"), ("s3c", "s3c"), ("b1c", "b1c"),
                 ("b2c", "b2c"), ("ln1wc", "ln1wc"), ("ln1bc", "ln1bc"),
                 ("ln2wc", "ln2wc"), ("ln2bc", "ln2bc")]
        for src, dst in pairs:
            E.dma_start(out=H[dst][:], in_=H[src]).then_inc(m.sem("cl"), 16)
    if eng == "gpsimd":
        E.dma_start(out=H["AA"].ap().rearrange("p (n o) -> p n o", o=4096),
                    in_=H["w1T"].rearrange("(n p) o -> p n o", p=128)
                    ).then_inc(m.sem("w_w1"), 16)
        E.dma_start(out=H["AB"].ap().rearrange("p (n o) -> p n o", o=4096),
                    in_=H["w2T"].rearrange("(n p) o -> p n o", p=128)
                    ).then_inc(m.sem("w_w2"), 16)
        E.dma_start(out=H["h1b"], in_=H["h1T"]).then_inc(m.sem("w_h1"), 16)
        E.dma_start(out=H["h2b"], in_=H["h2T"]).then_inc(m.sem("w_h2"), 16)
    if eng == "vector":
        E.wait_ge(m.sem("cl"), 18 * 16)
        m.ms(E.tensor_scalar(out=H["s3h"][:], in0=H["s3c"][:], scalar1=0.5,
                             scalar2=None, op0=ALU.mult), "s3h", "s3hd")
    m.stage_end()

    if "A" in phases:
        ffn_phase(m, "a", H["xT"], H["x1b"],
                  src_gate=None, first=True)
    if "B" in phases:
        emit_b(m)
    if "C" in phases:
        ffn_phase(m, "c", H["x2b"], H["outT"], src_gate="b2end_v", first=False,
                  srcx=lambda mm, cc, p: mm.wsem("b2xst",
                                                 mm.sembase("b2", "b2xst")
                                                 + (2 * cc + p + 1) * 16))


def ffn_phase(m, ph, xsrc, xdst, src_gate, first, srcx=None):
    H = m.H
    stage_a1(m, ph + "1", xsrc, first, src_gate, srcx=srcx)
    stage_a23(m, ph + "2", H["ybuf"], H["gbuf"], H["h1b"], "h1sc",
              prev_end=ph + "1end_t", whsem="w_h1",
              srcw=lambda mm, cc, p: (mm.wsem(ph + "1ysta", (4 * cc + 2 * p + 2) * 16),
                                      mm.wsem(ph + "1ystb", (4 * cc + 2 * p + 2) * 16)),
              prev_pb=(ph + "1ystb", KCFG["a1"] * 2 * 16))
    stage_a23(m, ph + "3", H["gbuf"], H["ybuf"], H["h2b"], "h2sc",
              prev_end=ph + "2end_t", whsem="w_h2",
              srcw=lambda mm, cc, p: (mm.wsem(ph + "2gsta", (4 * cc + 2 * p + 2) * 16),
                                      mm.wsem(ph + "2gstb", (4 * cc + 2 * p + 2) * 16)),
              prev_gst=ph + "2gst")
    stage_a4(m, ph + "4", H["ybuf"], xsrc, xdst,
             prev_end=ph + "3end_t", a1_v_end=ph + "1end_v",
             srcw=lambda mm, cc, p: (mm.wsem(ph + "3gsta", (4 * cc + 2 * p + 2) * 16),
                                     mm.wsem(ph + "3gstb", (4 * cc + 2 * p + 2) * 16)),
             prev_gst=ph + "3gst",
             srcx=(None if srcx is None else
                   (lambda mm, cc, p: mm.wsem(
                       "b2xst", mm.sembase("b2", "b2xst") + (4 * cc + 2 * p + 2) * 16))))


def stage_a1(m, sg, xsrc, first, src_gate, srcx=None):
    H = m.H; E = m.E; eng = m.eng
    AA, AB, XF, XN, PB = H["AA"], H["AB"], H["XF"], H["XN"], H["PB"]
    STT, T1 = H["STT"], H["T1"]
    ps = H["ps"]; pstat = H["pstat"]
    n_iter = KCFG["a1"]

    def pro(m):
        if eng == "sync":
            if src_gate is not None:
                m.w(src_gate)

        if eng in ("vector", "scalar", "tensor"):
            E.wait_ge(m.sem("cl"), 18 * 16)
        if eng == "tensor":
            if first:
                E.wait_ge(m.sem("w_w1"), 16)
                E.wait_ge(m.sem("w_w2"), 16)
            else:
                E.wait_ge(m.sem("w_cw1"), 16)
                E.wait_ge(m.sem("w_cw2"), 16)
        if eng == "gpsimd" and not first:
            m.w("b2end_t")
            m.w("b2end_s")
            m.w("b2end_v")
            for s_ in ["b2ldp0", "b2ldp1", "b2ldx0", "b2ldx1"]:
                m.wsem(s_, KCFG["b2"] * 16)
            E.dma_start(out=AA.ap().rearrange("p (n o) -> p n o", o=4096),
                        in_=H["w1T"].rearrange("(n p) o -> p n o", p=128)
                        ).then_inc(m.sem("w_cw1"), 16)
            E.dma_start(out=AB.ap().rearrange("p (n o) -> p n o", o=4096),
                        in_=H["w2T"].rearrange("(n p) o -> p n o", p=128)
                        ).then_inc(m.sem("w_cw2"), 16)

    def body(m, cc):
        for p in range(2):
            cg = cc * 2 + p
            xoff = p * 2048
            noff = p * 2048
            ro = p * 1024
            if p == 0:
                m.lbase(cc)
            if eng == "gpsimd":
                m.wprev(sg + f"g_{p}_31", cc)
                if srcx is not None:
                    srcx(m, cc, p)
                E.dma_start(
                    out=XF.ap()[:, xoff:xoff + 2048].rearrange("p (n t) -> p n t", t=256),
                    in_=xsrc.rearrange("n p t -> p n t")[:, :, ts(cg, 256)]
                ).then_inc(m.sem(sg + "ldx" + str(p)), 16)
            m.lad("vector", lambda E_: E_.tensor_copy(
                out=XN[:, noff:noff + 2048], in_=XF[:, xoff:xoff + 2048]),
                extra_waits=((sg + "ldx" + str(p), (cc + 1) * 16),))
            m.lad("scalar", lambda E_: E_.activation(
                out=PB[:, 0:2048], in_=XN[:, noff:noff + 2048], func=AF.Square),
                extra_waits=((sg + "ysta", cg * 16),))
            def _s1(E_):
                for it in range(8):
                    i = E_.matmul(out=pstat[0:1, 0:256], lhsT=H["onesb"][:, 0:1],
                                  rhs=XN[:, noff + it * 256: noff + (it + 1) * 256],
                                  start=(it == 0), stop=(it == 7))
                return i
            m.lad("tensor", _s1)
            def _s2(E_):
                for it in range(8):
                    i = E_.matmul(out=pstat[0:1, 256:512], lhsT=H["onesb"][:, 0:1],
                                  rhs=PB[:, it * 256:(it + 1) * 256],
                                  start=(it == 0), stop=(it == 7))
                return i
            m.lad("tensor", _s2)
            mrow = STT[0:1, ro:ro + 256]
            erow = STT[0:1, ro + 256:ro + 512]
            m2row = STT[0:1, ro + 512:ro + 768]
            m.lad("vector", lambda E_: E_.tensor_scalar(
                out=mrow, in0=pstat[0:1, 0:256], scalar1=1.0 / 1024,
                scalar2=None, op0=ALU.mult))
            m.lad("vector", lambda E_: E_.tensor_scalar(
                out=erow, in0=pstat[0:1, 256:512], scalar1=1.0 / 1024,
                scalar2=None, op0=ALU.mult))
            m.lad("scalar", lambda E_: E_.activation(out=m2row, in_=mrow, func=AF.Square))
            m.lad("vector", lambda E_: E_.tensor_sub(out=erow, in0=erow, in1=m2row))
            m.lad("scalar", lambda E_: E_.activation(out=erow, in_=erow,
                                                     func=AF.Sqrt, bias=1e-5))
            m.lad("vector", lambda E_: E_.reciprocal(out=erow, in_=erow))
            m.lad("tensor", lambda E_: E_.matmul(
                out=ps[0][:, 0:256], lhsT=H["onesf"][:], rhs=mrow,
                start=True, stop=True))
            m.lad("tensor", lambda E_: E_.matmul(
                out=ps[0][:, 256:512], lhsT=H["onesf"][:], rhs=erow,
                start=True, stop=True))
            for it in range(8):
                tt = XF[:, 4096 + it * 256: 4096 + (it + 1) * 256]
                m.lad("vector", lambda E_, it=it, tt=tt: E_.tensor_sub(
                    out=tt, in0=XF[:, xoff + it * 256: xoff + (it + 1) * 256],
                    in1=ps[0][:, 0:256]))
                m.lad("vector", lambda E_, it=it, tt=tt: E_.tensor_mul(
                    out=XN[:, noff + it * 256: noff + (it + 1) * 256],
                    in0=tt, in1=ps[0][:, 256:512]),
                    name=(f"xr{p}" if it == 7 else None))
            if eng == "tensor":
                m.ladwait(f"xr{p}", cc)
                for ot in range(32):
                    a = ot % 2
                    if ot >= 2:
                        m.w(sg + f"t1_{p}_{ot - 2}", cc)
                        m.w(sg + f"y_{p}_{ot - 2}", cc)
                    else:
                        pm = 1 - p
                        if p == 1:
                            m.w(sg + f"t1_{pm}_{ot + 30}", cc)
                            m.w(sg + f"y_{pm}_{ot + 30}", cc)
                        else:
                            m.wprev(sg + f"t1_{pm}_{ot + 30}", cc)
                            m.wprev(sg + f"y_{pm}_{ot + 30}", cc)
                    for it in range(8):
                        E.matmul(out=ps[1 + 2 * a][:, 0:256],
                                 lhsT=AA[:, it * 4096 + ot * 128: it * 4096 + ot * 128 + 128],
                                 rhs=XN[:, noff + it * 256: noff + (it + 1) * 256],
                                 start=(it == 0), stop=(it == 7))
                    for it in range(8):
                        i = E.matmul(out=ps[2 + 2 * a][:, 0:256],
                                     lhsT=AB[:, it * 4096 + ot * 128: it * 4096 + ot * 128 + 128],
                                     rhs=XN[:, noff + it * 256: noff + (it + 1) * 256],
                                     start=(it == 0), stop=(it == 7))
                    m.op(i, sg + f"g_{p}_{ot}")
            if eng == "scalar":
                for ot in range(32):
                    a = ot % 2
                    m.w(sg + f"g_{p}_{ot}", cc)
                    if ot >= 2:
                        m.w(sg + f"y_{p}_{ot - 2}", cc)
                    else:
                        pm = 1 - p
                        if p == 1:
                            m.w(sg + f"y_{pm}_{ot + 30}", cc)
                        else:
                            m.wprev(sg + f"y_{pm}_{ot + 30}", cc)
                    i = E.activation(out=T1[:, (ot % 2) * 256:(ot % 2) * 256 + 256],
                                     in_=ps[1 + 2 * a][:, 0:256], func=AF.Silu,
                                     scale=H["s1c"][:, ot:ot + 1])
                    m.op(i, sg + f"t1_{p}_{ot}")
            if eng == "vector":
                for ot in range(32):
                    a = ot % 2
                    if ot == 0:
                        m.wsem(sg + "ysta", cg * 16)
                    elif ot == 16:
                        m.wsem(sg + "ystb", cg * 16)
                    m.w(sg + f"t1_{p}_{ot}", cc)
                    i = E.scalar_tensor_tensor(
                        out=PB[:, ot * 256:(ot + 1) * 256],
                        in0=ps[2 + 2 * a][:, 0:256],
                        scalar=H["s2c"][:, ot:ot + 1],
                        in1=T1[:, (ot % 2) * 256:(ot % 2) * 256 + 256],
                        op0=ALU.mult, op1=ALU.mult)
                    m.op(i, sg + f"y_{p}_{ot}")
            if eng == "sync":
                for hh in range(2):
                    m.w(sg + f"y_{p}_{15 if hh == 0 else 31}", cc)
                    E.dma_start(
                        out=H["ybuf"].rearrange("n p t -> p n t")[:, ts(hh, 16), :][:, :, ts(cg, 256)],
                        in_=PB.ap()[:, hh * 4096:(hh + 1) * 4096].rearrange(
                            "p (n t) -> p n t", t=256)
                    ).then_inc(m.sem(sg + "yst" + "ab"[hh]), 16)

    def epi(m):
        if eng == "tensor":
            m.epims(sg + "end_t", sg + "_t")
        if eng == "vector":
            m.epims(sg + "end_v", sg + "_v")

    m.run_stage(sg, n_iter, pro, body, epi, sq="f1")


def stage_a23(m, sg, src, dst, wsrc, scname, prev_end, whsem, srcw, prev_gst=None, prev_pb=None):
    H = m.H; E = m.E; eng = m.eng
    AA, AB, PB = H["AA"], H["AB"], H["PB"]
    ps = H["ps"]
    n_iter = KCFG["a23"]

    def pro(m):
        if eng == "sync":
            m.w(prev_end)
        if eng == "gpsimd":
            m.w(prev_end)
            E.wait_ge(m.sem(whsem), 16)
        if eng == "scalar":
            m.w(prev_end)
            if prev_gst is not None:
                m.wsem(prev_gst + "a", KCFG["a23"] * 2 * 2 * 16)
                m.wsem(prev_gst + "b", KCFG["a23"] * 2 * 2 * 16)
            if prev_pb is not None:
                m.wsem(prev_pb[0], prev_pb[1])
                m.wsem(prev_pb[0][:-1] + "a", prev_pb[1])
        if eng == "tensor":
            m.credit(sg + "ssd", 2)
            m.credit(sg + "yfr0", 1)
            m.credit(sg + "yfr1", 1)

    def body(m, cc):
        for p in range(2):
            c = cc * 2 + p
            if eng == "sync":
                m.wsem(sg + "yfr" + str(p), cc)
                srcw(m, cc, p)
                E.dma_start(
                    out=AA.ap()[:, p * 16384:(p + 1) * 16384].rearrange("p (n t) -> p n t", t=512),
                    in_=src.rearrange("n p t -> p n t")[:, :, ts(c, 512)]
                ).then_inc(m.sem(sg + "ldy" + str(p)), 16)
            if eng == "gpsimd":
                def gslab(sccc):
                    for sp2 in range(2):
                        P = c * 8 + sccc * 2 + sp2
                        for sp in range(2):
                            s = sccc * 4 + sp2 * 2 + sp
                            slot = sp2 * 2 + sp
                            m.wsem(sg + "ssd", P - 1)
                            E.dma_start(
                                out=AB.ap()[:, slot * 8192:(slot + 1) * 8192].rearrange(
                                    "p (n o) -> p n o", o=256),
                                in_=wsrc.rearrange("(n p) o -> p n o", p=128)[:, :, ts(s, 256)]
                            ).then_inc(m.sem(sg + "lds" + str(slot)), 16)
                m.inner(4, gslab)
            if eng == "tensor":
                m.wsem(sg + "ldy" + str(p), (cc + 1) * 16)
                def tslab(sccc):
                    for sp2 in range(2):
                        P = c * 8 + sccc * 2 + sp2
                        for sp in range(2):
                            s = sccc * 4 + sp2 * 2 + sp
                            slot = sp2 * 2 + sp
                            m.wsem(sg + "lds" + str(slot), (c * 4 + sccc + 1) * 16)
                            for ot2 in range(2):
                                m.wsem(sg + "sse", P)
                                for it in range(32):
                                    i = E.matmul(
                                        out=ps[sp * 2 + ot2][:, 0:512],
                                        lhsT=AB[:, slot * 8192 + it * 256 + ot2 * 128:
                                                slot * 8192 + it * 256 + ot2 * 128 + 128],
                                        rhs=AA[:, p * 16384 + it * 512:
                                               p * 16384 + (it + 1) * 512],
                                        start=(it == 0), stop=(it == 31))
                                m.op(i)
                        m.wsem(sg + "_t", c * 32 + (sccc * 2 + sp2 + 1) * 4)
                        m.credit(sg + "ssd", 1)
                m.inner(4, tslab)
                m.credit(sg + "yfr" + str(p), 1)
            if eng == "scalar":
                def sslab(sc2):
                    for sl in range(2):
                        sccc = sc2 * 2 + sl
                        for sp2 in range(2):
                            for sp in range(2):
                                s = sccc * 4 + sp2 * 2 + sp
                                for ot2 in range(2):
                                    os = s * 2 + ot2
                                    m.wsem(sg + "_t", c * 32 + s * 2 + ot2 + 1)
                                    m.wsem(sg + "gst" + "ab"[sl],
                                           (c * 2 + sc2) * 16)
                                    i = E.activation(
                                        out=PB[:, ts(os % 16, 512)],
                                        in_=ps[sp * 2 + ot2][:, 0:512], func=AF.Gelu,
                                        scale=H[scname][:, ds(os, 1)])
                                    m.op(i)
                            m.wsem(sg + "_s", c * 32 + (sccc * 2 + sp2 + 1) * 4)
                            m.credit(sg + "sse", 1)
                        E.dma_start(
                            out=dst.rearrange("n p t -> p n t")[:, ts(sccc, 8), :][:, :, ts(c, 512)],
                            in_=PB.ap().rearrange("p (n t) -> p n t", t=512)[:, ts(sl, 8), :]
                        ).then_inc(m.sem(sg + "gst" + "ab"[sl]), 16)
                m.inner(2, sslab)

    def epi(m):
        if eng == "tensor":
            m.epims(sg + "end_t", sg + "_t")

    m.run_stage(sg, n_iter, pro, body, epi, sq="f2")


def stage_a4(m, sg, gsrc, xsrc, xdst, prev_end, a1_v_end, srcw, prev_gst=None, srcx=None):
    H = m.H; E = m.E; eng = m.eng
    AA, AB, XF = H["AA"], H["AB"], H["XF"]
    ps = H["ps"]
    n_iter = KCFG["a4"]

    def pro(m):
        if eng == "gpsimd":
            m.w(prev_end)
            if prev_gst is not None:
                m.wsem(prev_gst + "a", KCFG["a23"] * 2 * 2 * 16)
                m.wsem(prev_gst + "b", KCFG["a23"] * 2 * 2 * 16)
            E.dma_start(out=AA.ap().rearrange("p (n o) -> p n o", o=1024),
                        in_=H["w3T"].rearrange("(n p) o -> p n o", p=128)
                        ).then_inc(m.sem(sg + "w3"), 16)
        if eng == "sync":
            m.w(prev_end)
            m.w(a1_v_end)
            m.credit(sg + "xst", 16)
            if prev_gst is not None:
                m.wsem(prev_gst + "a", KCFG["a23"] * 2 * 2 * 16)
                m.wsem(prev_gst + "b", KCFG["a23"] * 2 * 2 * 16)
        if eng == "tensor":
            m.wsem0(sg + "w3", 16)
        if eng == "vector":
            m.w("s3h")

    def body(m, cc):
        for p in range(2):
            c = cc * 2 + p
            xoff = p * 4096
            if eng == "gpsimd":
                m.wsem(sg + "_ti", cc)
                m.wsem(sg + "xst", (c - 1) * 16)
                srcw(m, cc, p)
                if srcx is not None:
                    srcx(m, cc, p)
                E.dma_start(
                    out=AB.ap()[:, p * 16384:(p + 1) * 16384].rearrange("p (n t) -> p n t", t=512),
                    in_=gsrc.rearrange("n p t -> p n t")[:, :, ts(c, 512)]
                ).then_inc(m.sem(sg + "ldg" + str(p)), 16)
                E.dma_start(
                    out=XF.ap()[:, xoff:xoff + 4096].rearrange("p (n t) -> p n t", t=512),
                    in_=xsrc.rearrange("n p t -> p n t")[:, :, ts(c, 512)]
                ).then_inc(m.sem(sg + "ldx" + str(p)), 16)
            if eng == "sync":
                m.w(sg + f"x1_{p}", cc)
                E.dma_start(
                    out=xdst.rearrange("n p t -> p n t")[:, :, ts(c, 512)],
                    in_=XF.ap()[:, xoff:xoff + 4096].rearrange("p (n t) -> p n t", t=512)
                ).then_inc(m.sem(sg + "xst"), 16)
                m.wsem(sg + "xst", (c + 1) * 16)
            if eng == "tensor":
                m.wsem(sg + "ldg" + str(p), (cc + 1) * 16)
                for ot in range(8):
                    if ot >= 2:
                        m.w(sg + f"x1e_{p}_{ot - 2}", cc)
                    else:
                        pm = 1 - p
                        if p == 1:
                            m.w(sg + f"x1e_{pm}_{ot + 6}", cc)
                        else:
                            m.wprev(sg + f"x1e_{pm}_{ot + 6}", cc)
                    for it in range(32):
                        i = E.matmul(
                            out=ps[ot % 2][:, 0:512],
                            lhsT=AA[:, it * 1024 + ot * 128: it * 1024 + ot * 128 + 128],
                            rhs=AB[:, p * 16384 + it * 512: p * 16384 + (it + 1) * 512],
                            start=(it == 0), stop=(it == 31))
                    m.op(i, sg + f"g_{p}_{ot}")
            if eng == "vector":
                m.wsem(sg + "ldx" + str(p), (cc + 1) * 16)
                for ot in range(8):
                    m.w(sg + f"g_{p}_{ot}", cc)
                    xt = XF[:, xoff + ot * 512: xoff + (ot + 1) * 512]
                    i = E.scalar_tensor_tensor(
                        out=xt, in0=ps[ot % 2][:, 0:512],
                        scalar=H["s3h"][:, ot:ot + 1], in1=xt,
                        op0=ALU.mult, op1=ALU.add)
                    m.op(i, sg + f"x1e_{p}_{ot}")
                    if ot == 7:
                        if m.count_mode:
                            m.S["reg"][sg + f"x1_{p}"] = m.S["reg"][sg + f"x1e_{p}_{ot}"]

    def epi(m):
        if eng == "sync":
            m.epims(sg + "end_st", sg + "_sy")
        if eng == "vector":
            m.epims(sg + "end_v", sg + "_v")
        if eng == "tensor":
            m.epims(sg + "end_t", sg + "_t")

    m.run_stage(sg, n_iter, pro, body, epi, sq="f4")


def stage_b1(m, sg="b1"):
    H = m.H; E = m.E; eng = m.eng
    AA, AB, XF, XN, PB, PF = H["AA"], H["AB"], H["XF"], H["XN"], H["PB"], H["PF"]
    STT = H["STT"]
    ps = H["ps"]; pstat = H["pstat"]; pbf = H["pbf"]
    n_iter = KCFG["b1"]
    QO, KO, VO, OFM = 0, 1024, 2048, 3072
    PT = 4096
    XC, YO, SB, QKT, KH = 4992, 5120, 5248, 5376, 5504
    PT_B, XC_B, YO_B, SB_B, QKT_B, KH_B = 5632, 6528, 6656, 6784, 6912, 7040
    EO, ETO, FBO, FBTO, U1O = 0, 128, 256, 384, 512
    TRLG, TRSG, GCO, GCT = 640, 656, 672, 680
    SOFF = 1024
    RAW, T16A, T16B, Z16 = 2048, 2176, 2304, 2432
    EO_B, ETO_B, FBO_B = 2048, 2176, 2304
    FBTO_B, U1O_B, KF_A, KF_B = 4096, 4224, 4352, 4480
    GCT_B = 681
    VF = 7168

    def pro(m):
        if eng == "gpsimd":
            m.w("a4end_t")
            m.wsem("a4xst", KCFG["a4"] * 2 * 16)
            for pp in range(2):
                E.dma_start(
                    out=XF.ap()[:, pp * 1024:pp * 1024 + 1024].rearrange(
                        "p (n t) -> p n t", t=128),
                    in_=H["x1b"].rearrange("n p t -> p n t")[:, :, ts(pp, 128)]
                ).then_inc(m.sem(sg + "ldx" + str(pp)), 16)
            for nm, off in [("Wq", 0), ("Wk", 8192), ("Wv", 16384), ("Wo", 24576)]:
                E.dma_start(out=AB.ap()[:, off:off + 8192].rearrange("p (n o) -> p n o", o=1024),
                            in_=H[nm].rearrange("(n p) o -> p n o", p=128)
                            ).then_inc(m.sem("w_" + nm.lower()), 16)
            E.dma_start(out=H["WBA"].ap().rearrange("p (n o) -> p n o", o=16),
                        in_=H["Wba"].rearrange("(n p) o -> p n o", p=128)
                        ).then_inc(m.sem("w_wba"), 16)
            E.dma_start(out=AA.ap().rearrange("p (n o) -> p n o", o=4096),
                        in_=H["W1"].rearrange("(n p) o -> p n o", p=128)
                        ).then_inc(m.sem("w_bw1"), 16)
        if eng == "tensor":
            for s_ in ["w_wq", "w_wk", "w_wv", "w_wo", "w_wba"]:
                E.wait_ge(m.sem(s_), 16)
            m.w("a4end_t")
        if eng == "vector":
            m.w("a4end_v")
            m.op(E.memset(PF[:, SOFF:SOFF + 1024], 0.0))
            m.op(E.memset(PF[0:16, Z16:Z16 + 128], 0.0))
        if eng == "sync":
            m.w("a4end_st")
        if eng == "scalar":
            m.w("a4end_t")

    def body(m, cc):
        for p in range(2):
            cg = cc * 2 + p
            x1o = p * 1024
            xao = 2048 + p * 1024
            xpo = 5120 + p * 1024
            no = p * 1024
            xbo = 2048 + p * 1024
            ro = p * 1024
            if p == 0:
                m.lbase(cc)
            if eng == "gpsimd":
                m.w(sg + f"pj_{p}", cc)
                E.dma_start(
                    out=XF.ap()[:, x1o:x1o + 1024].rearrange("p (n t) -> p n t", t=128),
                    in_=H["x1b"].rearrange("n p t -> p n t")[:, :, ts((cg + 2) % 32, 128)]
                ).then_inc(m.sem(sg + "ldx" + str(p)), 16)
            if eng == "sync":
                m.w(sg + f"woe_{p}_7", cc)
                E.dma_start(
                    out=H["xpb"].rearrange("n p t -> p n t")[:, :, ts(cg, 128)],
                    in_=XF.ap()[:, xpo:xpo + 1024].rearrange("p (n t) -> p n t", t=128)
                ).then_inc(m.sem(sg + "xst"), 16)
                m.wsem(sg + "xst", (cg + 1) * 16)
                m.op(E.nop(), sg + f"sst_{p}")

            L = m.lad
            mrow = STT[0:1, ro:ro + 128]
            erow = STT[0:1, ro + 128:ro + 256]
            m2row = STT[0:1, ro + 256:ro + 384]
            L("vector", lambda E_: E_.tensor_copy(out=XN[:, xbo:xbo + 1024],
                                                  in_=XF[:, x1o:x1o + 1024]),
              extra_waits=((sg + "ldx" + str(p), cc * 16),), )
            def _s1(E_, src_off=xbo):
                for it in range(8):
                    i = E_.matmul(out=pstat[0:1, 0:128], lhsT=H["onesb"][:, 0:1],
                                  rhs=XN[:, src_off + it * 128: src_off + (it + 1) * 128],
                                  start=(it == 0), stop=(it == 7))
                return i
            L("tensor", _s1)
            L("scalar", lambda E_: E_.activation(out=XN[:, xbo:xbo + 1024],
                                                 in_=XN[:, xbo:xbo + 1024], func=AF.Square))
            def _s2(E_, src_off=xbo):
                for it in range(8):
                    i = E_.matmul(out=pstat[0:1, 128:256], lhsT=H["onesb"][:, 0:1],
                                  rhs=XN[:, src_off + it * 128: src_off + (it + 1) * 128],
                                  start=(it == 0), stop=(it == 7))
                return i
            L("tensor", _s2)
            L("vector", lambda E_: E_.tensor_scalar(out=mrow, in0=pstat[0:1, 0:128],
                                                    scalar1=1.0 / 1024, scalar2=None, op0=ALU.mult))
            L("vector", lambda E_: E_.tensor_scalar(out=erow, in0=pstat[0:1, 128:256],
                                                    scalar1=1.0 / 1024, scalar2=None, op0=ALU.mult))
            L("scalar", lambda E_: E_.activation(out=m2row, in_=mrow, func=AF.Square))
            L("vector", lambda E_: E_.tensor_sub(out=erow, in0=erow, in1=m2row))
            L("scalar", lambda E_: E_.activation(out=erow, in_=erow, func=AF.Sqrt, bias=1e-5))
            L("vector", lambda E_: E_.reciprocal(out=erow, in_=erow))
            L("tensor", lambda E_: E_.matmul(out=ps[0][:, 0:128], lhsT=H["onesf"][:],
                                             rhs=mrow, start=True, stop=True))
            L("tensor", lambda E_: E_.matmul(out=ps[0][:, 128:256], lhsT=H["onesf"][:],
                                             rhs=erow, start=True, stop=True))
            for it in range(8):
                tt = XF[:, 4096 + it * 128: 4096 + (it + 1) * 128]
                pre0 = None
                if it == 0:
                    pre0 = (lambda mm_: mm_.wprev(sg + f"woe_{p}_7", cc))
                L("vector", lambda E_, it=it, tt=tt: E_.tensor_sub(
                    out=tt, in0=XF[:, x1o + it * 128: x1o + (it + 1) * 128],
                    in1=ps[0][:, 0:128]), pre=pre0)
                L("vector", lambda E_, it=it, tt=tt: E_.tensor_mul(
                    out=XF[:, xao + it * 128: xao + (it + 1) * 128],
                    in0=tt, in1=ps[0][:, 128:256]),
                  name=(f"xa{p}" if it == 7 else None))
            m4row = STT[0:1, ro + 384:ro + 512]
            e4row = STT[0:1, ro + 512:ro + 640]
            m24row = STT[0:1, ro + 640:ro + 768]
            L("vector", lambda E_: E_.tensor_copy(out=XN[:, xbo:xbo + 1024],
                                                  in_=XF[:, xao:xao + 1024]))
            L("tensor", _s1)
            L("scalar", lambda E_: E_.activation(out=XN[:, xbo:xbo + 1024],
                                                 in_=XN[:, xbo:xbo + 1024], func=AF.Square))
            L("tensor", _s2)
            L("vector", lambda E_: E_.tensor_scalar(out=m4row, in0=pstat[0:1, 0:128],
                                                    scalar1=1.0 / 1024, scalar2=None, op0=ALU.mult))
            L("vector", lambda E_: E_.tensor_scalar(out=e4row, in0=pstat[0:1, 128:256],
                                                    scalar1=1.0 / 1024, scalar2=None, op0=ALU.mult))
            L("scalar", lambda E_: E_.activation(out=m24row, in_=m4row, func=AF.Square))
            L("vector", lambda E_: E_.tensor_sub(out=e4row, in0=e4row, in1=m24row))
            L("scalar", lambda E_: E_.activation(out=e4row, in_=e4row, func=AF.Sqrt, bias=1e-5))
            L("vector", lambda E_: E_.reciprocal(out=e4row, in_=e4row))
            L("tensor", lambda E_: E_.matmul(out=ps[0][:, 256:384], lhsT=H["onesf"][:],
                                             rhs=m4row, start=True, stop=True))
            L("tensor", lambda E_: E_.matmul(out=ps[0][:, 384:512], lhsT=H["onesf"][:],
                                             rhs=e4row, start=True, stop=True))
            for it in range(8):
                tt = XF[:, 4096 + it * 128: 4096 + (it + 1) * 128]
                L("vector", lambda E_, it=it, tt=tt: E_.tensor_sub(
                    out=tt, in0=XF[:, xao + it * 128: xao + (it + 1) * 128],
                    in1=ps[0][:, 256:384]))
                L("vector", lambda E_, it=it, tt=tt: E_.tensor_mul(
                    out=tt, in0=tt, in1=ps[0][:, 384:512]))
                L("vector", lambda E_, it=it, tt=tt: E_.tensor_scalar(
                    out=XN[:, no + it * 128: no + (it + 1) * 128],
                    in0=tt, scalar1=H["ln1wc"][:, it:it + 1],
                    scalar2=H["ln1bc"][:, it:it + 1], op0=ALU.mult, op1=ALU.add),
                  name=(f"n{p}" if it == 7 else None))
            PRJORD = [0, 1, 8, 9, 16, 17, 2, 3, 10, 11, 18, 19,
                      4, 5, 12, 13, 20, 21, 6, 7, 14, 15, 22, 23]

            def proj_steps(idx):
                g = PRJORD[idx]
                wi, ot = g // 8, g % 8
                bank = ps[5] if idx % 2 == 0 else ps[0]
                boff = 0 if idx % 2 == 0 else 128
                woff = wi * 8192

                def tpre(mm, idx=idx):
                    if idx >= 2:
                        mm.w(sg + f"pse_{p}_{PRJORD[idx - 2]}", cc)
                    elif idx == 0:
                        wpc(mm, sg + "pse_{p}_22", p, cc)

                def temit(E_, ot=ot, woff=woff, bank=bank, boff=boff):
                    for it in range(8):
                        i = E_.matmul(out=bank[:, boff:boff + 128],
                                      lhsT=AB[:, woff + it * 1024 + ot * 128:
                                              woff + it * 1024 + ot * 128 + 128],
                                      rhs=XN[:, no + it * 128: no + (it + 1) * 128],
                                      start=(it == 0), stop=(it == 7))
                    return i

                def spre(mm, g=g, wi=wi, ot=ot):
                    mm.w(sg + f"pjg_{p}_{g}", cc)
                    fmt = {0: "s74", 1: "s54", 2: "s55"}[wi]
                    wpc(mm, sg + fmt + "_{p}_" + str(ot), p, cc)

                def semit(E_, wi=wi, ot=ot, bank=bank, boff=boff):
                    if wi == 0:
                        dst = PB[:, QO + ot * 128: QO + (ot + 1) * 128]
                    elif wi == 1:
                        dst = PB[:, KO + ot * 128: KO + (ot + 1) * 128]
                    else:
                        dst = XF[:, VF + ot * 128: VF + (ot + 1) * 128]
                    return E_.activation(out=dst, in_=bank[:, boff:boff + 128],
                                         func=AF.Silu)

                return [("tensor", sg + f"pjg_{p}_{g}", tpre, temit),
                        ("scalar", sg + f"pse_{p}_{g}", spre, semit)]

            def run_steps(stlist):
                for e_, nm_, pre_, emit_ in stlist:
                    if eng == e_:
                        if pre_ is not None:
                            pre_(m)
                        m.op(emit_(E), nm_)
                        if nm_ == sg + f"pjg_{p}_23" and m.count_mode:
                            m.S["reg"][sg + f"pj_{p}"] = m.S["reg"][nm_]

            if eng == "tensor":
                m.ladwait(f"n{p}", cc)
                for it in range(8):
                    i = E.matmul(out=pbf[0:16, 0:128],
                                 lhsT=H["WBA"][:, it * 16:(it + 1) * 16],
                                 rhs=XN[:, no + it * 128: no + (it + 1) * 128],
                                 start=(it == 0), stop=(it == 7))
                m.op(i, sg + f"bap_{p}")
            for idx in range(6):
                run_steps(proj_steps(idx))

            L("vector", lambda E_: E_.tensor_copy(out=PF[0:16, RAW:RAW + 128],
                                                  in_=pbf[0:16, 0:128]),
              pre=(lambda mm_: mm_.w(sg + f"bap_{p}", cc)))
            L("scalar", lambda E_: E_.activation(out=PF[0:16, T16A:T16A + 128],
                                                 in_=PF[0:16, RAW:RAW + 128], func=AF.Abs))
            L("scalar", lambda E_: E_.activation(out=PF[0:16, T16A:T16A + 128],
                                                 in_=PF[0:16, T16A:T16A + 128], func=AF.Exp, scale=-1.0))
            L("scalar", lambda E_: E_.activation(out=PF[0:16, T16A:T16A + 128],
                                                 in_=PF[0:16, T16A:T16A + 128], func=AF.Ln, bias=1.0))
            L("scalar", lambda E_: E_.activation(out=PF[0:16, T16B:T16B + 128],
                                                 in_=PF[0:16, RAW:RAW + 128], func=AF.Relu, scale=-1.0))
            L("vector", lambda E_: E_.tensor_add(out=PF[0:16, T16A:T16A + 128],
                                                 in0=PF[0:16, T16A:T16A + 128],
                                                 in1=PF[0:16, T16B:T16B + 128]))
            L("vector", lambda E_: E_.tensor_scalar(out=PF[0:16, T16A:T16A + 128],
                                                    in0=PF[0:16, T16A:T16A + 128],
                                                    scalar1=-1.0, scalar2=None, op0=ALU.mult))
            L("scalar", lambda E_: E_.activation(out=PF[0:16, T16B:T16B + 128],
                                                 in_=PF[0:16, RAW:RAW + 128], func=AF.Sigmoid))
            L("vector", lambda E_: E_.tensor_tensor_scan(out=PF[0:16, RAW:RAW + 128],
                                                         data0=PF[0:16, T16A:T16A + 128],
                                                         data1=PF[0:16, Z16:Z16 + 128],
                                                         initial=0.0, op0=ALU.add, op1=ALU.add))
            L("tensor", lambda E_: E_.transpose(out=pbf[0:128, 128:144],
                                                in_=PF[0:16, RAW:RAW + 128],
                                                identity=H["idf"][0:16, 0:16]))
            L("vector", lambda E_: E_.tensor_copy(out=PF[:, TRLG:TRLG + 16],
                                                  in_=pbf[0:128, 128:144]))
            L("tensor", lambda E_: E_.transpose(out=pbf[0:128, 128:144],
                                                in_=PF[0:16, T16B:T16B + 128],
                                                identity=H["idf"][0:16, 0:16]))
            L("vector", lambda E_: E_.tensor_copy(out=PF[:, TRSG:TRSG + 16],
                                                  in_=pbf[0:128, 128:144]))
            L("scalar", lambda E_: E_.activation(out=PF[:, GCO:GCO + 8],
                                                 in_=PF[:, TRLG:TRLG + 8], func=AF.Exp),
              name=f"ba{p}")

            def hsteps(h):
                ch = h % 2
                A0 = ps[1] if ch == 0 else ps[3]
                B0 = ps[2] if ch == 0 else ps[4]
                EOc = (EO, EO_B)[ch]; ETOc = (ETO, ETO_B)[ch]
                FBOc = (FBO, FBO_B)[ch]
                FBTa = (PF[:, FBTO:FBTO + 128], XF[:, FBTO_B:FBTO_B + 128])[ch]
                U1a = (PF[:, U1O:U1O + 128], XF[:, U1O_B:U1O_B + 128])[ch]
                KFa = XF[:, (KF_A, KF_B)[ch]:(KF_A, KF_B)[ch] + 128]
                GCTc = (GCT, GCT_B)[ch]
                PTc = (PT, PT_B)[ch]; XCc = (XC, XC_B)[ch]
                YOc = (YO, YO_B)[ch]; SBc = (SB, SB_B)[ch]
                QKc = (QKT, QKT_B)[ch]; KHc = (KH, KH_B)[ch]
                sqo = xbo + ch * 256
                rowo = (ro + 768) if ch == 0 else (ro + 256)
                qs = PB[:, QO + h * 128: QO + (h + 1) * 128]
                ks = PB[:, KO + h * 128: KO + (h + 1) * 128]
                Sh = PF[:, SOFF + h * 128: SOFF + (h + 1) * 128]
                lgc = PF[:, TRLG + h:TRLG + h + 1]
                bcl = PF[:, TRSG + 8 + h:TRSG + 8 + h + 1]
                gcl = PF[:, GCO + h:GCO + h + 1]
                qkrow = STT[0:1, rowo:rowo + 256]
                VFt = XF[:, VF + h * 128: VF + (h + 1) * 128]

                def N(base, hh=h):
                    return sg + base + f"_{p}_{hh}"

                def WN(*bases):
                    def f(mm):
                        for b_ in bases:
                            mm.w(N(b_), cc)
                    return f

                St = []

                def add(e_, nm_, pre_, emit_):
                    St.append((e_, N(nm_), pre_, emit_))

                add("vector", "s1",
                    lambda mm: wpc(mm, sg + "s84_{p}_" + str(h), p, cc),
                    lambda E_: E_.tensor_copy(out=PB[:, SBc:SBc + 128], in_=Sh))

                def pre16(mm):
                    mm.ladwait(f"ba{p}", cc)
                    if h >= 2:
                        mm.w(N("s21", h - 2), cc)
                        mm.w(N("s23", h - 2), cc)
                    else:
                        wpc(mm, sg + "s21_{p}_" + str(h + 6), p, cc)
                        wpc(mm, sg + "s23_{p}_" + str(h + 6), p, cc)
                add("vector", "s16", pre16,
                    lambda E_: E_.tensor_copy(out=PF[:, EOc:EOc + 128],
                                              in_=lgc.broadcast_to([128, 128])))

                def pre2a(mm):
                    mm.w(sg + f"pse_{p}_{h}", cc)
                    if h >= 2:
                        mm.w(N("s3", h - 2), cc)
                add("scalar", "s2a", pre2a,
                    lambda E_: E_.activation(out=XN[:, sqo:sqo + 128], in_=qs,
                                             func=AF.Square))
                add("scalar", "s2b",
                    lambda mm: mm.w(sg + f"pse_{p}_{8 + h}", cc),
                    lambda E_: E_.activation(out=XN[:, sqo + 128:sqo + 256], in_=ks,
                                             func=AF.Square))

                def pre17(mm):
                    mm.w(N("s16"), cc)
                    if h >= 2:
                        mm.w(N("s81", h - 2), cc)
                    elif h == 0:
                        wpc(mm, sg + "woe_{p}_7", p, cc)
                    else:
                        wpc(mm, sg + "s81_{p}_7", p, cc)
                add("tensor", "s17", pre17,
                    lambda E_: E_.transpose(out=B0[:, 0:128], in_=PF[:, EOc:EOc + 128],
                                            identity=H["idf"][:]))

                def pre3(mm):
                    mm.w(N("s2b"), cc)
                    if h >= 2:
                        mm.w(N("s84", h - 2), cc)
                    elif h == 0:
                        wpc(mm, sg + "woe_{p}_6", p, cc)
                    else:
                        wpc(mm, sg + "s84_{p}_7", p, cc)
                add("tensor", "s3", pre3,
                    lambda E_: E_.matmul(out=A0[0:1, 0:256], lhsT=H["onesb"][:, 0:1],
                                         rhs=XN[:, sqo:sqo + 256], start=True, stop=True))
                add("vector", "s18", WN("s17"),
                    lambda E_: E_.tensor_sub(out=PF[:, EOc:EOc + 128],
                                             in0=PF[:, EOc:EOc + 128], in1=B0[:, 0:128]))
                add("scalar", "s4", WN("s3"),
                    lambda E_: E_.activation(out=qkrow, in_=A0[0:1, 0:256],
                                             func=AF.Sqrt, bias=1e-6))
                add("vector", "s19", WN("s18"),
                    lambda E_: E_.tensor_scalar(out=PF[:, EOc:EOc + 128],
                                                in0=PF[:, EOc:EOc + 128],
                                                scalar1=0.0, scalar2=None, op0=ALU.min))
                add("scalar", "s20", WN("s19"),
                    lambda E_: E_.activation(out=PF[:, EOc:EOc + 128],
                                             in_=PF[:, EOc:EOc + 128], func=AF.Exp))
                add("vector", "s5", WN("s4"),
                    lambda E_: E_.reciprocal(out=qkrow, in_=qkrow))
                add("tensor", "s21", WN("s20"),
                    lambda E_: E_.transpose(out=B0[:, 128:256], in_=PF[:, EOc:EOc + 128],
                                            identity=H["idf"][:]))
                add("tensor", "s6", WN("s5"),
                    lambda E_: E_.matmul(out=A0[:, 128:384], lhsT=H["onesf"][:],
                                         rhs=qkrow, start=True, stop=True))
                add("vector", "s22", WN("s21"),
                    lambda E_: E_.tensor_copy(out=PF[:, ETOc:ETOc + 128],
                                              in_=B0[:, 128:256]))
                add("vector", "s7", WN("s6"),
                    lambda E_: E_.tensor_mul(out=qs, in0=qs, in1=A0[:, 128:256]))
                add("vector", "s23", None,
                    lambda E_: E_.scalar_tensor_tensor(out=PF[:, FBOc:FBOc + 128],
                                                       in0=PF[:, EOc:EOc + 128],
                                                       scalar=bcl, in1=H["trils"][:],
                                                       op0=ALU.mult, op1=ALU.mult))
                add("vector", "s13", None,
                    lambda E_: E_.tensor_mul(out=ks, in0=ks, in1=A0[:, 256:384]))
                add("tensor", "s24", WN("s23", "s22"),
                    lambda E_: E_.transpose(out=B0[:, 256:384], in_=PF[:, FBOc:FBOc + 128],
                                            identity=H["idf"][:]))
                add("tensor", "s14", WN("s13"),
                    lambda E_: E_.matmul(out=A0[:, 0:128], lhsT=ks, rhs=ks,
                                         start=True, stop=True))
                add("scalar", "s25", WN("s24"),
                    lambda E_: E_.copy(out=FBTa, in_=B0[:, 256:384]))
                add("tensor", "s15", None,
                    lambda E_: E_.matmul(out=A0[:, 384:512], lhsT=ks, rhs=qs,
                                         start=True, stop=True))
                add("vector", "s26", WN("s15"),
                    lambda E_: E_.tensor_mul(out=PB[:, XCc:XCc + 128],
                                             in0=A0[:, 0:128], in1=PF[:, FBOc:FBOc + 128]))
                add("vector", "s27", WN("s25"),
                    lambda E_: E_.tensor_mul(out=PB[:, PTc:PTc + 128],
                                             in0=A0[:, 0:128], in1=FBTa))
                add("vector", "s13b", None,
                    lambda E_: E_.tensor_copy(out=KFa, in_=ks))
                add("vector", "s28", None,
                    lambda E_: E_.tensor_mul(out=U1a,
                                             in0=PF[:, ETOc:ETOc + 128],
                                             in1=H["triui"][:]))
                add("vector", "s29", WN("s28"),
                    lambda E_: E_.tensor_mul(out=PB[:, QKc:QKc + 128],
                                             in0=A0[:, 384:512], in1=U1a))
                for j in range(1, 7):
                    pjm = PTc + (j - 1) * 128
                    if j == 1:
                        pa = WN("s26", "s27", "s29")
                        pb_ = WN("s18")
                    else:
                        pa = WN(f"sqc{j - 1}", f"sqd{j - 1}")
                        pb_ = None
                    add("tensor", f"sqa{j}", pa,
                        lambda E_, pjm=pjm: E_.matmul(out=A0[:, 128:256],
                                                      lhsT=PB[:, pjm:pjm + 128],
                                                      rhs=PB[:, XCc:XCc + 128],
                                                      start=True, stop=True))
                    add("tensor", f"sqb{j}", pb_,
                        lambda E_, pjm=pjm: E_.matmul(out=B0[:, 0:128],
                                                      lhsT=PB[:, XCc:XCc + 128],
                                                      rhs=PB[:, pjm:pjm + 128],
                                                      start=True, stop=True))
                    add("vector", f"sqc{j}", WN(f"sqb{j}"),
                        lambda E_: E_.tensor_copy(out=PB[:, XCc:XCc + 128],
                                                  in_=A0[:, 128:256]))
                    add("scalar", f"sqd{j}", WN(f"sqb{j}"),
                        lambda E_, j=j: E_.copy(out=PB[:, PTc + j * 128:PTc + (j + 1) * 128],
                                                in_=B0[:, 0:128]))
                add("tensor", "s54", WN("sqc6", "s1"),
                    lambda E_: E_.matmul(out=A0[:, 256:384], lhsT=ks,
                                         rhs=PB[:, SBc:SBc + 128], start=True, stop=True))

                def pre55(mm):
                    mm.w(N("sqd6"), cc)
                    mm.w(N("s22"), cc)
                    mm.w(sg + f"pse_{p}_{16 + h}", cc)
                add("tensor", "s55", pre55,
                    lambda E_: E_.transpose(out=B0[:, 128:256], in_=VFt,
                                            identity=H["idf"][:]))
                add("vector", "s56", WN("s54"),
                    lambda E_: E_.tensor_scalar(out=U1a,
                                                in0=A0[:, 256:384], scalar1=gcl,
                                                scalar2=bcl, op0=ALU.mult, op1=ALU.mult))
                add("vector", "s57", WN("s55", "s56"),
                    lambda E_: E_.scalar_tensor_tensor(out=PB[:, YOc:YOc + 128],
                                                       in0=B0[:, 128:256], scalar=bcl,
                                                       in1=U1a,
                                                       op0=ALU.mult, op1=ALU.subtract))
                for j in range(6, -1, -1):
                    pj = PTc + j * 128
                    pa = WN("s57") if j == 6 else WN(f"ap{j + 1}")
                    add("tensor", f"apm{j}", pa,
                        lambda E_, pj=pj: E_.matmul(out=B0[:, 256:384],
                                                    lhsT=PB[:, pj:pj + 128],
                                                    rhs=PB[:, YOc:YOc + 128],
                                                    start=True, stop=True))
                    add("vector", f"ap{j}" if j > 0 else "s73", WN(f"apm{j}"),
                        lambda E_, j=j: E_.tensor_tensor(out=PB[:, YOc:YOc + 128],
                                                         in0=PB[:, YOc:YOc + 128],
                                                         in1=B0[:, 256:384],
                                                         op=(ALU.add if j > 0 else ALU.subtract)))
                add("tensor", "s74", None,
                    lambda E_: E_.matmul(out=A0[:, 384:512], lhsT=qs,
                                         rhs=PB[:, SBc:SBc + 128], start=True, stop=True))
                add("vector", "s75", WN("s74"),
                    lambda E_: E_.tensor_scalar(out=PB[:, XCc:XCc + 128],
                                                in0=A0[:, 384:512], scalar1=gcl,
                                                scalar2=None, op0=ALU.mult))
                add("tensor", "s76", WN("s73", "s75"),
                    lambda E_: E_.matmul(out=A0[:, 128:256], lhsT=PB[:, QKc:QKc + 128],
                                         rhs=PB[:, YOc:YOc + 128], start=True, stop=True))
                add("vector", "s77", WN("s76", "s75"),
                    lambda E_: E_.tensor_tensor(out=U1a,
                                                in0=PB[:, XCc:XCc + 128],
                                                in1=A0[:, 128:256], op=ALU.add))
                add("tensor", "s78", WN("s77"),
                    lambda E_: E_.transpose(out=B0[:, 128:256], in_=U1a,
                                            identity=H["idf"][:]))

                def pre79(mm):
                    mm.w(N("s78"), cc)
                    wpc(mm, sg + "wog_{p}_7", p, cc)
                add("scalar", "s79", pre79,
                    lambda E_: E_.copy(out=PB[:, OFM + h * 128: OFM + (h + 1) * 128],
                                       in_=B0[:, 128:256]))
                add("tensor", "s80", WN("s79"),
                    lambda E_: E_.transpose(out=B0[:, 0:128], in_=KFa,
                                            identity=H["idf"][:]))
                add("vector", "s81", WN("s80"),
                    lambda E_: E_.tensor_scalar(out=PB[:, KHc:KHc + 128],
                                                in0=B0[:, 0:128],
                                                scalar1=PF[:, ETOc + 127:ETOc + 128],
                                                scalar2=None, op0=ALU.mult))
                add("tensor", "s82", WN("s81"),
                    lambda E_: E_.matmul(out=A0[:, 256:384], lhsT=PB[:, KHc:KHc + 128],
                                         rhs=PB[:, YOc:YOc + 128], start=True, stop=True))
                add("vector", "s83", None,
                    lambda E_: E_.tensor_mul(out=PF[:, GCTc:GCTc + 1], in0=gcl,
                                             in1=PF[:, ETOc + 127:ETOc + 128]))
                add("vector", "s84", WN("s82", "s83"),
                    lambda E_: E_.scalar_tensor_tensor(out=Sh, in0=Sh,
                                                       scalar=PF[:, GCTc:GCTc + 1],
                                                       in1=A0[:, 256:384],
                                                       op0=ALU.mult, op1=ALU.add))
                return St

            for pr in range(4):
                a_ = hsteps(2 * pr)
                b_ = hsteps(2 * pr + 1)
                merged = []
                for k_ in range(max(len(a_), len(b_))):
                    if k_ < len(a_):
                        merged.append(a_[k_])
                    if k_ < len(b_):
                        merged.append(b_[k_])
                if pr < 3:
                    extra = []
                    for idx in range(6 * (pr + 1), 6 * (pr + 1) + 6):
                        extra += proj_steps(idx)
                    stride = max(1, len(merged) // (len(extra) + 1))
                    out_ = []
                    ei = 0
                    for k_, stp in enumerate(merged):
                        out_.append(stp)
                        if ei < len(extra) and (k_ + 1) % stride == 0:
                            out_.append(extra[ei])
                            ei += 1
                    out_.extend(extra[ei:])
                    merged = out_
                run_steps(merged)

            if eng == "tensor":
                m.w(sg + f"s79_{p}_6", cc)
                m.w(sg + f"s79_{p}_7", cc)
                for ot in range(8):
                    bank = ps[1 + ot % 2]
                    if ot >= 2:
                        m.w(sg + f"woe_{p}_{ot - 2}", cc)
                    elif ot == 0:
                        m.w(sg + f"s84_{p}_6", cc)
                    else:
                        m.w(sg + f"s81_{p}_6", cc)
                    for it in range(8):
                        i = E.matmul(out=bank[:, 0:128],
                                     lhsT=AB[:, 24576 + it * 1024 + ot * 128:
                                             24576 + it * 1024 + ot * 128 + 128],
                                     rhs=PB[:, OFM + it * 128: OFM + (it + 1) * 128],
                                     start=(it == 0), stop=(it == 7))
                    m.op(i, sg + f"wog_{p}_{ot}")
            if eng == "vector":
                m.wprev(sg + f"sst_{p}", cc)
                for ot in range(8):
                    m.w(sg + f"wog_{p}_{ot}", cc)
                    i = E.tensor_tensor(out=XF[:, xpo + ot * 128: xpo + (ot + 1) * 128],
                                        in0=ps[1 + ot % 2][:, 0:128],
                                        in1=XF[:, xao + ot * 128: xao + (ot + 1) * 128],
                                        op=ALU.add)
                    m.op(i, sg + f"woe_{p}_{ot}")

    def epi(m):
        if eng == "tensor":
            m.epims(sg + "end_t", sg + "_t")
        if eng == "vector":
            m.epims(sg + "end_v", sg + "_v")
        if eng == "scalar":
            m.epims(sg + "end_s", sg + "_s")

    m.run_stage(sg, n_iter, pro, body, epi)


def stage_b2(m, sg="b2"):
    H = m.H; E = m.E; eng = m.eng
    AA, AB, XF, XN, PB = H["AA"], H["AB"], H["XF"], H["XN"], H["PB"]
    STT = H["STT"]
    ps = H["ps"]; pstat = H["pstat"]
    n_iter = KCFG["b2"]

    def pro(m):
        if eng == "gpsimd":
            m.w("b1end_t")
            m.w("b1end_v")
            m.wsem("b1xst", KCFG["b1"] * 2 * 16)
            m.wsem("b1ldx0", KCFG["b1"] * 16)
            m.wsem("b1ldx1", KCFG["b1"] * 16)
            for pp in range(2):
                E.dma_start(
                    out=XF.ap()[:, pp * 2048:pp * 2048 + 2048].rearrange(
                        "p (n t) -> p n t", t=256),
                    in_=H["xpb"].rearrange("n p t -> p n t")[:, :, ts(pp, 256)]
                ).then_inc(m.sem(sg + "ldp" + str(pp)), 16)
                E.dma_start(
                    out=XF.ap()[:, 4096 + pp * 2048:4096 + pp * 2048 + 2048].rearrange(
                        "p (n t) -> p n t", t=256),
                    in_=H["x1b"].rearrange("n p t -> p n t")[:, :, ts(pp, 256)]
                ).then_inc(m.sem(sg + "ldx" + str(pp)), 16)
            E.dma_start(out=AB.ap().rearrange("p (n o) -> p n o", o=1024),
                        in_=H["W2"].rearrange("(n p) o -> p n o", p=128)
                        ).then_inc(m.sem("w_bw2"), 16)
        if eng == "tensor":
            E.wait_ge(m.sem("w_bw1"), 16)
            m.w("b1end_t")
        if eng == "sync":
            m.w("b1end_v")
        if eng == "vector":
            m.w("b1end_v")
        if eng == "scalar":
            m.w("b1end_s")

    def body(m, cc):
        for p in range(2):
            c = cc * 2 + p
            xpo = p * 2048
            x1o = 4096 + p * 2048
            no = p * 2048
            xbo = no
            ro = p * 1024
            if p == 0:
                m.lbase(cc)
            if eng == "gpsimd":
                m.w(sg + f"w2e_{p}_7", cc)
                m.wsem(sg + "xst", (c + 1) * 16)
                E.dma_start(
                    out=XF.ap()[:, xpo:xpo + 2048].rearrange("p (n t) -> p n t", t=256),
                    in_=H["xpb"].rearrange("n p t -> p n t")[:, :, ts((c + 2) % 16, 256)]
                ).then_inc(m.sem(sg + "ldp" + str(p)), 16)
                E.dma_start(
                    out=XF.ap()[:, x1o:x1o + 2048].rearrange("p (n t) -> p n t", t=256),
                    in_=H["x1b"].rearrange("n p t -> p n t")[:, :, ts((c + 2) % 16, 256)]
                ).then_inc(m.sem(sg + "ldx" + str(p)), 16)
            if eng == "sync":
                m.w(sg + f"w2e_{p}_7", cc)
                E.dma_start(
                    out=H["x2b"].rearrange("n p t -> p n t")[:, :, ts(c, 256)],
                    in_=XF.ap()[:, x1o:x1o + 2048].rearrange("p (n t) -> p n t", t=256)
                ).then_inc(m.sem(sg + "xst"), 16)
                m.wsem(sg + "xst", (c + 1) * 16)

            L = m.lad
            mrow = STT[0:1, ro:ro + 256]
            erow = STT[0:1, ro + 256:ro + 512]
            m2row = STT[0:1, ro + 512:ro + 768]
            L("vector", lambda E_: E_.tensor_copy(out=XN[:, xbo:xbo + 2048],
                                                  in_=XF[:, xpo:xpo + 2048]),
              extra_waits=((sg + "ldp" + str(p), cc * 16),))
            def _s1(E_):
                for it in range(8):
                    i = E_.matmul(out=pstat[0:1, 0:256], lhsT=H["onesb"][:, 0:1],
                                  rhs=XN[:, xbo + it * 256: xbo + (it + 1) * 256],
                                  start=(it == 0), stop=(it == 7))
                return i
            def _s2(E_):
                for it in range(8):
                    i = E_.matmul(out=pstat[0:1, 256:512], lhsT=H["onesb"][:, 0:1],
                                  rhs=XN[:, xbo + it * 256: xbo + (it + 1) * 256],
                                  start=(it == 0), stop=(it == 7))
                return i
            L("tensor", _s1)
            L("scalar", lambda E_: E_.activation(out=XN[:, xbo:xbo + 2048],
                                                 in_=XN[:, xbo:xbo + 2048], func=AF.Square))
            L("tensor", _s2)
            L("vector", lambda E_: E_.tensor_scalar(out=mrow, in0=pstat[0:1, 0:256],
                                                    scalar1=1.0 / 1024, scalar2=None, op0=ALU.mult))
            L("vector", lambda E_: E_.tensor_scalar(out=erow, in0=pstat[0:1, 256:512],
                                                    scalar1=1.0 / 1024, scalar2=None, op0=ALU.mult))
            L("scalar", lambda E_: E_.activation(out=m2row, in_=mrow, func=AF.Square))
            L("vector", lambda E_: E_.tensor_sub(out=erow, in0=erow, in1=m2row))
            L("scalar", lambda E_: E_.activation(out=erow, in_=erow, func=AF.Sqrt, bias=1e-5))
            L("vector", lambda E_: E_.reciprocal(out=erow, in_=erow))
            L("tensor", lambda E_: E_.matmul(out=ps[0][:, 0:256], lhsT=H["onesf"][:],
                                             rhs=mrow, start=True, stop=True))
            L("tensor", lambda E_: E_.matmul(out=ps[0][:, 256:512], lhsT=H["onesf"][:],
                                             rhs=erow, start=True, stop=True))
            for it in range(8):
                tt = H["PF"][:, (it % 2) * 256: (it % 2) * 256 + 256]
                L("vector", lambda E_, it=it, tt=tt: E_.tensor_sub(
                    out=tt, in0=XF[:, xpo + it * 256: xpo + (it + 1) * 256],
                    in1=ps[0][:, 0:256]))
                L("vector", lambda E_, it=it, tt=tt: E_.tensor_mul(out=tt, in0=tt,
                                                                  in1=ps[0][:, 256:512]))
                L("vector", lambda E_, it=it, tt=tt: E_.tensor_scalar(
                    out=XN[:, no + it * 256: no + (it + 1) * 256],
                    in0=tt, scalar1=H["ln2wc"][:, it:it + 1],
                    scalar2=H["ln2bc"][:, it:it + 1], op0=ALU.mult, op1=ALU.add),
                  name=(f"n{p}" if it == 7 else None))

            if eng == "tensor":
                m.ladwait(f"n{p}", cc)
                for ot in range(32):
                    bank = ps[1 + ot % 2]
                    if ot >= 2:
                        m.w(sg + f"ge_{p}_{ot - 2}", cc)
                    else:
                        wpc(m, sg + "ge_{p}_" + str(30 + ot), p, cc)
                    for it in range(8):
                        i = E.matmul(out=bank[:, 0:256],
                                     lhsT=AA[:, it * 4096 + ot * 128: it * 4096 + ot * 128 + 128],
                                     rhs=XN[:, no + it * 256: no + (it + 1) * 256],
                                     start=(it == 0), stop=(it == 7))
                    m.op(i, sg + f"w1g_{p}_{ot}")
            if eng == "scalar":
                for ot in range(32):
                    bank = ps[1 + ot % 2]
                    m.w(sg + f"w1g_{p}_{ot}", cc)
                    if ot == 0:
                        wpc(m, sg + "w2g_{p}_7", p, cc)
                    i = E.activation(out=PB[:, ot * 256:(ot + 1) * 256], in_=bank[:, 0:256],
                                     func=AF.Gelu, bias=H["b1c"][:, ot:ot + 1])
                    m.op(i, sg + f"ge_{p}_{ot}")
            if eng == "tensor":
                E.wait_ge(m.sem("w_bw2"), 16)
                m.w(sg + f"ge_{p}_31", cc)
                for ot in range(8):
                    bank = ps[3 + ot % 2]
                    if ot >= 2:
                        m.w(sg + f"w2e_{p}_{ot - 2}", cc)
                    else:
                        wpc(m, sg + "e1_{p}_" + str(6 + ot), p, cc)
                    for it in range(32):
                        i = E.matmul(out=bank[:, 0:256],
                                     lhsT=AB[:, it * 1024 + ot * 128: it * 1024 + ot * 128 + 128],
                                     rhs=PB[:, it * 256:(it + 1) * 256],
                                     start=(it == 0), stop=(it == 31))
                    m.op(i, sg + f"w2g_{p}_{ot}")
            if eng == "vector":
                m.wsem(sg + "ldx" + str(p), cc * 16)
                for ot in range(8):
                    bank = ps[3 + ot % 2]
                    m.w(sg + f"w2g_{p}_{ot}", cc)
                    x1t = XF[:, x1o + ot * 256: x1o + (ot + 1) * 256]
                    i = E.scalar_tensor_tensor(out=x1t, in0=bank[:, 0:256],
                                               scalar=H["b2c"][:, ot:ot + 1], in1=x1t,
                                               op0=ALU.add, op1=ALU.add)
                    m.op(i, sg + f"e1_{p}_{ot}")
                    m.w(sg + f"e1_{p}_{ot}", cc)
                    i2 = E.tensor_add(out=x1t, in0=x1t,
                                      in1=XF[:, xpo + ot * 256: xpo + (ot + 1) * 256])
                    m.op(i2, sg + f"w2e_{p}_{ot}")

    def epi(m):
        if eng == "tensor":
            m.epims(sg + "end_t", sg + "_t")
        if eng == "vector":
            m.epims(sg + "end_v", sg + "_v")
        if eng == "scalar":
            m.epims(sg + "end_s", sg + "_s")
        if eng == "sync":
            m.epims(sg + "end_st", sg + "_sy")

    m.run_stage(sg, n_iter, pro, body, epi)


def emit_b(m):
    stage_b1(m)
    stage_b2(m)


def build_core(phases=("A",), dbg=False):
    nc = bacc.Bacc("TRN2", target_bir_lowering=False)
    H = {}
    H["xT"] = nc.dram_tensor("xT", [8, 128, 4096], F32, kind="ExternalInput").ap()
    for nm, shp, dt in [
        ("w1T", [1024, 4096], F32), ("w2T", [1024, 4096], F32),
        ("h1T", [4096, 4096], F32), ("h2T", [4096, 4096], F32),
        ("w3T", [4096, 1024], F32),
        ("s1c", [128, 32], F32), ("s2c", [128, 32], F32),
        ("h1sc", [128, 32], F32), ("h2sc", [128, 32], F32), ("s3c", [128, 8], F32),
        ("Wq", [1024, 1024], F32), ("Wk", [1024, 1024], F32),
        ("Wv", [1024, 1024], F32), ("Wo", [1024, 1024], F32),
        ("Wba", [1024, 16], F32), ("W1", [1024, 4096], F32), ("W2", [4096, 1024], F32),
        ("b1c", [128, 32], F32), ("b2c", [128, 8], F32),
        ("ln1wc", [128, 8], F32), ("ln1bc", [128, 8], F32),
        ("ln2wc", [128, 8], F32), ("ln2bc", [128, 8], F32),
        ("IDB", [128, 128], BF16), ("IDF", [128, 128], F32),
        ("TRILS", [128, 128], F32), ("TRIUI", [128, 128], F32),
        ("ONESB", [128, 128], BF16), ("ONESF", [1, 128], F32),
        ("CONSTV", [128, 4], F32),
    ]:
        H[nm] = nc.dram_tensor(nm, shp, dt, kind="ExternalInput").ap()
    sk = "ExternalOutput" if dbg else "Internal"
    H["h1b"] = nc.dram_tensor("h1b", [4096, 4096], BF16, kind="Internal").ap()
    H["h2b"] = nc.dram_tensor("h2b", [4096, 4096], BF16, kind="Internal").ap()
    H["ybuf"] = nc.dram_tensor("ybuf", [32, 128, 4096], BF16, kind="Internal").ap()
    H["gbuf"] = nc.dram_tensor("gbuf", [32, 128, 4096], BF16, kind="Internal").ap()
    H["x1b"] = nc.dram_tensor("x1b", [8, 128, 4096], F32, kind=sk).ap()
    H["xpb"] = nc.dram_tensor("xpb", [8, 128, 4096], F32, kind=sk).ap()
    H["x2b"] = nc.dram_tensor("x2b", [8, 128, 4096], F32, kind=sk).ap()
    H["outT"] = nc.dram_tensor("outT", [8, 128, 4096], F32, kind="ExternalOutput").ap()

    with ExitStack() as ctx:
        sb = lambda name, shp, dt: ctx.enter_context(nc.sbuf_tensor(name, shp, dt))
        H["AA"] = sb("AA", [128, 32768], BF16)
        H["AB"] = sb("AB", [128, 32768], BF16)
        H["XF"] = sb("XF", [128, 8192], F32)
        H["XN"] = sb("XNs", [128, 4096], BF16)
        H["PB"] = sb("PBs", [128, 8192], BF16)
        H["PF"] = sb("PFs", [128, 2560], F32)
        H["STT"] = sb("STT", [128, 2048], F32)
        H["T1"] = sb("T1", [128, 512], BF16)
        for nm, shp, dt in [("idb", [128, 128], BF16), ("idf", [128, 128], F32),
                            ("trils", [128, 128], F32), ("triui", [128, 128], F32),
                            ("onesb", [128, 128], BF16), ("onesf", [1, 128], F32),
                            ("s1c", [128, 32], F32), ("s2c", [128, 32], F32),
                            ("h1sc", [128, 32], F32), ("h2sc", [128, 32], F32),
                            ("s3c", [128, 8], F32), ("s3h", [128, 8], F32),
                            ("b1c", [128, 32], F32), ("b2c", [128, 8], F32),
                            ("ln1wc", [128, 8], F32), ("ln1bc", [128, 8], F32),
                            ("ln2wc", [128, 8], F32), ("ln2bc", [128, 8], F32)]:
            key = nm if nm not in H else nm
            H[key + "_sb"] = sb(key + "_sb", shp, dt)
        for nm in ["idb", "idf", "trils", "triui", "onesb", "onesf",
                   "s1c", "s2c", "h1sc", "h2sc", "s3c", "s3h",
                   "b1c", "b2c", "ln1wc", "ln1bc", "ln2wc", "ln2bc"]:
            H["dram_" + nm] = H.get(nm)
            H[nm] = H[nm + "_sb"]
        H["WBA"] = sb("WBA", [128, 128], BF16)
        H["constv"] = sb("constv", [128, 4], F32)
        nc.const_aps.register(F32, 1e-5, H["constv"][:, 0:1])
        nc.const_aps.register(F32, 1e-6, H["constv"][:, 1:2])
        nc.const_aps.register(F32, 1.0, H["constv"][:, 2:3])
        H["ps"] = [ctx.enter_context(nc.psum_tensor(f"ps{i}", [128, 512], F32))
                   for i in range(6)]
        H["pbf"] = ctx.enter_context(nc.psum_tensor("pbf", [128, 512], F32))
        H["pstat"] = ctx.enter_context(nc.psum_tensor("pstat", [1, 512], F32))

        semnames = ["cl", "w_w1", "w_w2", "w_h1", "w_h2", "w_cw1", "w_cw2", "s3hd"]
        engsfx = ["_sy", "_g", "_t", "_v", "_s",
                  "_syi", "_gi", "_ti", "_vi", "_si"]
        semnames += ["f1" + x for x in ["k", "ldx0", "ldx1", "ysta", "ystb"] + engsfx]
        semnames += ["f2" + x for x in ["ldy0", "ldy1", "lds0", "lds1", "lds2", "lds3",
                                        "gsta", "gstb", "ssd", "sse", "yfr0", "yfr1"] + engsfx]
        semnames += ["f4" + x for x in ["ldg0", "ldg1", "ldx0", "ldx1", "xst", "w3"] + engsfx]
        if "B" in phases:
            semnames += ["w_wq", "w_wk", "w_wv", "w_wo", "w_wba", "w_bw1", "w_bw2"]
            semnames += ["b1" + x for x in ["k", "ldx0", "ldx1", "xst"] + engsfx]
            semnames += ["b2" + x for x in ["k", "ldp0", "ldp1", "ldx0", "ldx1", "xst"] + engsfx]
        sems = {s_: ctx.enter_context(nc.semaphore(s_)) for s_ in semnames}

        shared = {"reg": {}, "stages": {}, "lad": {}}

        def fix_pro_consts(Hl):
            pass

        def emit_all(eng, E, count_mode):
            mm_ = M(nc, H, sems, eng, E, shared, count_mode)
            emit_program_patched(mm_, phases=phases, dbg=dbg)

        for eng in ENGS:
            shared["ladctr"] = {}
            shared["shctr"] = {}
            emit_all(eng, MockEngine(), True)
        block = ctx.enter_context(nc.Block())
        block.sync(lambda E: emit_all("sync", E, False))
        block.gpsimd(lambda E: emit_all("gpsimd", E, False))
        block.tensor(lambda E: emit_all("tensor", E, False))
        block.vector(lambda E: emit_all("vector", E, False))
        block.scalar(lambda E: emit_all("scalar", E, False))
    return nc


def emit_program_patched(m, phases, dbg):
    H = m.H; E = m.E; eng = m.eng
    m.stage_begin("PRO", 0)
    if eng == "sync":
        for nm in ["IDB", "IDF", "TRILS", "TRIUI", "ONESB", "ONESF"]:
            low = {"IDB": "idb", "IDF": "idf", "TRILS": "trils", "TRIUI": "triui",
                   "ONESB": "onesb", "ONESF": "onesf"}[nm]
            E.dma_start(out=H[low][:], in_=H[nm]).then_inc(m.sem("cl"), 16)
        for nm in ["s1c", "s2c", "h1sc", "h2sc", "s3c", "b1c", "b2c",
                   "ln1wc", "ln1bc", "ln2wc", "ln2bc"]:
            E.dma_start(out=H[nm][:], in_=H["dram_" + nm]).then_inc(m.sem("cl"), 16)
        E.dma_start(out=H["constv"][:], in_=H["CONSTV"]).then_inc(m.sem("cl"), 16)
    if eng == "gpsimd":
        E.dma_start(out=H["AA"].ap().rearrange("p (n o) -> p n o", o=4096),
                    in_=H["w1T"].rearrange("(n p) o -> p n o", p=128)
                    ).then_inc(m.sem("w_w1"), 16)
        E.dma_start(out=H["AB"].ap().rearrange("p (n o) -> p n o", o=4096),
                    in_=H["w2T"].rearrange("(n p) o -> p n o", p=128)
                    ).then_inc(m.sem("w_w2"), 16)
        E.dma_start(out=H["h1b"], in_=H["h1T"]).then_inc(m.sem("w_h1"), 16)
        E.dma_start(out=H["h2b"], in_=H["h2T"]).then_inc(m.sem("w_h2"), 16)
    if eng == "vector":
        E.wait_ge(m.sem("cl"), 18 * 16)
        m.ms(E.tensor_scalar(out=H["s3h"][:], in0=H["s3c"][:], scalar1=0.5,
                             scalar2=None, op0=ALU.mult), "s3h", "s3hd")
    m.stage_end()

    if "A" in phases:
        ffn_phase(m, "a", H["xT"], H["x1b"], src_gate=None, first=True)
    if "B" in phases:
        emit_b(m)
    if "C" in phases:
        ffn_phase(m, "c", H["x2b"], H["outT"], src_gate="b2end_v", first=False,
                  srcx=lambda mm, cc, p: mm.wsem(
                      "b2xst", mm.sembase("b2", "b2xst") + (2 * cc + p + 1) * 16))


def kernel(**inputs):
    com, shards = host_prepare(inputs)
    nc = build_core(phases=("A", "B", "C"), dbg=False)
    nc.finalize()
    in_maps = [dict(com, xT=shards[b]) for b in range(8)]
    res = run_bass_kernel_spmd(nc, in_maps, core_ids=list(range(8)))
    out = np.stack([np.ascontiguousarray(
        np.asarray(r["outT"], np.float32).reshape(1024, 4096).T)
        for r in res.results])
    return np.ascontiguousarray(out)
